# version 20
# baseline (speedup 1.0000x reference)
import math
from contextlib import ExitStack

import numpy as np

import concourse.bass as bass
import concourse.mybir as mybir
from concourse.bass_utils import run_bass_kernel_spmd

F32 = mybir.dt.float32
AL = mybir.AluOpType
AF = mybir.ActivationFunctionType
AX = mybir.AxisListType.X

T = 2048
CL = 128
NCH = T // CL
EPS = 1e-5



def _tiles(w, kp=128, mp=128):
    K, M = w.shape
    return np.ascontiguousarray(
        w.reshape(K // kp, kp, M // mp, mp).transpose(2, 0, 1, 3)
    )


def _alibi_slopes(n):
    def pow2(m):
        start = 2 ** (-(2 ** (-(math.log2(m) - 3))))
        return [start * start**i for i in range(m)]
    if math.log2(n).is_integer():
        return pow2(n)
    c = 2 ** math.floor(math.log2(n))
    return pow2(c) + _alibi_slopes(2 * c)[0::2][: n - c]


def _prep_inputs(inp):
    g = {}
    f32 = lambda a: np.ascontiguousarray(np.asarray(a, dtype=np.float32))

    g["c_ones_row"] = np.ones((1, 128), np.float32)
    g["c_ones_col"] = np.ones((128, 1), np.float32)
    g["c_ident"] = np.eye(128, dtype=np.float32)
    g["c_triu"] = np.triu(np.ones((128, 128), np.float32))
    g["c_negbig"] = np.where(
        np.arange(128)[:, None] <= np.arange(128)[None, :], 0.0, -1.0e30
    ).astype(np.float32)
    g["c_qpos"] = np.arange(T, dtype=np.float32).reshape(1, T)
    g["c_epscol"] = np.full((128, 1), 1e-5, np.float32)
    g["c_q2row"] = np.stack([np.ones(T, np.float32),
                             np.arange(T, dtype=np.float32)])
    e2t = np.zeros((2, 128), np.float32)
    e2t[0, :64] = 1.0
    e2t[1, 64:] = 1.0
    g["c_e2t"] = e2t
    g["qrw2_col"] = np.ascontiguousarray(
        np.tile(f32(inp["L1_attn_q_rms_w"]), 2).reshape(128, 1))
    g["krw2_col"] = np.ascontiguousarray(
        np.tile(f32(inp["L1_attn_k_rms_w"]), 2).reshape(128, 1))
    e2 = np.zeros((128, 2), np.float32)
    e2[:64, 0] = 1.0
    e2[64:, 1] = 1.0
    g["c_ehh"] = e2
    ew = np.zeros((8, 128), np.float32)
    for i in range(8):
        ew[i, i * 16 : (i + 1) * 16] = 1.0
    g["c_ew"] = ew

    g["wx_t"] = _tiles(f32(inp["L0_attn_Wx"]).T)
    g["wz_t"] = _tiles(f32(inp["L0_attn_Wz"]).T)
    g["wb_t"] = _tiles(f32(inp["L0_attn_Wb"]).T)
    g["wc_t"] = _tiles(f32(inp["L0_attn_Wc"]).T)
    g["wdt_t"] = _tiles(f32(inp["L0_attn_Wdt"]).T, mp=8)
    cw = f32(inp["L0_attn_conv_w"])[:, 0, :]
    g["convw_fm"] = np.ascontiguousarray(cw.reshape(16, 128, 4).transpose(1, 0, 2))
    g["convb_fm"] = np.ascontiguousarray(f32(inp["L0_attn_conv_b"]).reshape(16, 128).T)
    g["bdt_col"] = f32(inp["L0_attn_b_dt"]).reshape(8, 1)
    g["loga_col"] = f32(inp["L0_attn_log_a"]).reshape(8, 1)
    g["d_fm"] = np.ascontiguousarray(
        np.broadcast_to(np.repeat(f32(inp["L0_attn_d_param"]), 2)[None, :], (128, 16))
    )
    g["gnw_fm"] = np.ascontiguousarray(f32(inp["L0_attn_gn_w"]).reshape(16, 128).T)
    g["gnb_fm"] = np.ascontiguousarray(f32(inp["L0_attn_gn_b"]).reshape(16, 128).T)
    g["wout_t"] = _tiles(f32(inp["L0_attn_Wout"]).T)

    g["gw1_t"] = _tiles(f32(inp["L0_ffn_gw1"]).T)
    g["gb1_fm"] = np.ascontiguousarray(f32(inp["L0_ffn_gb1"]).reshape(2, 128).T)
    g["gw2_t"] = _tiles(f32(inp["L0_ffn_gw2"]).T, mp=8)
    g["gb2_row"] = f32(inp["L0_ffn_gb2"]).reshape(1, 8)
    for p in ("q", "k", "v", "o"):
        g[p + "wm_t"] = _tiles(f32(inp[f"L0_ffn_{p}Wm"]).T)
        V = f32(inp[f"L0_ffn_{p}V"]).reshape(128, 1024)
        g[p + "vcat_t"] = _tiles(V.T)
        U = f32(inp[f"L0_ffn_{p}U"]).transpose(0, 2, 1).reshape(128, 1024)
        g[p + "ucat_t"] = _tiles(U)
        g[p + "s_col"] = f32(inp[f"L0_ffn_{p}S"]).reshape(128, 1)
        g[p + "b_fm"] = np.ascontiguousarray(f32(inp[f"L0_ffn_{p}b"]).reshape(8, 128).T)

    g["wq_t"] = _tiles(f32(inp["L1_attn_Wq"]))
    g["wk_t"] = _tiles(f32(inp["L1_attn_Wk"]))
    wv = f32(inp["L1_attn_Wv"])
    g["wv_m"] = np.ascontiguousarray(wv.reshape(8, 128, 2, 512).transpose(0, 2, 1, 3))
    g["wproj_t"] = _tiles(f32(inp["L1_attn_Wproj"]))
    g["qrw_col"] = f32(inp["L1_attn_q_rms_w"]).reshape(64, 1)
    g["krw_col"] = f32(inp["L1_attn_k_rms_w"]).reshape(64, 1)
    g["omega_row"] = f32(inp["L1_attn_omega"]).reshape(1, 16)
    g["u16"] = f32(inp["L1_attn_u"])
    g["slopes_row"] = np.asarray(_alibi_slopes(16), np.float32).reshape(1, 16)

    g["rw_t"] = _tiles(f32(inp["L1_ffn_router_weight"]), mp=8)
    w1 = f32(inp["L1_ffn_w1"])
    g["w1_t"] = np.ascontiguousarray(
        w1.reshape(8, 8, 128, 22, 128).transpose(0, 3, 1, 2, 4)
    )
    w2 = f32(inp["L1_ffn_w2"])
    g["w2_t"] = np.ascontiguousarray(
        w2.reshape(8, 11, 128, 8, 128).transpose(0, 3, 1, 2, 4)
    )
    return g


_IN_SHAPES = [
    ("c_ones_row", (1, 128)), ("c_ones_col", (128, 1)), ("c_ident", (128, 128)),
    ("c_triu", (128, 128)), ("c_negbig", (128, 128)), ("c_qpos", (1, T)),
    ("c_ew", (8, 128)), ("c_ehh", (128, 2)), ("c_epscol", (128, 1)),
    ("c_e2t", (2, 128)), ("qrw2_col", (128, 1)), ("krw2_col", (128, 1)),
    ("c_q2row", (2, T)),
    ("wx_t", (16, 8, 128, 128)), ("wz_t", (16, 8, 128, 128)),
    ("wb_t", (4, 16, 128, 128)), ("wc_t", (4, 16, 128, 128)),
    ("wdt_t", (1, 16, 128, 8)),
    ("convw_fm", (128, 16, 4)), ("convb_fm", (128, 16)),
    ("bdt_col", (8, 1)), ("loga_col", (8, 1)), ("d_fm", (128, 16)),
    ("gnw_fm", (128, 16)), ("gnb_fm", (128, 16)),
    ("wout_t", (8, 16, 128, 128)),
    ("gw1_t", (2, 8, 128, 128)), ("gb1_fm", (128, 2)),
    ("gw2_t", (1, 2, 128, 8)), ("gb2_row", (1, 8)),
    ("qwm_t", (8, 8, 128, 128)), ("qvcat_t", (1, 8, 128, 128)),
    ("qucat_t", (8, 1, 128, 128)), ("qs_col", (128, 1)), ("qb_fm", (128, 8)),
    ("kwm_t", (8, 8, 128, 128)), ("kvcat_t", (1, 8, 128, 128)),
    ("kucat_t", (8, 1, 128, 128)), ("ks_col", (128, 1)), ("kb_fm", (128, 8)),
    ("vwm_t", (8, 8, 128, 128)), ("vvcat_t", (1, 8, 128, 128)),
    ("vucat_t", (8, 1, 128, 128)), ("vs_col", (128, 1)), ("vb_fm", (128, 8)),
    ("owm_t", (8, 8, 128, 128)), ("ovcat_t", (1, 8, 128, 128)),
    ("oucat_t", (8, 1, 128, 128)), ("os_col", (128, 1)), ("ob_fm", (128, 8)),
    ("wq_t", (8, 8, 128, 128)), ("wk_t", (8, 8, 128, 128)),
    ("wv_m", (8, 2, 128, 512)), ("wproj_t", (8, 8, 128, 128)),
    ("qrw_col", (64, 1)), ("krw_col", (64, 1)),
    ("omega_row", (1, 16)), ("u16", (16, 64)), ("slopes_row", (1, 16)),
    ("rw_t", (1, 8, 128, 8)),
    ("w1_t", (8, 22, 8, 128, 128)), ("w2_t", (8, 8, 11, 128, 128)),
]




class _FakeInstr:
    __slots__ = ("incs",)

    def __init__(self):
        self.incs = []

    def then_inc(self, sem, val=1, **kw):
        self.incs.append((sem, val))
        return self


class _EngProxy:
    def __init__(self, rec, kind):
        self._rec = rec
        self._kind = kind

    def __getattr__(self, attr):
        rec, kind = self._rec, self._kind

        def call(*a, **k):
            fake = _FakeInstr()
            rec[kind].append((attr, a, k, fake))
            return fake

        return call


class _FakeBlk:

    def __init__(self):
        self.rec = {k: [] for k in ("tensor", "vector", "scalar", "gpsimd",
                                    "sync")}
        for k in self.rec:
            setattr(self, k, self._mk(k))

    def _mk(self, kind):
        def section(fn):
            fn(_EngProxy(self.rec, kind))
        return section

    def flush(self, blk):
        for kind in ("sync", "gpsimd", "tensor", "vector", "scalar"):
            q = self.rec[kind]
            if not q:
                continue

            @getattr(blk, kind)
            def _(h, q=q):
                for attr, a, k, fake in q:
                    instr = getattr(h, attr)(*a, **k)
                    for sem, val in fake.incs:
                        instr.then_inc(sem, val)
            self.rec[kind] = []


class KB:
    def __init__(self, nc, stack, sbuf_els):
        self.nc = nc
        self.stack = stack
        self.sems = {}
        self.big = stack.enter_context(nc.sbuf_tensor("BIG", [128, sbuf_els], F32))
        self.banks = [
            stack.enter_context(nc.psum_tensor(f"bank{i}", [128, 512], F32))
            for i in range(8)
        ]
        self.bank_free = [None] * 8
        self.sbuf_els = sbuf_els
        self.regions = {}

    def region(self, name, els):
        assert name not in self.regions, name
        taken = sorted(self.regions.values())
        off = 0
        for a, b in taken:
            if off + els <= a:
                break
            off = max(off, b)
        if off + els > self.sbuf_els:
            print("MAP at overflow of", name, els)
            for n, (a, b) in sorted(self.regions.items(), key=lambda x: x[1]):
                print(f"  {a:6d}..{b:6d}  {n}")
            raise AssertionError(f"SBUF overflow: {name} @{off}+{els}")
        self.regions[name] = (off, off + els)
        return off

    def free(self, *names):
        for n in names:
            del self.regions[n]

    def view(self, name):
        a, b = self.regions[name]
        return self.big.ap()[:, a:b]

    def sem(self, name):
        if name not in self.sems:
            self.sems[name] = [self.stack.enter_context(self.nc.semaphore(name)), 0]
        return self.sems[name]

    def inc(self, instr, name, k=1):
        s = self.sem(name)
        instr.then_inc(s[0], k)
        s[1] += k
        return (name, s[1])

    def tok(self, name):
        return (name, self.sem(name)[1])

    def wait(self, e, *toks):
        for tk in toks:
            if tk is None:
                continue
            name, v = tk
            if v > 0:
                e.wait_ge(self.sems[name][0], v)

    def chain(self, e, name):
        s = self.sem(name)
        if s[1] > 0:
            e.wait_ge(s[0], s[1])

    def dma(self, e, sem, out, in_, slow=False):
        self.chain(e, sem)
        return self.inc(
            e.dma_start(out=out, in_=in_, allow_slow_non_contiguous=slow),
            sem, 16)

    @staticmethod
    def nop(e):
        if isinstance(e, _EngProxy):
            return e.engine_nop() if e._kind in ("vector", "gpsimd") else e.nop()
        f = getattr(e, "engine_nop", None)
        return f() if f is not None else e.nop()

    def use_bank(self, e, *idxs):
        for i in idxs:
            if self.bank_free[i] is not None:
                self.wait(e, self.bank_free[i])
                self.bank_free[i] = None




def build_program():
    nc = bass.Bass(trn_type="TRN2", target_bir_lowering=False)
    dr = {}
    dr["x0fm"] = nc.dram_tensor("x0fm", [8, 128, T], F32, kind="ExternalInput").ap()
    for nm, shp in _IN_SHAPES:
        dr[nm] = nc.dram_tensor(nm, list(shp), F32, kind="ExternalInput").ap()
    for nm, shp in [
        ("xc_sp", (16, 128, T)), ("z_sp", (16, 128, T)), ("y_sp", (16, 128, T)),
        ("q0_sp", (8, 128, T)), ("k0_sp", (8, 128, T)), ("r2_sp", (8, 128, T)),
        ("q1_sp", (8, 128, T)), ("k1_sp", (8, 128, T)), ("r3_sp", (8, 128, T)), ("r1_sp", (8, 128, T)),
    ]:
        dr[nm] = nc.dram_tensor(nm, list(shp), F32).ap()
    out_fm = nc.dram_tensor("out_fm", [8, 128, T], F32, kind="ExternalOutput").ap()
    dr["out_fm"] = out_fm

    with ExitStack() as stack:
        kb = KB(nc, stack, sbuf_els=52480)
        with nc.Block() as realblk:
            blk = _FakeBlk()
            C = _load_consts(nc, blk, kb, dr)
            _phase_A1(nc, blk, kb, dr, C)
            _barrier(blk, kb)
            kb.free("XF")
            _phase_A2a(nc, blk, kb, dr, C)
            _barrier(blk, kb)
            _phase_A2b(nc, blk, kb, dr, C)
            _barrier(blk, kb)
            kb.free("BB", "CC", "DT8")
            _phase_A3(nc, blk, kb, dr, C)
            _barrier(blk, kb)
            _phase_B(nc, blk, kb, dr, C)
            _barrier(blk, kb)
            _phase_C(nc, blk, kb, dr, C)
            _barrier(blk, kb)
            _phase_D(nc, blk, kb, dr, C, out_fm)
            blk.flush(realblk)
    return nc


def _preload_set(nc, blk, kb, name):
    from concourse.hw_specs import get_activation_tables
    import concourse.mybir as _mb
    idx = list(get_activation_tables(nc.m.arch)).index(name)

    @blk.scalar
    def _(s):
        s.add_instruction(_mb.InstLoadActFuncSet(
            name=nc.get_next_instruction_name(), act_func_set_id=idx,
            ins=[], outs=[]))


def _barrier(blk, kb):
    if not hasattr(kb, "_bar_snap"):
        kb._bar_snap = {}
    toks = []
    for name, (h, cnt) in kb.sems.items():
        if cnt > 0 and kb._bar_snap.get(name) != cnt:
            toks.append((name, cnt))
            kb._bar_snap[name] = cnt
    for e in ["tensor", "vector", "scalar", "gpsimd", "sync"]:
        @getattr(blk, e)
        def _(h):
            kb.wait(h, *toks)
    kb.bank_free = [None] * 8


class _C:
    pass


def _load_consts(nc, blk, kb, dr):
    C = _C()
    kb.region("XF", 8 * T)
    kb.region("CONST", 768)
    kb.region("QPOS", T)
    big = kb.big.ap()
    co = kb.regions["CONST"][0]
    C.ones_row = big[0:1, co : co + 128]
    C.ones_col = big[:, co + 128 : co + 129]
    C.ident = big[:, co + 144 : co + 272]
    C.triu = big[:, co + 272 : co + 400]
    C.negbig = big[:, co + 400 : co + 528]
    C.d_fm = big[:, co + 528 : co + 544]
    C.gnw_fm = big[:, co + 544 : co + 560]
    C.gnb_fm = big[:, co + 560 : co + 576]
    C.convw = big[:, co + 576 : co + 640].rearrange("p (c k) -> p c k", k=4)
    C.convb = big[:, co + 640 : co + 656]
    C.bdt_col = big[0:8, co + 656 : co + 657]
    C.loga_col = big[0:8, co + 657 : co + 658]
    C.a_col = big[0:8, co + 658 : co + 659]
    C.epscol = big[:, co + 660 : co + 661]
    qo = kb.regions["QPOS"][0]
    C.qpos_row = big[0:1, qo : qo + T]
    C.XF = kb.view("XF").rearrange("p (c t) -> p c t", t=T)

    @blk.sync
    def _(s):
        items = [(C.XF[:, c, :], None, c) for c in range(8)] + [
            (ap, nm, None) for ap, nm in [
            (C.ones_row, "c_ones_row"), (C.ones_col, "c_ones_col"),
            (C.ident, "c_ident"), (C.triu, "c_triu"), (C.negbig, "c_negbig"),
            (C.qpos_row, "c_qpos"), (C.d_fm, "d_fm"), (C.gnw_fm, "gnw_fm"),
            (C.gnb_fm, "gnb_fm"), (C.convw, "convw_fm"), (C.convb, "convb_fm"),
            (C.bdt_col, "bdt_col"), (C.loga_col, "loga_col"),
            (C.epscol, "c_epscol"),
        ]]
        for j, (ap, nm, c) in enumerate(items):
            src_ap = dr["x0fm"][c] if nm is None else dr[nm]
            kb.dma(s, "LC%d" % (j % 2), ap, src_ap)
    C.lw0 = (kb.tok("LC0"), kb.tok("LC1"))

    @blk.scalar
    def _(s):
        kb.wait(s, *C.lw0)
        kb.inc(s.activation(C.a_col, C.loga_col, AF.Exp), "S")
    sa = kb.tok("S")

    @blk.vector
    def _(v):
        kb.wait(v, sa)
        kb.inc(v.tensor_scalar_mul(C.a_col, C.a_col, -1.0), "ACOL")
    return C




def _phase_A1(nc, blk, kb, dr, C):
    kb.region("WA", 2 * 1024)
    kb.region("WZ", 2 * 1024)
    kb.region("CS", 2 * 515)
    kb.region("CT", 2 * 512)
    kb.region("XCB", 2 * 512)
    kb.region("ZB", 2 * 512)
    WA = kb.view("WA").rearrange("p (b k c) -> p b k c", b=2, k=8)
    WZ = kb.view("WZ").rearrange("p (b k c) -> p b k c", b=2, k=8)
    CS = kb.view("CS").rearrange("p (b t) -> p b t", b=2)
    CT = kb.view("CT").rearrange("p (b t) -> p b t", b=2)
    XCB = kb.view("XCB").rearrange("p (b t) -> p b t", b=2)
    ZB = kb.view("ZB").rearrange("p (b t) -> p b t", b=2)
    prev_silu = [None]
    spillx, spillz = {}, {}
    a1free = {}

    for m in range(16):
        mb = m % 2

        wtoks = {}

        @blk.sync
        def _(s, m=m, mb=mb):
            if m >= 2:
                kb.wait(s, *a1free[m - 2])
            wtoks["x"] = kb.dma(s, "WX%d" % mb, WA[:, mb],
                                dr["wx_t"][m].rearrange("k p c -> p k c"))
            wtoks["z"] = kb.dma(s, "WZ%d" % mb, WZ[:, mb],
                                dr["wz_t"][m].rearrange("k p c -> p k c"))

        for nt in range(4):
            i = 4 * m + nt
            tb = i % 2
            bx = [0, 2][nt % 2]
            bz = [4, 6][nt % 2]
            tks = {}

            @blk.tensor
            def _(t, m=m, nt=nt, bx=bx, bz=bz, mb=mb):
                kb.wait(t, wtoks["x"], wtoks["z"], *C.lw0)
                kb.use_bank(t, bx)
                for k in range(8):
                    mm = t.matmul(kb.banks[bx].ap()[:, :], WA[:, mb, k, :],
                                  C.XF[:, k, 512 * nt : 512 * nt + 512],
                                  start=(k == 0), stop=(k == 7))
                tks["xz"] = kb.inc(mm, "T")
                kb.use_bank(t, bz)
                for k in range(8):
                    mm = t.matmul(kb.banks[bz].ap()[:, :], WZ[:, mb, k, :],
                                  C.XF[:, k, 512 * nt : 512 * nt + 512],
                                  start=(k == 0), stop=(k == 7))
                tks["z"] = kb.inc(mm, "T")
                if nt == 3:
                    a1free[m] = (tks["xz"], tks["z"])

            @blk.vector
            def _(v, m=m, nt=nt, i=i, tb=tb, bx=bx):
                kb.wait(v, tks["xz"])
                kb.chain(v, "V")
                if i >= 2:
                    kb.wait(v, kb.tok("Vcs%d" % tb))
                if nt == 0:
                    v.memset(CS[:, tb, 0:3], 0.0)
                else:
                    v.tensor_copy(CS[:, tb, 0:3], CS[:, 1 - tb, 512:515])
                kb.inc(KB.nop(v), "V")
                kb.chain(v, "V")
                ev = v.tensor_copy(CS[:, tb, 3:515], kb.banks[bx].ap()[:, :])
                kb.bank_free[bx] = kb.inc(ev, "V")
                kb.chain(v, "V")
                kb.wait(v, prev_silu[0])
                v.tensor_scalar(CT[:, tb], CS[:, tb, 0:512],
                                C.convw[:, m, 0:1], None, AL.mult)
                kb.inc(KB.nop(v), "V")
                kb.chain(v, "V")
                v.scalar_tensor_tensor(CT[:, 1 - tb], CS[:, tb, 1:513],
                                       C.convw[:, m, 1:2], CT[:, tb], AL.mult, AL.add)
                kb.inc(KB.nop(v), "V")
                kb.chain(v, "V")
                v.scalar_tensor_tensor(CT[:, tb], CS[:, tb, 2:514],
                                       C.convw[:, m, 2:3], CT[:, 1 - tb], AL.mult, AL.add)
                kb.inc(KB.nop(v), "V")
                kb.chain(v, "V")
                cv = v.scalar_tensor_tensor(CT[:, 1 - tb], CS[:, tb, 3:515],
                                            C.convw[:, m, 3:4], CT[:, tb], AL.mult, AL.add)
                kb.inc(cv, "Vcs%d" % tb)
                kb.chain(v, "Vcs%d" % tb)
                tks["conv"] = kb.inc(KB.nop(v), "V")

            @blk.scalar
            def _(s, m=m, nt=nt, i=i, tb=tb, bz=bz):
                kb.wait(s, tks["conv"])
                if i >= 2:
                    kb.wait(s, spillx.get(tb), spillz.get(tb))
                kb.chain(s, "S")
                k1 = s.activation(XCB[:, tb], CT[:, 1 - tb], AF.Silu,
                                  bias=C.convb[:, m : m + 1])
                prev_silu[0] = kb.inc(k1, "S")
                kb.wait(s, tks["z"])
                kb.chain(s, "S")
                k2 = s.activation(ZB[:, tb], kb.banks[bz].ap()[:, :], AF.Silu)
                kb.bank_free[bz] = kb.inc(k2, "S")
            s_now = kb.tok("S")

            @blk.gpsimd
            def _(g, m=m, nt=nt, tb=tb, s_now=s_now):
                kb.wait(g, s_now)
                spillx[tb] = kb.dma(g, "SX%d" % tb,
                                    dr["xc_sp"][m][:, 512 * nt : 512 * nt + 512],
                                    XCB[:, tb])
                spillz[tb] = kb.dma(g, "SZ%d" % tb,
                                    dr["z_sp"][m][:, 512 * nt : 512 * nt + 512],
                                    ZB[:, tb])

    kb.free("WA", "WZ", "CS", "CT", "XCB", "ZB")




def _phase_A2a(nc, blk, kb, dr, C):
    kb.region("BB", 4 * T)
    kb.region("CC", 4 * T)
    kb.region("DT8", T)
    kb.region("XCN", 2 * 16 * 512)
    kb.region("WST", 2 * 16 * 128)
    BB = kb.view("BB").rearrange("p (c t) -> p c t", t=T)
    CC = kb.view("CC").rearrange("p (c t) -> p c t", t=T)
    DT8 = kb.view("DT8")[0:8, :]
    XCN = kb.view("XCN").rearrange("p (b k t) -> p b k t", b=2, k=16)
    WST = kb.view("WST").rearrange("p (b k c) -> p b k c", b=2, k=16)
    C.BB, C.CC, C.DT8 = BB, CC, DT8

    groups = [("wb_t", 0), ("wb_t", 1), ("wb_t", 2), ("wb_t", 3),
              ("wc_t", 0), ("wc_t", 1), ("wc_t", 2), ("wc_t", 3),
              ("wdt_t", 0)]
    a2wfree, a2xfree = {}, {}

    for nt in range(4):
        xb = nt % 2

        gtk = {}

        @blk.gpsimd
        def _(g, nt=nt, xb=xb):
            if xb in a2xfree:
                kb.wait(g, a2xfree[xb])
            gtk["x"] = kb.dma(g, "GXN%d" % xb, XCN[:, xb],
                              dr["xc_sp"][:, :, 512 * nt : 512 * nt + 512]
                              .rearrange("k p t -> p k t"))
        gd_now = gtk["x"]

        for gi, (wnm, m) in enumerate(groups):
            wb = gi % 2
            bank = gi % 2
            tks = {}
            @blk.sync
            def _(s, wnm=wnm, m=m, wb=wb, gi=gi, nt=nt):
                if (wb in a2wfree):
                    kb.wait(s, a2wfree[wb])
                mp = dr[wnm].shape[3]
                tks["w"] = kb.dma(s, "WS%d" % wb, WST[:, wb, :, 0:mp],
                                  dr[wnm][m].rearrange("k p c -> p k c"))

            @blk.tensor
            def _(t, wnm=wnm, wb=wb, bank=bank, gi=gi, xb=xb, gd_now=gd_now):
                kb.wait(t, tks["w"], gd_now)
                kb.use_bank(t, bank)
                mp = dr[wnm].shape[3]
                for k in range(16):
                    mm = t.matmul(kb.banks[bank].ap()[0:mp, :], WST[:, wb, k, 0:mp],
                                  XCN[:, xb, k, :], start=(k == 0), stop=(k == 15))
                tks["mm"] = kb.inc(mm, "T")
                a2wfree[wb] = tks["mm"]
                if gi == len(groups) - 1:
                    a2xfree[xb] = tks["mm"]

            if wnm == "wdt_t":
                @blk.scalar
                def _(s, nt=nt, bank=bank):
                    kb.wait(s, tks["mm"])
                    kb.chain(s, "S")
                    ev = s.activation(DT8[:, 512 * nt : 512 * nt + 512],
                                      kb.banks[bank].ap()[0:8, :], AF.Exp,
                                      bias=C.bdt_col)
                    kb.bank_free[bank] = kb.inc(ev, "S")
            else:
                @blk.vector
                def _(v, wnm=wnm, m=m, nt=nt, bank=bank):
                    kb.wait(v, tks["mm"])
                    kb.chain(v, "V")
                    dst = BB if wnm == "wb_t" else CC
                    ev = v.tensor_copy(dst[:, m, 512 * nt : 512 * nt + 512],
                                       kb.banks[bank].ap()[:, :])
                    kb.bank_free[bank] = kb.inc(ev, "V")

    @blk.scalar
    def _(s):
        kb.wait(s, kb.tok("S"), kb.tok("V"))
        kb.chain(s, "S")
        kb.inc(s.activation(DT8, DT8, AF.Ln, bias=1.0), "S")
    kb.free("XCN", "WST")




def _phase_A2b(nc, blk, kb, dr, C):
    kb.region("LC8", T)
    kb.region("EPR", 2 * T)
    kb.region("EMR", 2 * T)
    kb.region("EP8", T)
    kb.region("EM8", T)
    kb.region("XCT", 2 * 256)
    kb.region("BPF", 2 * T)
    kb.region("CPF", 2 * T)
    kb.region("ECF", 32)
    kb.region("BTM", 2 * 64)
    kb.region("GMT", 2 * 128)
    kb.region("XTM", 2 * 256)
    kb.region("SST", 4 * 256)
    kb.region("YEV", 2 * 256)
    kb.region("STAT", 32)
    kb.region("TMP1", 1)
    kb.region("SQT", 128)

    LC8 = kb.view("LC8")[0:8, :]
    EPR = kb.view("EPR")[0:1, :].rearrange("p (b t) -> p b t", b=2)
    EMR = kb.view("EMR")[0:1, :].rearrange("p (b t) -> p b t", b=2)
    EP8 = kb.view("EP8")[0:8, :]
    EM8 = kb.view("EM8")[0:8, :]
    XCT = kb.view("XCT").rearrange("p (b t) -> p b t", b=2)
    BPF = kb.view("BPF").rearrange("p (b t) -> p b t", b=2)[0:64]
    CPF = kb.view("CPF").rearrange("p (b t) -> p b t", b=2)[0:64]
    ECF = kb.view("ECF").rearrange("p (b t) -> p b t", b=2)[0:64]
    BTM = kb.view("BTM").rearrange("p (b t) -> p b t", b=2)
    GMT = kb.view("GMT").rearrange("p (b t) -> p b t", b=2)
    XTM = kb.view("XTM").rearrange("p (b t) -> p b t", b=2)
    SST = kb.view("SST").rearrange("p (b t) -> p b t", b=4)[0:64]
    YEV = kb.view("YEV").rearrange("p (b t) -> p b t", b=2)
    STAT = kb.view("STAT").rearrange("p (b t) -> p b t", b=2)
    TMP1 = kb.view("TMP1")
    SQT = kb.view("SQT")
    DT8 = C.DT8

    @blk.vector
    def _(v):
        kb.wait(v, kb.tok("S"), kb.tok("ACOL"))
        kb.chain(v, "V")
        v.tensor_scalar(LC8, DT8, C.a_col, None, AL.mult)
        kb.inc(KB.nop(v), "V")
        ones8 = C.ones_col[0:8, 0:1]
        for l in range(NCH):
            kb.chain(v, "V")
            sl = slice(CL * l, CL * l + CL)
            v.tensor_tensor_scan(LC8[:, sl], ones8.broadcast_to([8, CL]),
                                 LC8[:, sl], 0.0, AL.mult, AL.add)
            kb.inc(KB.nop(v), "V")
    v_lc = kb.tok("V")

    @blk.scalar
    def _(s):
        kb.wait(s, v_lc)
        kb.chain(s, "S")
        kb.inc(s.activation(EP8, LC8, AF.Exp), "S")
        kb.chain(s, "S")
        kb.inc(s.activation(EM8, LC8, AF.Exp, scale=-1.0), "S")
    s_e8 = kb.tok("S")

    @blk.vector
    def _(v):
        kb.wait(v, s_e8)
        kb.chain(v, "V")
        kb.inc(v.tensor_tensor(EM8, EM8, DT8, AL.mult), "V")
        kb.chain(v, "V")
        v.memset(STAT[:, 0, :], 0.0)
        kb.inc(KB.nop(v), "V")
        kb.chain(v, "V")
        v.memset(STAT[:, 1, :], 0.0)
        kb.inc(KB.nop(v), "V")
    v_em = kb.tok("V")

    BB, CC = C.BB, C.CC
    spilly, schfree, bpf_free = {}, {}, {}
    g_rows = {}

    def a2b_head_rows(h):
        hb = h % 2

        @blk.gpsimd
        def _(g, h=h, hb=hb):
            kb.wait(g, v_em)
            if hb in schfree:
                kb.wait(g, schfree[hb])
            kb.dma(g, "GR%d" % hb, EPR[:, hb, :], EP8[h : h + 1, :])
            g_rows[hb] = kb.dma(g, "GR%d" % hb, EMR[:, hb, :], EM8[h : h + 1, :])

    pre_tk = {}
    pre_done = {}

    def a2b_pre(h, step):
        hb = h % 2
        cb = h // 2
        hp = 64 * (h % 2)
        bA = 4 * hb

        @blk.tensor
        def _(t, h=h, hb=hb, bA=bA, step=step):
            kb.wait(t, g_rows[hb])
            kb.use_bank(t, bA)
            if step < 8:
                nt = step % 4
                row = EMR if step < 4 else EPR
                mm = t.matmul(kb.banks[bA].ap()[:, :], C.ones_row,
                              row[:, hb, 512 * nt : 512 * nt + 512],
                              start=True, stop=True)
            else:
                mm = t.matmul(kb.banks[bA].ap()[0:64, 0:NCH],
                              C.ones_row[0:1, 0:64],
                              EPR[:, hb, CL - 1 :: CL], start=True, stop=True)
                schfree[hb] = kb.inc(mm, "T")
            pre_tk[hb] = kb.inc(mm, "T") if step < 8 else schfree[hb]

        @blk.vector
        def _(v, hb=hb, cb=cb, hp=hp, bA=bA, step=step):
            kb.wait(v, pre_tk[hb])
            if step == 0:
                if hb in bpf_free:
                    kb.wait(v, bpf_free[hb])
                kb.chain(v, "V")
                v.memset(SST[:, 2 * hb, :], 0.0)
                kb.inc(KB.nop(v), "V")
            kb.chain(v, "V")
            if step < 8:
                nt = step % 4
                dst = BPF if step < 4 else CPF
                srcm = BB if step < 4 else CC
                ev = v.tensor_tensor(dst[:, hb, 512 * nt : 512 * nt + 512],
                                     srcm[hp : hp + 64, cb,
                                          512 * nt : 512 * nt + 512],
                                     kb.banks[bA].ap()[hp : hp + 64, :], AL.mult)
            else:
                ev = v.tensor_copy(ECF[:, hb, 0:NCH],
                                   kb.banks[bA].ap()[0:64, 0:NCH])
            kb.bank_free[bA] = kb.inc(ev, "V")

    cur = {}

    def a2b_s_xc(h, l):
        hb = h % 2
        cur[hb] = tks = {}

        @blk.gpsimd
        def _(g, h=h, l=l, hb=hb):
            if h >= 2 or l >= 1:
                kb.wait(g, kb.tok("XCF%d" % hb))
            tks["xc"] = kb.dma(g, "GC%d" % hb,
                               XCT[:, hb].rearrange("p (c t) -> p c t", c=2),
                               dr["xc_sp"][2 * h : 2 * h + 2, :, CL * l : CL * l + CL]
                               .rearrange("c p t -> p c t"))
    def a2b_s_tpg(h, l):
        hb = h % 2
        bB, bC = 4 * hb + 1, 4 * hb + 2
        tks = cur[hb]

        tcol = slice(CL * l, CL * l + CL)

        @blk.tensor
        def _(t, l=l, hb=hb, bB=bB, bC=bC, g_xc=tks["xc"], tcol=tcol):
            if l == 0:
                kb.wait(t, pre_done["tok"])
            kb.use_bank(t, bB)
            tks["tbp"] = kb.inc(
                t.transpose(kb.banks[bB].ap()[:, 0:64], BPF[:, hb, tcol],
                            C.ident[0:64, 0:64]), "T")
            kb.wait(t, g_xc)
            kb.use_bank(t, bC)
            t.transpose(kb.banks[bC].ap()[:, 0:128], XCT[:, hb, 0:128], C.ident)
            tks["tx"] = kb.inc(
                t.transpose(kb.banks[bC].ap()[:, 128:256], XCT[:, hb, 128:256],
                            C.ident), "T")
            mm = t.matmul(kb.banks[bB].ap()[:, 64 : 64 + CL], BPF[:, hb, tcol],
                          CPF[:, hb, tcol], start=True, stop=True)
            tks["g"] = kb.inc(mm, "T")

    def a2b_s_evac(h, l):
        hb = h % 2
        bB, bC = 4 * hb + 1, 4 * hb + 2
        tks = cur[hb]

        @blk.vector
        def _(v, hb=hb, bB=bB, bC=bC):
            kb.wait(v, tks["tbp"])
            kb.chain(v, "V")
            kb.inc(v.tensor_copy(BTM[:, hb], kb.banks[bB].ap()[:, 0:64]), "V")
            kb.wait(v, tks["tx"])
            kb.chain(v, "V")
            ev = v.tensor_copy(XTM[:, hb], kb.banks[bC].ap()[:, 0:256])
            kb.bank_free[bC] = kb.inc(ev, "V")
            kb.wait(v, tks["g"])
            kb.chain(v, "V")
            ev = v.tensor_tensor(GMT[:, hb], kb.banks[bB].ap()[:, 64 : 64 + CL],
                                 C.triu, AL.mult)
            kb.bank_free[bB] = kb.inc(ev, "V")
            tks["gmt"] = kb.bank_free[bB]

    def a2b_s_yds(h, l):
        hb = h % 2
        bD = 4 * hb + 3
        sb_old = 2 * hb + l % 2
        tcol = slice(CL * l, CL * l + CL)
        tks = cur[hb]

        @blk.tensor
        def _(t, hb=hb, sb_old=sb_old, bD=bD, tcol=tcol):
            kb.wait(t, tks["gmt"])
            kb.use_bank(t, bD)
            t.matmul(kb.banks[bD].ap()[:, 0:128], XTM[:, hb, 0:128], GMT[:, hb],
                     start=True, stop=False)
            t.matmul(kb.banks[bD].ap()[:, 0:128], SST[:, sb_old, 0:128],
                     CPF[:, hb, tcol], start=False, stop=True)
            t.matmul(kb.banks[bD].ap()[:, 128:256], XTM[:, hb, 128:256], GMT[:, hb],
                     start=True, stop=False)
            mm = t.matmul(kb.banks[bD].ap()[:, 128:256], SST[:, sb_old, 128:256],
                          CPF[:, hb, tcol], start=False, stop=True)
            tks["y"] = kb.inc(mm, "T")
            mm = t.matmul(kb.banks[bD].ap()[0:64, 256:512], BTM[:, hb], XTM[:, hb],
                          start=True, stop=True)
            tks["ds"] = kb.inc(mm, "T")
            if l == NCH - 1:
                bpf_free[hb] = tks["ds"]

    def a2b_s_sst(h, l):
        hb = h % 2
        bD = 4 * hb + 3
        sb_old, sb_new = 2 * hb + l % 2, 2 * hb + 1 - l % 2
        tks = cur[hb]

        @blk.vector
        def _(v, l=l, hb=hb, sb_old=sb_old, sb_new=sb_new, bD=bD):
            kb.wait(v, tks["ds"])
            kb.chain(v, "V")
            v.tensor_tensor(SST[:, sb_new, :], SST[:, sb_old, :],
                            kb.banks[bD].ap()[0:64, 256:512], AL.add)
            kb.inc(KB.nop(v), "V")
            kb.chain(v, "V")
            v.tensor_scalar(SST[:, sb_new, :], SST[:, sb_new, :],
                            ECF[:, hb, l : l + 1], None, AL.mult)
            kb.inc(KB.nop(v), "V")

    def a2b_s_yev(h, l):
        hb = h % 2
        bD = 4 * hb + 3
        tks = cur[hb]

        @blk.vector
        def _(v, h=h, l=l, hb=hb, bD=bD):
            kb.wait(v, tks["y"])
            if spilly.get(hb) is not None:
                kb.wait(v, spilly[hb])
            kb.chain(v, "V")
            v.scalar_tensor_tensor(YEV[:, hb, 0:128], XCT[:, hb, 0:128],
                                   C.d_fm[:, 2 * h : 2 * h + 1],
                                   kb.banks[bD].ap()[:, 0:128], AL.mult, AL.add)
            kb.inc(KB.nop(v), "V")
            kb.chain(v, "V")
            ev = v.scalar_tensor_tensor(YEV[:, hb, 128:256], XCT[:, hb, 128:256],
                                        C.d_fm[:, 2 * h + 1 : 2 * h + 2],
                                        kb.banks[bD].ap()[:, 128:256], AL.mult,
                                        AL.add)
            kb.bank_free[bD] = kb.inc(ev, "V")
            kb.chain(v, "V")
            kb.inc(KB.nop(v), "XCF%d" % hb)
            for half in range(2):
                c = 2 * h + half
                sl = slice(128 * half, 128 * half + 128)
                kb.chain(v, "V")
                v.reduce_sum(TMP1[:, 0:1], YEV[:, hb, sl], axis=AX)
                kb.inc(KB.nop(v), "V")
                kb.chain(v, "V")
                v.tensor_tensor(STAT[:, 0, c : c + 1], STAT[:, 0, c : c + 1],
                                TMP1[:, 0:1], AL.add)
                kb.inc(KB.nop(v), "V")
                kb.chain(v, "V")
                v.tensor_tensor(SQT, YEV[:, hb, sl], YEV[:, hb, sl], AL.mult)
                kb.inc(KB.nop(v), "V")
                kb.chain(v, "V")
                v.reduce_sum(TMP1[:, 0:1], SQT, axis=AX)
                kb.inc(KB.nop(v), "V")
                kb.chain(v, "V")
                v.tensor_tensor(STAT[:, 1, c : c + 1], STAT[:, 1, c : c + 1],
                                TMP1[:, 0:1], AL.add)
                kb.inc(KB.nop(v), "V")
            tks["yev"] = kb.tok("V")

    def a2b_s_spill(h, l):
        hb = h % 2
        tks = cur[hb]

        @blk.gpsimd
        def _(g, h=h, l=l, hb=hb):
            kb.wait(g, tks["yev"])
            spilly[hb] = kb.dma(
                g, "GY%d" % hb,
                dr["y_sp"][2 * h : 2 * h + 2, :, CL * l : CL * l + CL]
                .rearrange("c p t -> p c t"),
                YEV[:, hb].rearrange("p (c t) -> p c t", c=2))

    for pj in range(4):
        hh = (2 * pj, 2 * pj + 1)
        a2b_head_rows(hh[0])
        a2b_head_rows(hh[1])
        for step in range(9):
            a2b_pre(hh[0], step)
            a2b_pre(hh[1], step)
        pre_done["tok"] = kb.tok("V")
        for l in range(NCH):
            for h in hh:
                for st in (a2b_s_xc, a2b_s_tpg, a2b_s_evac, a2b_s_yds,
                           a2b_s_sst, a2b_s_yev, a2b_s_spill):
                    st(h, l)

    kb.free("LC8", "EPR", "EMR", "EP8", "EM8", "XCT", "BPF", "CPF", "ECF",
            "BTM", "GMT", "XTM", "YEV", "TMP1", "SQT", "SST")




def _phase_A3(nc, blk, kb, dr, C):
    kb.region("R1", 8 * T)
    kb.region("GBUF", 16 * 512)
    kb.region("YZP", 4 * 512)
    kb.region("X0B", 2 * 512)
    kb.region("RES", 8 * 512)
    kb.region("WOT", 2 * 16 * 128)
    kb.region("MRS", 96)
    kb.region("ROWS", 3 * 512)
    kb.region("SQ3", 512)

    R1 = kb.view("R1").rearrange("p (c t) -> p c t", t=T)
    GB = kb.view("GBUF").rearrange("p (k t) -> p k t", k=16)
    YZ = kb.view("YZP").rearrange("p (b t) -> p b t", b=4)
    X0B = kb.view("X0B").rearrange("p (b t) -> p b t", b=2)
    RES = kb.view("RES").rearrange("p (c t) -> p c t", c=8)
    WOT = kb.view("WOT").rearrange("p (b k c) -> p b k c", b=2, k=16)
    MRS = kb.view("MRS")
    MUB = MRS[:, 0:8]
    RSB = MRS[:, 8:16]
    ROWS = kb.view("ROWS").rearrange("p (b t) -> p b t", b=3)
    SQ3 = kb.view("SQ3")
    STATf = kb.view("STAT")

    tks = {}

    @blk.tensor
    def _(t):
        kb.wait(t, kb.tok("V"))
        kb.use_bank(t, 0)
        mm = t.matmul(kb.banks[0].ap()[0:1, 0:32], C.ones_col, STATf,
                      start=True, stop=True)
        tks["tsum"] = kb.inc(mm, "T")

    @blk.vector
    def _(v):
        kb.wait(v, tks["tsum"])
        N = float(T * 256)
        kb.chain(v, "V")
        ev = v.tensor_copy(MRS[0:1, 16:48], kb.banks[0].ap()[0:1, 0:32])
        kb.bank_free[0] = kb.inc(ev, "V")
        p2 = lambda o: MRS[0:1, o : o + 16].rearrange("p (h b) -> p h b", b=2)
        o1 = lambda o: MRS[0:1, o : o + 8].rearrange("p (h b) -> p h b", b=1)
        kb.chain(v, "V")
        v.tensor_tensor(o1(48), p2(16)[:, :, 0:1], p2(16)[:, :, 1:2], AL.add)
        kb.inc(KB.nop(v), "V")
        kb.chain(v, "V")
        v.tensor_tensor(o1(56), p2(32)[:, :, 0:1], p2(32)[:, :, 1:2], AL.add)
        kb.inc(KB.nop(v), "V")
        kb.chain(v, "V")
        v.tensor_scalar(MRS[0:1, 64:72], MRS[0:1, 48:56], 1.0 / N, None, AL.mult)
        kb.inc(KB.nop(v), "V")
        kb.chain(v, "V")
        v.tensor_scalar(MRS[0:1, 72:80], MRS[0:1, 56:64], 1.0 / N, None, AL.mult)
        kb.inc(KB.nop(v), "V")
        kb.chain(v, "V")
        v.tensor_tensor(MRS[0:1, 80:88], MRS[0:1, 64:72], MRS[0:1, 64:72], AL.mult)
        kb.inc(KB.nop(v), "V")
        kb.chain(v, "V")
        v.tensor_tensor(MRS[0:1, 72:80], MRS[0:1, 72:80], MRS[0:1, 80:88],
                        AL.subtract)
        tks["var"] = kb.inc(KB.nop(v), "V")

    @blk.scalar
    def _(s):
        kb.wait(s, tks["var"])
        kb.chain(s, "S")
        kb.inc(s.activation(MRS[0:1, 80:88], MRS[0:1, 72:80], AF.Sqrt, bias=C.epscol[0:1, 0:1]), "S")
    s_sd0 = kb.tok("S")

    @blk.vector
    def _(v):
        kb.wait(v, s_sd0)
        kb.chain(v, "V")
        v.reciprocal(MRS[0:1, 88:96], MRS[0:1, 80:88])
        tks["rstd"] = kb.inc(KB.nop(v), "V")

    @blk.tensor
    def _(t):
        kb.wait(t, tks["rstd"])
        kb.use_bank(t, 0)
        t.matmul(kb.banks[0].ap()[:, 0:8], C.ones_row, MRS[0:1, 64:72],
                 start=True, stop=True)
        mm = t.matmul(kb.banks[0].ap()[:, 8:16], C.ones_row, MRS[0:1, 88:96],
                      start=True, stop=True)
        tks["bcmu"] = kb.inc(mm, "T")

    @blk.vector
    def _(v):
        kb.wait(v, tks["bcmu"])
        kb.chain(v, "V")
        ev = v.tensor_copy(MRS[:, 0:16], kb.banks[0].ap()[:, 0:16])
        kb.bank_free[0] = kb.inc(ev, "V")
    v_mrs = kb.tok("V")

    for nt in range(4):
        ntc = slice(512 * nt, 512 * nt + 512)
        for k in range(16):
            j = 16 * nt + k
            pb = j % 2
            tks2 = {}

            @blk.gpsimd
            def _(g, nt=nt, k=k, j=j, pb=pb):
                if j >= 2:
                    kb.wait(g, kb.tok("A3G%d" % pb))
                tks2["y"] = kb.dma(g, "YL%d" % pb, YZ[:, 2 * pb],
                                   dr["y_sp"][k][:, ntc])
                tks2["z"] = kb.dma(g, "ZL%d" % pb, YZ[:, 2 * pb + 1],
                                   dr["z_sp"][k][:, ntc])

            @blk.vector
            def _(v, nt=nt, k=k, pb=pb):
                kb.wait(v, tks2["y"], tks2["z"], v_mrs)
                kb.chain(v, "V")
                hk = k // 2
                v.tensor_scalar(GB[:, k, :], YZ[:, 2 * pb],
                                MUB[:, hk : hk + 1], RSB[:, hk : hk + 1],
                                AL.subtract, AL.mult)
                kb.inc(KB.nop(v), "V")
                kb.chain(v, "V")
                v.tensor_scalar(GB[:, k, :], GB[:, k, :],
                                C.gnw_fm[:, k : k + 1], C.gnb_fm[:, k : k + 1],
                                AL.mult, AL.add)
                kb.inc(KB.nop(v), "V")
                kb.chain(v, "V")
                v.tensor_tensor(GB[:, k, :], GB[:, k, :], YZ[:, 2 * pb + 1], AL.mult)
                kb.inc(KB.nop(v), "V")
                kb.chain(v, "V")
                kb.inc(KB.nop(v), "A3G%d" % pb)
        v_g = kb.tok("V")

        last_st = [None]
        a3wfree = getattr(kb, "_a3wfree", {})
        kb._a3wfree = a3wfree
        for m in range(8):
            wb = m % 2
            bank = 1 + m % 2
            tks3 = {}

            @blk.sync
            def _(s, nt=nt, m=m, wb=wb):
                if wb in a3wfree:
                    kb.wait(s, a3wfree[wb])
                tks3["w"] = kb.dma(s, "WO%d" % wb, WOT[:, wb],
                                   dr["wout_t"][m].rearrange("k p c -> p k c"))

            @blk.gpsimd
            def _(g, nt=nt, m=m):
                if 8 * nt + m >= 2:
                    kb.wait(g, kb.tok("A3X%d" % (m % 2)))
                tks3["x0"] = kb.dma(g, "XL%d" % (m % 2), X0B[:, m % 2],
                                    dr["x0fm"][m][:, ntc])

            @blk.tensor
            def _(t, m=m, wb=wb, bank=bank):
                kb.wait(t, tks3["w"], v_g)
                kb.use_bank(t, bank)
                for k in range(16):
                    mm = t.matmul(kb.banks[bank].ap()[:, :], WOT[:, wb, k, :],
                                  GB[:, k, :], start=(k == 0), stop=(k == 15))
                tks3["mm"] = kb.inc(mm, "T")
                a3wfree[wb] = tks3["mm"]

            @blk.vector
            def _(v, m=m, bank=bank):
                kb.wait(v, tks3["mm"], tks3["x0"])
                kb.chain(v, "V")
                ev = v.tensor_tensor(RES[:, m, :], kb.banks[bank].ap()[:, :],
                                     X0B[:, m % 2], AL.add)
                kb.bank_free[bank] = kb.inc(ev, "V")
                kb.chain(v, "V")
                kb.inc(KB.nop(v), "A3X%d" % (m % 2))
                kb.chain(v, "V")
                kb.wait(v, last_st[0])
                v.tensor_tensor(SQ3, RES[:, m, :], RES[:, m, :], AL.mult)
                tks3["sq"] = kb.inc(KB.nop(v), "V")

            @blk.tensor
            def _(t, m=m):
                if m == 0:
                    kb.use_bank(t, 3, 4)
                kb.wait(t, kb.tok("V"))
                t.matmul(kb.banks[3].ap()[0:1, :], C.ones_col, RES[:, m, :],
                         start=(m == 0), stop=(m == 7))
                mm = t.matmul(kb.banks[4].ap()[0:1, :], C.ones_col, SQ3,
                              start=(m == 0), stop=(m == 7))
                tks3["st"] = last_st[0] = kb.inc(mm, "T")
        t_st = kb.tok("T")

        tks4 = {}

        @blk.vector
        def _(v, nt=nt):
            kb.wait(v, t_st)
            kb.chain(v, "V")
            v.tensor_scalar(ROWS[0:1, 0, :], kb.banks[3].ap()[0:1, :],
                            1.0 / 1024.0, None, AL.mult)
            kb.bank_free[3] = kb.inc(KB.nop(v), "V")
            kb.chain(v, "V")
            v.tensor_scalar(ROWS[0:1, 1, :], kb.banks[4].ap()[0:1, :],
                            1.0 / 1024.0, None, AL.mult)
            kb.bank_free[4] = kb.inc(KB.nop(v), "V")
            kb.chain(v, "V")
            v.tensor_tensor(ROWS[0:1, 2, :], ROWS[0:1, 0, :], ROWS[0:1, 0, :],
                            AL.mult)
            kb.inc(KB.nop(v), "V")
            kb.chain(v, "V")
            v.tensor_tensor(ROWS[0:1, 1, :], ROWS[0:1, 1, :], ROWS[0:1, 2, :],
                            AL.subtract)
            tks4["var"] = kb.inc(KB.nop(v), "V")

        @blk.scalar
        def _(s):
            kb.wait(s, tks4["var"])
            kb.chain(s, "S")
            kb.inc(s.activation(ROWS[0:1, 2, :], ROWS[0:1, 1, :], AF.Sqrt,
                                bias=C.epscol[0:1, 0:1]), "S")
        s_sd2b = kb.tok("S")

        @blk.vector
        def _(v):
            kb.wait(v, s_sd2b)
            kb.chain(v, "V")
            v.reciprocal(ROWS[0:1, 1, :], ROWS[0:1, 2, :])
            tks4["rstd"] = kb.inc(KB.nop(v), "V")

        @blk.tensor
        def _(t):
            kb.wait(t, tks4["rstd"])
            kb.use_bank(t, 5, 6)
            mm = t.matmul(kb.banks[5].ap()[:, :], C.ones_row, ROWS[0:1, 0, :],
                          start=True, stop=True)
            tks4["bmu"] = kb.inc(mm, "T")
            mm = t.matmul(kb.banks[6].ap()[:, :], C.ones_row, ROWS[0:1, 1, :],
                          start=True, stop=True)
            tks4["brs"] = kb.inc(mm, "T")

        @blk.vector
        def _(v, nt=nt):
            kb.wait(v, tks4["bmu"], tks4["brs"])
            for m in range(8):
                kb.chain(v, "V")
                v.tensor_tensor(R1[:, m, ntc], RES[:, m, :], kb.banks[5].ap()[:, :],
                                AL.subtract)
                kb.inc(KB.nop(v), "V")
                kb.chain(v, "V")
                v.tensor_tensor(R1[:, m, ntc], R1[:, m, ntc],
                                kb.banks[6].ap()[:, :], AL.mult)
                kb.inc(KB.nop(v), "V")
            kb.bank_free[5] = kb.tok("V")
            kb.bank_free[6] = kb.tok("V")

    kb.free("GBUF", "YZP", "X0B", "RES", "WOT", "MRS", "ROWS", "SQ3", "STAT")




def kernel(**inputs):
    g = _prep_inputs(inputs)
    x = np.asarray(inputs["L0_attn_input_tensor"], dtype=np.float32)
    nc = build_program()
    in_maps = []
    for b in range(8):
        m = dict(g)
        m["x0fm"] = np.ascontiguousarray(x[b].T).reshape(8, 128, T)
        in_maps.append(m)
    res = run_bass_kernel_spmd(nc, in_maps, core_ids=list(range(8)))
    outs = []
    for b in range(8):
        o = np.asarray(res.results[b]["out_fm"]).reshape(1024, T)
        outs.append(np.ascontiguousarray(o.T))
    return np.stack(outs).astype(np.float32)




def _phase_B(nc, blk, kb, dr, C):
    R1 = kb.view("R1").rearrange("p (c t) -> p c t", t=T)
    kb.region("BSM", 192)
    kb.region("BBIAS", 32)
    kb.region("XVS", 2048)
    kb.region("VCST", 1024)
    kb.region("WMST", 2 * 1024)
    kb.region("UCST", 2 * 128)
    kb.region("QKB", 2 * 512)
    kb.region("SQB", 512)
    kb.region("SQ4", 512)
    kb.region("VT", 16 * 1040)
    kb.region("VF", 512)

    big = kb.big.ap()
    sm = kb.regions["BSM"][0]
    PL = big[:, sm : sm + 8]
    H2 = big[:, sm + 8 : sm + 10]
    LG8 = big[0:8, sm + 10 : sm + 11]
    WCOL = big[0:8, sm + 11 : sm + 12]
    SCL = {p: big[:, sm + 12 + i : sm + 13 + i] for i, p in enumerate("qkvo")}
    NEGMB = big[:, sm + 16 : sm + 32]
    MAXSQ = {"q": big[0:2, sm + 32 : sm + 40], "k": big[0:2, sm + 40 : sm + 48]}
    LROW = big[0:1, sm + 48 : sm + 56]
    E1R = big[0:1, sm + 56 : sm + 64]
    L2R = big[0:1, sm + 64 : sm + 72]
    MKR = big[0:1, sm + 72 : sm + 80]
    PMR = big[0:1, sm + 80 : sm + 88]
    BR = big[0:1, sm + 88 : sm + 96]
    MROW = {"q": big[0:1, sm + 96 : sm + 112], "k": big[0:1, sm + 112 : sm + 128]}
    NMROW = big[0:1, sm + 128 : sm + 144]
    MX8 = big[0:8, sm + 144 : sm + 146]
    SRAW = {p: big[:, sm + 146 + i : sm + 147 + i] for i, p in enumerate("qkvo")}
    GB2R = big[0:1, sm + 152 : sm + 160]
    EHH = big[:, sm + 160 : sm + 162]
    GB1 = big[:, sm + 162 : sm + 164]
    bb = kb.regions["BBIAS"][0]
    PB = {p: big[:, bb + 8 * i : bb + 8 * i + 8] for i, p in enumerate("qkvo")}
    XVS = kb.view("XVS")
    VCST = kb.view("VCST").rearrange("p (k c) -> p k c", k=8)
    WMST = kb.view("WMST").rearrange("p (b k c) -> p b k c", b=2, k=8)
    UCST = kb.view("UCST").rearrange("p (b c) -> p b c", b=2)
    QKB = kb.view("QKB").rearrange("p (b t) -> p b t", b=2)
    SQB = kb.view("SQB")
    SQ4 = kb.view("SQ4")
    VT3 = kb.view("VT").rearrange("p (s c) -> p s c", c=1040)
    VF = kb.view("VF")

    kb.region("GRT", 2048 + 128 + 16)
    gr = kb.regions["GRT"][0]
    GW1 = big[:, gr : gr + 2048].rearrange("p (m k c) -> p m k c", m=2, k=8)
    EWc = big[0:8, gr + 2048 : gr + 2176]
    GW2 = big[:, gr + 2176 : gr + 2192].rearrange("p (k c) -> p k c", k=2)

    tks = {}

    @blk.sync
    def _(s):
        for mi in range(2):
            kb.dma(s, "LC0", GW1[:, mi], dr["gw1_t"][mi].rearrange("k p c -> p k c"))
        kb.dma(s, "LC1", EWc, dr["c_ew"])
        kb.dma(s, "LC0", GW2, dr["gw2_t"][0].rearrange("k p c -> p k c"))
        kb.dma(s, "LC1", GB1, dr["gb1_fm"])
        kb.dma(s, "LC0", GB2R, dr["gb2_row"])
        kb.dma(s, "LC1", EHH, dr["c_ehh"])
        for i, p in enumerate("qkvo"):
            kb.dma(s, "LC0", PB[p], dr[p + "b_fm"])
            kb.dma(s, "LC1", SRAW[p], dr[p + "s_col"])
    lcend = (kb.tok("LC0"), kb.tok("LC1"))

    @blk.vector
    def _(v):
        kb.wait(v, *lcend)
        kb.chain(v, "V")
        v.reduce_sum(PL.rearrange("p (a b) -> p a b", b=1), R1, axis=AX)
        tks["pl"] = kb.inc(KB.nop(v), "V")

    @blk.tensor
    def _(t):
        kb.wait(t, tks["pl"])
        kb.use_bank(t, 0)
        for mi in range(2):
            for k in range(8):
                mm = t.matmul(kb.banks[0].ap()[:, mi : mi + 1], GW1[:, mi, k, :],
                              PL[:, k : k + 1], start=(k == 0), stop=(k == 7))
        tks["h"] = kb.inc(mm, "T")

    @blk.scalar
    def _(s):
        kb.wait(s, tks["h"])
        for mi in range(2):
            kb.chain(s, "S")
            ev = s.activation(H2[:, mi : mi + 1], kb.banks[0].ap()[:, mi : mi + 1],
                              AF.Relu, bias=GB1[:, mi : mi + 1], scale=1.0 / T)
            kb.bank_free[0] = kb.inc(ev, "S")
    tks["h2"] = kb.tok("S")

    @blk.tensor
    def _(t):
        kb.wait(t, tks["h2"])
        kb.use_bank(t, 1)
        for k in range(2):
            mm = t.matmul(kb.banks[1].ap()[0:8, 0:1], GW2[:, k, :],
                          H2[:, k : k + 1], start=(k == 0), stop=(k == 1))
        tks["lg"] = kb.inc(mm, "T")

    @blk.vector
    def _(v):
        kb.wait(v, tks["lg"])
        kb.chain(v, "V")
        ev = v.tensor_copy(LG8, kb.banks[1].ap()[0:8, 0:1])
        kb.bank_free[1] = kb.inc(ev, "V")
        tks["lg8"] = kb.bank_free[1]

    @blk.gpsimd
    def _(g):
        kb.wait(g, tks["lg8"])
        tks["lgt"] = kb.dma(g, "TRX", LROW.rearrange("p (a c) -> p a c", a=8),
                            LG8)

    @blk.vector
    def _(v):
        kb.wait(v, tks["lgt"])
        kb.chain(v, "V")
        ev = v.tensor_tensor(LROW, LROW, GB2R, AL.add)
        kb.inc(ev, "V")
        for op in range(5):
            kb.chain(v, "V")
            if op == 0:
                v.reduce_max(BR[:, 0:1], LROW, axis=AX)
            elif op == 1:
                v.tensor_scalar(E1R, LROW, BR[:, 0:1], None, AL.is_equal)
            elif op == 2:
                v.scalar_tensor_tensor(L2R, E1R, -1.0e30, LROW, AL.mult, AL.add)
            elif op == 3:
                v.reduce_max(BR[:, 1:2], L2R, axis=AX)
            elif op == 4:
                v.tensor_scalar(MKR, LROW, BR[:, 1:2], None, AL.is_ge)
            kb.inc(KB.nop(v), "V")
        kb.chain(v, "V")
        v.tensor_scalar(BR[:, 3:4], BR[:, 0:1], -1.0, None, AL.mult)
        tks["brm"] = kb.inc(KB.nop(v), "V")

    @blk.scalar
    def _(s):
        kb.wait(s, tks["brm"])
        kb.chain(s, "S")
        kb.inc(s.activation(L2R, LROW, AF.Exp, bias=BR[:, 3:4]), "S")
    tks["pexp"] = kb.tok("S")

    @blk.vector
    def _(v):
        kb.wait(v, tks["pexp"])
        for op in range(4):
            kb.chain(v, "V")
            if op == 0:
                v.tensor_tensor(PMR, L2R, MKR, AL.mult)
            elif op == 1:
                v.reduce_sum(BR[:, 2:3], PMR, axis=AX)
            elif op == 2:
                v.reciprocal(BR[:, 4:5], BR[:, 2:3])
            elif op == 3:
                v.tensor_scalar(PMR, PMR, BR[:, 4:5], None, AL.mult)
            kb.inc(KB.nop(v), "V")
        tks["wrow"] = kb.tok("V")

    @blk.gpsimd
    def _(g):
        kb.wait(g, tks["wrow"])
        tks["wcol"] = kb.dma(g, "TRX", WCOL,
                             PMR.rearrange("p (a c) -> p a c", a=8))

    @blk.tensor
    def _(t):
        kb.wait(t, tks["wcol"])
        kb.use_bank(t, 4)
        tks["w128"] = kb.inc(
            t.matmul(kb.banks[4].ap()[:, 0:1], EWc, WCOL, start=True, stop=True), "T")

    @blk.vector
    def _(v):
        kb.wait(v, tks["w128"])
        for p in "qkvo":
            kb.chain(v, "V")
            kb.inc(v.tensor_tensor(SCL[p], SRAW[p], kb.banks[4].ap()[:, 0:1],
                                   AL.mult), "V")
        kb.bank_free[4] = kb.tok("V")
        kb.chain(v, "V")
        v.memset(MAXSQ["q"], 0.0)
        kb.inc(KB.nop(v), "V")
        kb.chain(v, "V")
        v.memset(MAXSQ["k"], 0.0)
        kb.inc(KB.nop(v), "V")
        kb.chain(v, "V")
        v.memset(VT3.rearrange("p s (h d) -> p s h d", d=65)[:, :, :, 64:65], 1.0)
        kb.inc(KB.nop(v), "V")
    b1_done = kb.tok("V")
    kb.free("GRT")

    wfree = {}
    for p in ("q", "k", "v"):
        @blk.sync
        def _(s, p=p):
            if "vc" in wfree:
                kb.wait(s, wfree["vc"])
            tks["vc"] = kb.dma(s, "VC", VCST,
                               dr[p + "vcat_t"][0].rearrange("k p c -> p k c"))

        for nt in range(4):
            @blk.tensor
            def _(t, p=p, nt=nt):
                kb.wait(t, tks["vc"], b1_done)
                kb.use_bank(t, 0)
                for k in range(8):
                    mm = t.matmul(kb.banks[0].ap()[:, :], VCST[:, k, :],
                                  R1[:, k, 512 * nt : 512 * nt + 512],
                                  start=(k == 0), stop=(k == 7))
                tks["xv"] = kb.inc(mm, "T")
                if nt == 3:
                    wfree["vc"] = tks["xv"]

            @blk.vector
            def _(v, p=p, nt=nt):
                kb.wait(v, tks["xv"])
                kb.chain(v, "V")
                ev = v.tensor_scalar(XVS[:, 512 * nt : 512 * nt + 512],
                                     kb.banks[0].ap()[:, :], SCL[p], None, AL.mult)
                kb.bank_free[0] = kb.inc(ev, "V")
        xvs_done = kb.tok("V")

        for oc in range(8):
            wb = oc % 2

            @blk.sync
            def _(s, p=p, oc=oc, wb=wb):
                if ("wm", wb) in wfree:
                    kb.wait(s, wfree[("wm", wb)])
                tks["wm"] = kb.dma(s, "WM%d" % wb, WMST[:, wb],
                                   dr[p + "wm_t"][oc].rearrange("k p c -> p k c"))
                if ("uc", wb) in wfree:
                    kb.wait(s, wfree[("uc", wb)])
                tks["uc"] = kb.dma(s, "UC%d" % wb, UCST[:, wb],
                                   dr[p + "ucat_t"][oc][0])

            for nt in range(4):
                bo = 1 + nt % 2

                @blk.tensor
                def _(t, p=p, oc=oc, nt=nt, wb=wb, bo=bo, xvs_done=xvs_done):
                    kb.wait(t, tks["wm"], tks["uc"], xvs_done)
                    kb.use_bank(t, bo)
                    for k in range(8):
                        t.matmul(kb.banks[bo].ap()[:, :], WMST[:, wb, k, :],
                                 R1[:, k, 512 * nt : 512 * nt + 512],
                                 start=(k == 0), stop=False)
                    mm = t.matmul(kb.banks[bo].ap()[:, :], UCST[:, wb],
                                  XVS[:, 512 * nt : 512 * nt + 512],
                                  start=False, stop=True)
                    tks["pr"] = kb.inc(mm, "T")
                    if nt == 3:
                        wfree[("wm", wb)] = tks["pr"]
                        wfree[("uc", wb)] = tks["pr"]

                if p in ("q", "k"):
                    spname = {"q": "q0_sp", "k": "k0_sp"}[p]

                    @blk.vector
                    def _(v, p=p, oc=oc, nt=nt, bo=bo):
                        kb.wait(v, tks["pr"])
                        if ("qkb", nt % 2) in wfree:
                            kb.wait(v, wfree[("qkb", nt % 2)])
                        kb.chain(v, "V")
                        if p == "q":
                            ev = v.tensor_scalar(QKB[:, nt % 2],
                                                 kb.banks[bo].ap()[:, :],
                                                 PB[p][:, oc : oc + 1], 0.125,
                                                 AL.add, AL.mult)
                        else:
                            ev = v.tensor_scalar(QKB[:, nt % 2],
                                                 kb.banks[bo].ap()[:, :],
                                                 PB[p][:, oc : oc + 1], None, AL.add)
                        kb.bank_free[bo] = kb.inc(ev, "V")
                        tks["qkev"] = kb.bank_free[bo]

                    @blk.scalar
                    def _(s, nt=nt):
                        kb.wait(s, tks["qkev"])
                        if "sqb" in wfree:
                            kb.wait(s, wfree["sqb"])
                        kb.chain(s, "S")
                        kb.inc(s.activation(SQB, QKB[:, nt % 2], AF.Square), "S")
                        tks["sq"] = kb.tok("S")

                    @blk.tensor
                    def _(t):
                        kb.wait(t, tks["sq"])
                        kb.use_bank(t, 3)
                        mm = t.matmul(kb.banks[3].ap()[0:2, :], EHH, SQB,
                                      start=True, stop=True)
                        tks["ss"] = kb.inc(mm, "T")
                        wfree["sqb"] = tks["ss"]

                    @blk.vector
                    def _(v, p=p, oc=oc):
                        kb.wait(v, tks["ss"])
                        kb.chain(v, "V")
                        ev = v.tensor_copy(SQ4[0:2, 0:512], kb.banks[3].ap()[0:2, :])
                        kb.bank_free[3] = kb.inc(ev, "V")
                        kb.chain(v, "V")
                        v.reduce_max(MX8[0:2, 0:1], SQ4[0:2, 0:512], axis=AX)
                        kb.inc(KB.nop(v), "V")
                        kb.chain(v, "V")
                        v.tensor_tensor(MAXSQ[p][:, oc : oc + 1],
                                        MAXSQ[p][:, oc : oc + 1], MX8[0:2, 0:1],
                                        AL.max)
                        kb.inc(KB.nop(v), "V")

                    @blk.gpsimd
                    def _(g, oc=oc, nt=nt, spname=spname):
                        kb.wait(g, tks["qkev"])
                        wfree[("qkb", nt % 2)] = kb.dma(
                            g, "QS%d" % (nt % 2),
                            dr[spname][oc][:, 512 * nt : 512 * nt + 512],
                            QKB[:, nt % 2])
                else:
                    @blk.vector
                    def _(v, oc=oc, nt=nt, bo=bo):
                        kb.wait(v, tks["pr"])
                        if "vf" in wfree:
                            kb.wait(v, wfree["vf"])
                        kb.chain(v, "V")
                        ev = v.tensor_scalar(VF, kb.banks[bo].ap()[:, :],
                                             PB["v"][:, oc : oc + 1], None, AL.add)
                        kb.bank_free[bo] = kb.inc(ev, "V")
                        tks["vf"] = kb.bank_free[bo]

                    for sub in range(4):
                        @blk.tensor
                        def _(t, sub=sub):
                            kb.wait(t, tks["vf"])
                            kb.use_bank(t, 6)
                            tks["vt"] = kb.inc(
                                t.transpose(kb.banks[6].ap()[:, 0:128],
                                            VF[:, 128 * sub : 128 * sub + 128],
                                            C.ident), "T")

                        @blk.vector
                        def _(v, oc=oc, nt=nt, sub=sub):
                            kb.wait(v, tks["vt"])
                            tt = 4 * nt + sub
                            h2o = oc // 2
                            kb.chain(v, "V")
                            dst = VT3[:, tt, :].rearrange("c (h d) -> c h d", d=65)
                            ev = v.tensor_copy(
                                dst[:, 2 * oc : 2 * oc + 2, 0:64],
                                kb.banks[6].ap()[:, 0:128].rearrange(
                                    "p (h d) -> p h d", d=64))
                            kb.bank_free[6] = kb.inc(ev, "V")
                            if sub == 3:
                                wfree["vf"] = kb.bank_free[6]
        if p == "v":
            b2v_done = kb.tok("V")

    @blk.gpsimd
    def _(g):
        kb.wait(g, b2v_done)
        for c in range(8):
            tks["r1s"] = kb.dma(g, "RS%d" % (c % 2), dr["r1_sp"][c], R1[:, c, :])
    r1_spilled = (kb.tok("RS0"), kb.tok("RS1"))

    @blk.gpsimd
    def _(g):
        kb.wait(g, kb.tok("V"))
        for pp in ("q", "k"):
            for half in range(2):
                tks["mrow"] = kb.dma(
                    g, "MQ",
                    MROW[pp].rearrange("p (c h) -> p c h", c=8)[:, :, half],
                    MAXSQ[pp][half : half + 1, :], slow=True)

    @blk.scalar
    def _(s):
        kb.wait(s, tks["mrow"])
        kb.chain(s, "S")
        kb.inc(s.activation(MROW["q"], MROW["q"], AF.Sqrt), "S")
        kb.chain(s, "S")
        kb.inc(s.activation(MROW["k"], MROW["k"], AF.Sqrt), "S")
    tks["msq"] = kb.tok("S")

    @blk.vector
    def _(v):
        kb.wait(v, tks["msq"])
        kb.chain(v, "V")
        v.tensor_tensor(NMROW, MROW["q"], MROW["k"], AL.mult)
        kb.inc(KB.nop(v), "V")
        kb.chain(v, "V")
        v.tensor_scalar(NMROW, NMROW, -1.0, None, AL.mult)
        tks["nm"] = kb.inc(KB.nop(v), "V")

    @blk.tensor
    def _(t):
        kb.wait(t, tks["nm"])
        kb.use_bank(t, 0)
        tks["nmb"] = kb.inc(
            t.matmul(kb.banks[0].ap()[:, 0:16], C.ones_row, NMROW,
                     start=True, stop=True), "T")

    @blk.vector
    def _(v):
        kb.wait(v, tks["nmb"])
        kb.chain(v, "V")
        ev = v.tensor_copy(NEGMB, kb.banks[0].ap()[:, 0:16])
        kb.bank_free[0] = kb.inc(ev, "V")
    negmb_done = kb.tok("V")

    kb.free("R1", "XVS", "VCST", "QKB", "SQB")
    _phase_B34(nc, blk, kb, dr, C, dict(
        VT3=VT3, CTX_done=None, NEGMB=NEGMB, SCL=SCL, PB=PB,
        negmb_done=negmb_done, r1_spilled=r1_spilled,
        wm0=wfree.get(("wm", 0)), wm1=wfree.get(("wm", 1)),
        uc0=wfree.get(("uc", 0)), uc1=wfree.get(("uc", 1)),
        vc=wfree.get("vc")))
    kb.free("BSM", "BBIAS", "WMST", "UCST", "VT", "VF", "SQ4")


def _phase_B34(nc, blk, kb, dr, C, E):
    VT3, NEGMB, SCL, PB = E["VT3"], E["NEGMB"], E["SCL"], E["PB"]
    kb.region("CTXQ", 8 * 512)
    kb.region("KH", 2 * 2048)
    kb.region("QH", 2 * 512)
    kb.region("PT", 2 * 512)
    kb.region("BCV", 512)
    kb.region("XVO", 512)
    kb.region("OVC", 1024)
    kb.region("RES2", 8 * 512)
    kb.region("ROW3", 3 * 512)
    kb.region("SQ5", 512)
    kb.region("R1B", 2 * 512)
    CTXQ = kb.view("CTXQ").rearrange("p (c t) -> p c t", c=8)
    KH = kb.view("KH").rearrange("p (b t) -> p b t", b=2)[0:64]
    QH = kb.view("QH").rearrange("p (b t) -> p b t", b=2)[0:64]
    PT = kb.view("PT").rearrange("p (b t) -> p b t", b=2)
    BCV = kb.view("BCV")[0:64, :]
    XVO = kb.view("XVO")
    OVC = kb.view("OVC").rearrange("p (k c) -> p k c", k=8)
    RES2 = kb.view("RES2").rearrange("p (c t) -> p c t", c=8)
    ROW3 = kb.view("ROW3").rearrange("p (b t) -> p b t", b=3)
    SQ5 = kb.view("SQ5")
    R1B = kb.view("R1B").rearrange("p (b t) -> p b t", b=2)
    WMST = kb.view("WMST").rearrange("p (b k c) -> p b k c", b=2, k=8)
    UCST = kb.view("UCST").rearrange("p (b c) -> p b c", b=2)

    tks = {}
    wfree = {("owm", 0): E["wm0"], ("owm", 1): E["wm1"],
             ("ouc", 0): E["uc0"], ("ouc", 1): E["uc1"]}
    wfree = {k: v for k, v in wfree.items() if v is not None}
    kfree = {}

    b2t = kb.tok("T")
    b2s = kb.tok("S")
    b2qs = (kb.tok("QS0"), kb.tok("QS1"))

    @blk.sync
    def _(s):
        if E["vc"] is not None:
            kb.wait(s, E["vc"])
        kb.wait(s, b2t)
        tks["ovc"] = kb.dma(s, "VC", OVC,
                            dr["ovcat_t"][0].rearrange("k p c -> p k c"))

    for qt in range(4):
        qcol = slice(512 * qt, 512 * qt + 512)
        for h in range(16):
            hb = h % 2
            ih = 16 * qt + h

            @blk.gpsimd
            def _(g, qt=qt, h=h, hb=hb, ih=ih):
                if ih == 0:
                    kb.wait(g, b2t, b2s, *b2qs)
                if ("kh", hb) in kfree:
                    kb.wait(g, kfree[("kh", hb)])
                tks["kh"] = kb.dma(g, "KH%d" % hb, KH[:, hb, :],
                                   dr["k0_sp"][h // 2][64 * (h % 2) : 64 * (h % 2) + 64, :])
                if ("qh", hb) in kfree:
                    kb.wait(g, kfree[("qh", hb)])
                tks["qh"] = kb.dma(g, "QH%d" % hb, QH[:, hb, :],
                                   dr["q0_sp"][h // 2][64 * (h % 2) : 64 * (h % 2) + 64,
                                                      512 * qt : 512 * qt + 512])

            cbk = 2 + h % 2
            st_toks = {}

            def emit_S(kt, h=h, hb=hb, qt=qt):
                sb = kt % 2

                @blk.tensor
                def _(t):
                    kb.wait(t, tks["kh"], tks["qh"], E["negmb_done"])
                    kb.use_bank(t, sb)
                    mm = t.matmul(kb.banks[sb].ap()[:, :],
                                  KH[:, hb, 128 * kt : 128 * kt + 128],
                                  QH[:, hb, :], start=True, stop=True)
                    st_toks[kt] = kb.inc(mm, "T")
                    if kt >= 14:
                        kfree[("kh", hb)] = st_toks[kt]
                    if kt == 15:
                        kfree[("qh", hb)] = st_toks[kt]

            emit_S(0)
            for kt in range(16):
                sb = kt % 2
                if kt + 1 <= 15:
                    emit_S(kt + 1)

                @blk.scalar
                def _(s, h=h, kt=kt, sb=sb):
                    kb.wait(s, st_toks[kt])
                    if ("pt", sb) in kfree:
                        kb.wait(s, kfree[("pt", sb)])
                    kb.chain(s, "S")
                    ev = s.activation(PT[:, sb], kb.banks[sb].ap()[:, :], AF.Exp,
                                      bias=NEGMB[:, h : h + 1])
                    kb.bank_free[sb] = kb.inc(ev, "S")
                    tks["pt"] = kb.tok("S")

                @blk.tensor
                def _(t, h=h, kt=kt, sb=sb, cbk=cbk):
                    kb.wait(t, tks["pt"])
                    if kt == 0:
                        kb.use_bank(t, cbk)
                    mm = t.matmul(kb.banks[cbk].ap()[0:65, :],
                                  VT3[:, kt, 65 * h : 65 * h + 65], PT[:, sb],
                                  start=(kt == 0), stop=(kt == 15))
                    tks["ctx"] = kb.inc(mm, "T")
                    kfree[("pt", sb)] = tks["ctx"]

            @blk.vector
            def _(v, h=h, cbk=cbk):
                kb.wait(v, tks["ctx"])
                kb.chain(v, "V")
                v.reciprocal(ROW3[0:1, 2, :], kb.banks[cbk].ap()[64:65, :])
                tks["dr"] = kb.inc(KB.nop(v), "V")

            @blk.tensor
            def _(t, h=h):
                kb.wait(t, tks["dr"])
                kb.use_bank(t, 6)
                tks["bc"] = kb.inc(
                    t.matmul(kb.banks[6].ap()[0:64, :], C.ones_row[0:1, 0:64],
                             ROW3[0:1, 2, :], start=True, stop=True), "T")

            @blk.vector
            def _(v, qt=qt, h=h, cbk=cbk):
                kb.wait(v, tks["bc"])
                kb.chain(v, "V")
                ev = v.tensor_copy(BCV, kb.banks[6].ap()[0:64, :])
                kb.bank_free[6] = kb.inc(ev, "V")
                kb.chain(v, "V")
                if "ctxq_free" in wfree:
                    kb.wait(v, wfree["ctxq_free"])
                hp = 64 * (h % 2)
                ev = v.tensor_tensor(CTXQ[hp : hp + 64, h // 2, :],
                                     kb.banks[cbk].ap()[0:64, :], BCV, AL.mult)
                kb.bank_free[cbk] = kb.inc(ev, "V")
        ctx_done = kb.tok("V")

        @blk.tensor
        def _(t, qt=qt):
            kb.wait(t, tks["ovc"], ctx_done)
            kb.use_bank(t, 0)
            for k in range(8):
                mm = t.matmul(kb.banks[0].ap()[:, :], OVC[:, k, :], CTXQ[:, k, :],
                              start=(k == 0), stop=(k == 7))
            tks["xvo"] = kb.inc(mm, "T")

        @blk.vector
        def _(v):
            kb.wait(v, tks["xvo"])
            kb.chain(v, "V")
            ev = v.tensor_scalar(XVO, kb.banks[0].ap()[:, :], SCL["o"], None,
                                 AL.mult)
            kb.bank_free[0] = kb.inc(ev, "V")
        xvo_done = kb.tok("V")

        last_st = [None]
        for oc in range(8):
            wb = oc % 2
            bo = 4 + wb

            @blk.sync
            def _(s, qt=qt, oc=oc, wb=wb):
                if ("owm", wb) in wfree:
                    kb.wait(s, wfree[("owm", wb)])
                tks["owm"] = kb.dma(s, "WM%d" % wb, WMST[:, wb],
                                    dr["owm_t"][oc].rearrange("k p c -> p k c"))
                if ("ouc", wb) in wfree:
                    kb.wait(s, wfree[("ouc", wb)])
                tks["ouc"] = kb.dma(s, "UC%d" % wb, UCST[:, wb],
                                    dr["oucat_t"][oc][0])

            @blk.gpsimd
            def _(g, qt=qt, oc=oc):
                if ("r1b", oc % 2) in wfree:
                    kb.wait(g, wfree[("r1b", oc % 2)])
                kb.wait(g, *E["r1_spilled"])
                tks["r1b"] = kb.dma(g, "RB%d" % (oc % 2), R1B[:, oc % 2],
                                    dr["r1_sp"][oc][:, 512 * qt : 512 * qt + 512])

            @blk.tensor
            def _(t, qt=qt, oc=oc, wb=wb, bo=bo, xvo_done=xvo_done):
                kb.wait(t, tks["owm"], tks["ouc"], xvo_done)
                kb.use_bank(t, bo)
                for k in range(8):
                    t.matmul(kb.banks[bo].ap()[:, :], WMST[:, wb, k, :],
                             CTXQ[:, k, :], start=(k == 0), stop=False)
                mm = t.matmul(kb.banks[bo].ap()[:, :], UCST[:, wb], XVO,
                              start=False, stop=True)
                tks["opr"] = kb.inc(mm, "T")
                wfree[("owm", wb)] = tks["opr"]
                wfree[("ouc", wb)] = tks["opr"]
                if oc == 7:
                    wfree["ctxq_free"] = tks["opr"]

            @blk.vector
            def _(v, oc=oc, bo=bo):
                kb.wait(v, tks["opr"], tks["r1b"])
                if ("r2s", oc) in wfree:
                    kb.wait(v, wfree[("r2s", oc)])
                kb.chain(v, "V")
                ev = v.scalar_tensor_tensor(RES2[:, oc, :], kb.banks[bo].ap()[:, :],
                                            PB["o"][:, oc : oc + 1], R1B[:, oc % 2],
                                            AL.add, AL.add)
                kb.bank_free[bo] = kb.inc(ev, "V")
                wfree[("r1b", oc % 2)] = kb.bank_free[bo]
                kb.chain(v, "V")
                kb.wait(v, last_st[0])
                v.tensor_tensor(SQ5, RES2[:, oc, :], RES2[:, oc, :], AL.mult)
                tks["sq"] = kb.inc(KB.nop(v), "V")

            @blk.tensor
            def _(t, oc=oc):
                if oc == 0:
                    kb.use_bank(t, 6, 7)
                kb.wait(t, kb.tok("V"))
                t.matmul(kb.banks[6].ap()[0:1, :], C.ones_col, RES2[:, oc, :],
                         start=(oc == 0), stop=(oc == 7))
                mm = t.matmul(kb.banks[7].ap()[0:1, :], C.ones_col, SQ5,
                              start=(oc == 0), stop=(oc == 7))
                tks["lnst"] = last_st[0] = kb.inc(mm, "T")

        @blk.vector
        def _(v):
            kb.wait(v, tks["lnst"])
            kb.chain(v, "V")
            v.tensor_scalar(ROW3[0:1, 0, :], kb.banks[6].ap()[0:1, :],
                            1.0 / 1024.0, None, AL.mult)
            kb.bank_free[6] = kb.inc(KB.nop(v), "V")
            kb.chain(v, "V")
            v.tensor_scalar(ROW3[0:1, 1, :], kb.banks[7].ap()[0:1, :],
                            1.0 / 1024.0, None, AL.mult)
            kb.bank_free[7] = kb.inc(KB.nop(v), "V")
            kb.chain(v, "V")
            v.tensor_tensor(ROW3[0:1, 2, :], ROW3[0:1, 0, :], ROW3[0:1, 0, :],
                            AL.mult)
            kb.inc(KB.nop(v), "V")
            kb.chain(v, "V")
            v.tensor_tensor(ROW3[0:1, 1, :], ROW3[0:1, 1, :], ROW3[0:1, 2, :],
                            AL.subtract)
            tks["var"] = kb.inc(KB.nop(v), "V")

        @blk.scalar
        def _(s):
            kb.wait(s, tks["var"])
            kb.chain(s, "S")
            kb.inc(s.activation(ROW3[0:1, 2, :], ROW3[0:1, 1, :], AF.Sqrt,
                                bias=C.epscol[0:1, 0:1]), "S")
        s_sdb4 = kb.tok("S")

        @blk.vector
        def _(v):
            kb.wait(v, s_sdb4)
            kb.chain(v, "V")
            v.reciprocal(ROW3[0:1, 1, :], ROW3[0:1, 2, :])
            tks["rstd"] = kb.inc(KB.nop(v), "V")

        @blk.tensor
        def _(t):
            kb.wait(t, tks["rstd"])
            kb.use_bank(t, 6, 7)
            tks["bmu"] = kb.inc(
                t.matmul(kb.banks[6].ap()[:, :], C.ones_row, ROW3[0:1, 0, :],
                         start=True, stop=True), "T")
            tks["brs"] = kb.inc(
                t.matmul(kb.banks[7].ap()[:, :], C.ones_row, ROW3[0:1, 1, :],
                         start=True, stop=True), "T")

        for oc in range(8):
            @blk.vector
            def _(v, qt=qt, oc=oc):
                kb.wait(v, tks["bmu"], tks["brs"])
                if ("r2s", oc) in wfree:
                    kb.wait(v, wfree[("r2s", oc)])
                kb.chain(v, "V")
                v.tensor_tensor(RES2[:, oc, :], RES2[:, oc, :],
                                kb.banks[6].ap()[:, :], AL.subtract)
                kb.inc(KB.nop(v), "V")
                kb.chain(v, "V")
                v.tensor_tensor(RES2[:, oc, :], RES2[:, oc, :],
                                kb.banks[7].ap()[:, :], AL.mult)
                tks["r2"] = kb.inc(KB.nop(v), "V")

            @blk.gpsimd
            def _(g, qt=qt, oc=oc):
                kb.wait(g, tks["r2"])
                wfree[("r2s", oc)] = kb.dma(
                    g, "R2%d" % (oc % 2), dr["r2_sp"][oc][:, 512 * qt : 512 * qt + 512],
                    RES2[:, oc, :])
        kb.bank_free[6] = kb.tok("V")
        kb.bank_free[7] = kb.tok("V")
    r2_done = [kb.tok("R20"), kb.tok("R21")]
    kb._r2_done = r2_done

    kb.free("CTXQ", "KH", "QH", "PT", "BCV", "XVO", "OVC", "RES2", "ROW3",
            "SQ5", "R1B")




def _phase_C(nc, blk, kb, dr, C):
    kb.region("CSM", 128)
    kb.region("VT", 16 * 1040)
    kb.region("R2N", 2 * 8 * 512)
    kb.region("WVM", 8 * 2 * 512)
    kb.region("WQK", 2 * 1024)
    kb.region("QKE", 2 * 512)
    kb.region("CP1", 136)
    kb.region("SQP", 2 * 512)
    kb.region("SDR", 2 * 512)

    big = kb.big.ap()
    VT3 = kb.view("VT").rearrange("p (s c) -> p s c", c=1040)
    R2N = kb.view("R2N").rearrange("p (b k t) -> p b k t", b=2, k=8)
    WVM = kb.view("WVM").rearrange("p (k o t) -> p k o t", k=8, o=2)
    WQK = kb.view("WQK").rearrange("p (b k c) -> p b k c", b=2, k=8)
    QKE = kb.view("QKE").rearrange("p (b t) -> p b t", b=2)
    cs = kb.regions["CSM"][0]
    UNT = big[:, cs : cs + 16]
    U16B = big[0:16, cs + 16 : cs + 80]
    UN1 = big[0:16, cs + 80 : cs + 81]
    OMEF = big[0:1, cs + 81 : cs + 97]
    NMC = big[:, cs + 97 : cs + 98]
    QWS = big[0:64, cs + 98 : cs + 99]
    KWS = big[0:64, cs + 99 : cs + 100]
    WROW = big[0:1, cs + 100 : cs + 116]
    cp = kb.regions["CP1"][0]
    EHHC = big[:, cp : cp + 2]
    E2TC = big[0:2, cp + 2 : cp + 130]
    QW2 = big[:, cp + 130 : cp + 131]
    KW2 = big[:, cp + 131 : cp + 132]
    SQP = kb.view("SQP").rearrange("p (b t) -> p b t", b=2)
    SDR = kb.view("SDR").rearrange("p (b t) -> p b t", b=2)[0:2]
    tks = {}
    wfree = {}

    @blk.sync
    def _(s):
        kb.wait(s, *kb._r2_done)
        for k in range(8):
            tks["wv"] = kb.dma(s, "LC0", WVM[:, k],
                               dr["wv_m"][k].rearrange("o p c -> p o c"))
        kb.dma(s, "LC1", EHHC, dr["c_ehh"])
        kb.dma(s, "LC0", E2TC, dr["c_e2t"])
        kb.dma(s, "LC1", QW2, dr["qrw2_col"])
        kb.dma(s, "LC0", KW2, dr["krw2_col"])
    wv_done = tks["wv"]
    cp1_done = (kb.tok("LC0"), kb.tok("LC1"))

    tk2 = {}
    emitted = set()

    def _qst(gi):
        nt, rem = divmod(gi, 16)
        pj, oc = divmod(rem, 8)
        return (nt, pj, oc, gi % 2,
                "wq_t" if pj == 0 else "wk_t",
                "q1_sp" if pj == 0 else "k1_sp")

    def emit_wload(gi):
        nt, pj, oc, wb, wt, sp = _qst(gi)

        @blk.sync
        def _(s, gi=gi, oc=oc, wb=wb, wt=wt):
            if ("wqk", wb) in wfree:
                kb.wait(s, wfree[("wqk", wb)])
            tk2[("w", gi)] = kb.dma(s, "WM%d" % wb, WQK[:, wb],
                                    dr[wt][oc].rearrange("k p c -> p k c"))

    def emit_proj(gi, r2n_tok):
        nt, pj, oc, wb, wt, sp = _qst(gi)
        nb = nt % 2

        @blk.tensor
        def _(t, gi=gi, nb=nb, wb=wb, pj=pj, oc=oc, r2n_tok=r2n_tok):
            kb.wait(t, tk2[("w", gi)], r2n_tok)
            kb.use_bank(t, wb)
            for k in range(8):
                mm = t.matmul(kb.banks[wb].ap()[:, :], WQK[:, wb, k, :],
                              R2N[:, nb, k, :], start=(k == 0), stop=(k == 7))
            tk2[("pr", gi)] = kb.inc(mm, "T")
            wfree[("wqk", wb)] = tk2[("pr", gi)]
            if pj == 1 and oc == 7:
                wfree[("r2n_qk", nb)] = tk2[("pr", gi)]

    def emit_square(gi):
        wb = gi % 2

        @blk.scalar
        def _(s, gi=gi, wb=wb):
            kb.wait(s, tk2[("pr", gi)])
            if ("sqp", wb) in wfree:
                kb.wait(s, wfree[("sqp", wb)])
            kb.chain(s, "S")
            kb.inc(s.activation(SQP[:, wb], kb.banks[wb].ap()[:, :], AF.Square),
                   "S")
            tk2[("sqp", gi)] = kb.tok("S")

    def emit_stat(gi):
        if ("sqp", gi) not in tk2 or ("stat", gi) in emitted:
            return
        emitted.add(("stat", gi))
        wb = gi % 2

        @blk.tensor
        def _(t, gi=gi, wb=wb):
            kb.wait(t, tk2[("sqp", gi)])
            if gi < 2:
                kb.wait(t, *cp1_done)
            kb.use_bank(t, 6)
            mm = t.matmul(kb.banks[6].ap()[0:2, :], EHHC, SQP[:, wb],
                          start=True, stop=True)
            tk2[("stat", gi)] = kb.inc(mm, "T")
            wfree[("sqp", wb)] = tk2[("stat", gi)]

    def emit_sqrt(gi):
        if ("stat", gi) not in tk2 or ("sqrt", gi) in emitted:
            return
        emitted.add(("sqrt", gi))
        wb = gi % 2

        @blk.scalar
        def _(s, gi=gi, wb=wb):
            kb.wait(s, tk2[("stat", gi)])
            if ("sdr", wb) in wfree:
                kb.wait(s, wfree[("sdr", wb)])
            kb.chain(s, "S")
            ev = s.activation(SDR[:, wb], kb.banks[6].ap()[0:2, :], AF.Sqrt,
                              bias=C.epscol[0:2, 0:1], scale=1.0 / 64.0)
            kb.bank_free[6] = kb.inc(ev, "S")
            tk2[("sd", gi)] = kb.tok("S")

    def emit_recip(gi):
        if ("sd", gi) not in tk2 or ("rcp", gi) in emitted:
            return
        emitted.add(("rcp", gi))
        wb = gi % 2

        @blk.vector
        def _(v, gi=gi, wb=wb):
            kb.wait(v, tk2[("sd", gi)])
            kb.chain(v, "V")
            v.reciprocal(SDR[:, wb], SDR[:, wb])
            tk2[("rcp", gi)] = kb.inc(KB.nop(v), "V")

    def emit_bcast(gi):
        if ("rcp", gi) not in tk2 or ("bc", gi) in emitted:
            return
        emitted.add(("bc", gi))
        wb = gi % 2

        @blk.tensor
        def _(t, gi=gi, wb=wb):
            kb.wait(t, tk2[("rcp", gi)])
            if gi < 1:
                kb.wait(t, *cp1_done)
            kb.use_bank(t, 7)
            mm = t.matmul(kb.banks[7].ap()[:, :], E2TC, SDR[:, wb],
                          start=True, stop=True)
            tk2[("rbc", gi)] = kb.inc(mm, "T")
            wfree[("sdr", wb)] = tk2[("rbc", gi)]

    def emit_evacA(gi):
        if ("sqp", gi) not in tk2 or ("evA", gi) in emitted:
            return
        emitted.add(("evA", gi))
        nt, pj, oc, wb, wt, sp = _qst(gi)

        @blk.vector
        def _(v, gi=gi, wb=wb, pj=pj):
            kb.wait(v, tk2[("sqp", gi)])
            if gi < 2:
                kb.wait(v, *cp1_done)
            if ("qke", wb) in wfree:
                kb.wait(v, wfree[("qke", wb)])
            kb.chain(v, "V")
            if pj == 0:
                ev = v.tensor_scalar(QKE[:, wb], kb.banks[wb].ap()[:, :],
                                     QW2, 0.125, AL.mult, AL.mult)
            else:
                ev = v.tensor_scalar(QKE[:, wb], kb.banks[wb].ap()[:, :],
                                     KW2, None, AL.mult)
            kb.bank_free[wb] = kb.inc(ev, "V")

    def emit_evacB(gi):
        if ("rbc", gi) not in tk2 or ("evB", gi) in emitted:
            return
        emitted.add(("evB", gi))
        wb = gi % 2

        @blk.vector
        def _(v, gi=gi, wb=wb):
            kb.wait(v, tk2[("rbc", gi)])
            kb.chain(v, "V")
            ev = v.tensor_tensor(QKE[:, wb], QKE[:, wb],
                                 kb.banks[7].ap()[:, :], AL.mult)
            kb.bank_free[7] = kb.inc(ev, "V")
            tk2[("evB", gi)] = kb.bank_free[7]

    def emit_spill(gi):
        if ("evB", gi) not in tk2 or ("sp", gi) in emitted:
            return
        emitted.add(("sp", gi))
        nt, pj, oc, wb, wt, sp = _qst(gi)

        @blk.gpsimd
        def _(g, gi=gi, nt=nt, oc=oc, wb=wb, sp=sp):
            kb.wait(g, tk2[("evB", gi)])
            wfree[("qke", wb)] = kb.dma(
                g, "QS%d" % wb, dr[sp][oc][:, 512 * nt : 512 * nt + 512],
                QKE[:, wb])

    for nt in range(4):
        nb = nt % 2

        @blk.gpsimd
        def _(g, nt=nt, nb=nb):
            kb.wait(g, *kb._r2_done)
            if ("r2n", nb) in wfree:
                kb.wait(g, wfree[("r2n", nb)])
            tks["r2n"] = kb.dma(g, "RN%d" % nb, R2N[:, nb],
                                dr["r2_sp"][:, :, 512 * nt : 512 * nt + 512]
                                .rearrange("k p t -> p k t"))
        r2n_tok = tks["r2n"]

        for local in range(16):
            gi = nt * 16 + local
            emit_wload(gi)
            emit_stat(gi - 2)
            emit_bcast(gi - 3)
            emit_proj(gi, r2n_tok)
            emit_sqrt(gi - 2)
            emit_square(gi)
            emit_recip(gi - 2)
            emit_evacB(gi - 3)
            emit_spill(gi - 3)
            emit_evacA(gi - 1)
        for d in range(3):
            gi = nt * 16 + 16 + d
            emit_stat(gi - 2)
            emit_bcast(gi - 3)
            emit_sqrt(gi - 2)
            emit_recip(gi - 2)
            emit_evacB(gi - 3)
            emit_spill(gi - 3)
            emit_evacA(gi - 1)

        for sub in range(4):
            tt = 4 * nt + sub
            for oc2 in range(2):
                bo = 2 + oc2

                @blk.tensor
                def _(t, nb=nb, sub=sub, oc2=oc2, bo=bo, r2n_tok=r2n_tok, wv_done=wv_done):
                    kb.wait(t, wv_done, r2n_tok)
                    kb.use_bank(t, bo)
                    for k in range(8):
                        mm = t.matmul(kb.banks[bo].ap()[:, :],
                                      R2N[:, nb, k, 128 * sub : 128 * sub + 128],
                                      WVM[:, k, oc2], start=(k == 0), stop=(k == 7))
                    tks["v"] = kb.inc(mm, "T")
                    if sub == 3 and oc2 == 1:
                        wfree[("r2n_v", nb)] = tks["v"]

                @blk.vector
                def _(v, tt=tt, oc2=oc2, bo=bo):
                    kb.wait(v, tks["v"])
                    kb.chain(v, "V")
                    dst = VT3[:, tt, 520 * oc2 : 520 * oc2 + 520].rearrange(
                        "c (h d) -> c h d", d=65)
                    ev = v.tensor_copy(dst[:, :, 0:64],
                                       kb.banks[bo].ap()[:, :].rearrange(
                                           "p (h d) -> p h d", d=64))
                    kb.bank_free[bo] = kb.inc(ev, "V")

        def _mk(nb=nb):
            a = wfree.get(("r2n_qk", nb))
            b = wfree.get(("r2n_v", nb))
            wfree[("r2n", nb)] = b if (a is None or (b and b[1] > a[1])) else a
        _mk()

    @blk.vector
    def _(v):
        kb.chain(v, "V")
        v.memset(VT3.rearrange("p s (h d) -> p s h d", d=65)[:, :, :, 64:65], 1.0)
        kb.inc(KB.nop(v), "V")
    p1_done = kb.tok("V")

    @blk.sync
    def _(s):
        kb.dma(s, "LC1", U16B, dr["u16"])
        kb.dma(s, "LC0", OMEF, dr["omega_row"])
        kb.dma(s, "LC1", WROW[0:1, 0:16], dr["slopes_row"])
        kb.dma(s, "LC0", QWS, dr["qrw_col"])
        tks["c2"] = kb.dma(s, "LC1", KWS, dr["krw_col"])
    c2 = (kb.tok("LC0"), kb.tok("LC1"))

    kb.region("U2S", 64)
    U2S = kb.view("U2S")[0:16, :]

    @blk.vector
    def _(v):
        kb.wait(v, *c2)
        kb.chain(v, "V")
        v.tensor_tensor(U2S, U16B, U16B, AL.mult)
        kb.inc(KB.nop(v), "V")
        kb.chain(v, "V")
        v.reduce_sum(UN1, U2S, axis=AX)
        tks["un1"] = kb.inc(KB.nop(v), "V")

    @blk.scalar
    def _(s):
        kb.wait(s, tks["un1"])
        kb.chain(s, "S")
        kb.inc(s.activation(UN1, UN1, AF.Sqrt), "S")
    s_un = kb.tok("S")

    @blk.vector
    def _(v):
        kb.wait(v, s_un)
        kb.chain(v, "V")
        v.tensor_scalar(UN1, UN1, 1e-6, None, AL.max)
        kb.inc(KB.nop(v), "V")
        kb.chain(v, "V")
        v.reciprocal(UN1, UN1)
        kb.inc(KB.nop(v), "V")
        kb.chain(v, "V")
        v.tensor_scalar(U16B, U16B, UN1, None, AL.mult)
        tks["un"] = kb.inc(KB.nop(v), "V")

    @blk.gpsimd
    def _(g):
        kb.wait(g, tks["un"])
        for hh in range(16):
            tks["unt"] = kb.dma(g, "TRX", UNT[0:64, hh : hh + 1],
                                U16B[hh : hh + 1, :]
                                .rearrange("p (a c) -> p a c", a=64), slow=True)

    @blk.scalar
    def _(s):
        kb.wait(s, *c2)
        kb.chain(s, "S")
        kb.inc(s.activation(OMEF, OMEF, AF.Exp), "S")
        kb.chain(s, "S")
        kb.inc(s.activation(OMEF, OMEF, AF.Ln, bias=1.0), "S")
    s_om = kb.tok("S")

    @blk.vector
    def _(v):
        kb.wait(v, s_om)
        kb.chain(v, "V")
        v.tensor_tensor(OMEF, OMEF, WROW[0:1, 0:16], AL.mult)
        kb.inc(KB.nop(v), "V")
        tks["om"] = kb.tok("V")

    kb.region("CR2", 192)
    CR2 = kb.view("CR2")[0:1, :]

    @blk.gpsimd
    def _(g):
        kb.wait(g, *c2)
        kb.dma(g, "TRX", CR2[:, 0:64].rearrange("p (a c) -> p a c", a=64), QWS,
               slow=True)
        tks["wr2"] = kb.dma(g, "TRX",
                            CR2[:, 64:128].rearrange("p (a c) -> p a c", a=64),
                            KWS, slow=True)

    @blk.vector
    def _(v):
        kb.wait(v, tks["wr2"])
        kb.chain(v, "V")
        v.reduce_max(CR2[:, 128:129], CR2[:, 0:64], axis=AX,
                     apply_absolute_value=True)
        kb.inc(KB.nop(v), "V")
        kb.chain(v, "V")
        v.reduce_max(CR2[:, 129:130], CR2[:, 64:128], axis=AX,
                     apply_absolute_value=True)
        kb.inc(KB.nop(v), "V")
        kb.chain(v, "V")
        v.tensor_tensor(CR2[:, 130:131], CR2[:, 128:129], CR2[:, 129:130], AL.mult)
        kb.inc(KB.nop(v), "V")
        kb.chain(v, "V")
        v.tensor_scalar(CR2[:, 130:131], CR2[:, 130:131], -8.0, None, AL.mult)
        kb.inc(KB.nop(v), "V")
        tks["negm"] = kb.tok("V")

    @blk.tensor
    def _(t):
        kb.wait(t, tks["negm"])
        kb.use_bank(t, 3)
        tks["nmb"] = kb.inc(
            t.matmul(kb.banks[3].ap()[:, 0:1], C.ones_row, CR2[:, 130:131],
                     start=True, stop=True), "T")

    @blk.vector
    def _(v):
        kb.wait(v, tks["nmb"])
        kb.chain(v, "V")
        ev = v.tensor_copy(NMC, kb.banks[3].ap()[:, 0:1])
        kb.bank_free[3] = kb.inc(ev, "V")
    p2_done = kb.tok("V")

    kb.free("R2N", "WVM", "WQK", "QKE", "U2S")
    kb.free("CR2", "CP1", "SQP", "SDR")

    kb.region("CTX1", 8 * T)
    kb.region("QSC", 2 * T)
    kb.region("KSC", 2 * T)
    kb.region("PT2", 2 * 128)
    kb.region("BC2", 128)
    kb.region("GRW", 3 * T)
    QSC = kb.view("QSC").rearrange("p (b t) -> p b t", b=2)
    KSC = kb.view("KSC").rearrange("p (b t) -> p b t", b=2)
    CTX1 = kb.view("CTX1").rearrange("p (c t) -> p c t", c=8)
    PT2 = kb.view("PT2").rearrange("p (b t) -> p b t", b=2)
    BC2 = kb.view("BC2")[0:64, :]
    GRW = kb.view("GRW").rearrange("p (b t) -> p b t", b=3)[0:1]

    p3guard = tuple(kb.tok(n) for n in
                    ("QS0", "QS1", "WM0", "WM1", "RN0", "RN1", "LC0", "LC1",
                     "T", "V", "S", "TRX", "KE0", "KE1"))

    @blk.sync
    def _(s):
        kb.wait(s, *p3guard)
        for b in range(2):
            tks["q2"] = kb.dma(s, "LC0", QSC[64:66, b, :], dr["c_q2row"])
    q2_done = tks["q2"]

    kfree = {}
    for h in range(16):
        hb = h % 2
        hp = 64 * (h % 2)
        cb = h // 2

        @blk.gpsimd
        def _(g, h=h, hb=hb, hp=hp, cb=cb):
            kb.wait(g, p1_done, p2_done, q2_done)
            if h == 0:
                kb.wait(g, *p3guard)
            if ("qsc", hb) in kfree:
                kb.wait(g, kfree[("qsc", hb)])
            tks["scq"] = kb.dma(g, "KH%d" % hb, QSC[0:64, hb, :],
                                dr["q1_sp"][cb][hp : hp + 64, :])
            tks["sc"] = kb.dma(g, "QH%d" % hb, KSC[0:64, hb, :],
                               dr["k1_sp"][cb][hp : hp + 64, :])
        sc_tok = tks["sc"]
        scq_tok = tks["scq"]

        tks["rmsd"] = sc_tok

        for tl in range(4):
            tcol = slice(512 * tl, 512 * tl + 512)
            gbk = 4 + tl % 2

            @blk.tensor
            def _(t, h=h, hb=hb, tcol=tcol, gbk=gbk):
                kb.wait(t, tks["rmsd"], scq_tok, tks["unt"])
                kb.use_bank(t, gbk)
                mm = t.matmul(kb.banks[gbk].ap()[0:1, :], UNT[0:64, h : h + 1],
                              KSC[0:64, hb, tcol], start=True, stop=True)
                tks["gl"] = kb.inc(mm, "T")

            @blk.scalar
            def _(s, tl=tl, tcol=tcol, gbk=gbk):
                kb.wait(s, tks["gl"])
                kb.chain(s, "S")
                ev = s.activation(GRW[:, 0, tcol], kb.banks[gbk].ap()[0:1, :],
                                  AF.Exp, scale=0.125)
                kb.bank_free[gbk] = kb.inc(ev, "S")

        @blk.scalar
        def _(s):
            kb.chain(s, "S")
            kb.inc(s.activation(GRW[:, 0, :], GRW[:, 0, :], AF.Ln, bias=1.0), "S")
        s_gate = kb.tok("S")

        @blk.vector
        def _(v, h=h):
            kb.wait(v, s_gate, tks["om"])
            kb.chain(v, "V")
            v.scalar_tensor_tensor(GRW[:, 1, :], GRW[:, 0, :],
                                   OMEF[0:1, h : h + 1], C.qpos_row,
                                   AL.mult, AL.mult)
            kb.inc(KB.nop(v), "V")
            kb.chain(v, "V")
            v.tensor_scalar(GRW[:, 2, :], GRW[:, 0, :], OMEF[0:1, h : h + 1],
                            -1.0, AL.mult, AL.mult)
            tks["bg"] = kb.inc(KB.nop(v), "V")

        @blk.gpsimd
        def _(g, hb=hb):
            kb.wait(g, tks["bg"])
            kb.dma(g, "KE%d" % hb, KSC[64:65, hb, :], GRW[:, 1, :])
            tks["ke"] = kb.dma(g, "KE%d" % hb, KSC[65:66, hb, :], GRW[:, 2, :])

        for qt in range(16):
            qcol = slice(128 * qt, 128 * qt + 128)
            cbk = 2 + qt % 2
            st_toks = {}

            def emit_S(kt, qt=qt, h=h, hb=hb, qcol=qcol):
                sb = kt % 2

                @blk.tensor
                def _(t):
                    kb.wait(t, tks["ke"])
                    kb.use_bank(t, sb)
                    mm = t.matmul(kb.banks[sb].ap()[:, 0:128],
                                  KSC[0:66, hb, 128 * kt : 128 * kt + 128],
                                  QSC[0:66, hb, qcol], start=True, stop=True)
                    st_toks[kt] = kb.inc(mm, "T")
                    if kt >= qt - 1:
                        kfree[("sc_part", hb)] = st_toks[kt]

            emit_S(0)
            for kt in range(qt + 1):
                sb = kt % 2
                if kt + 1 <= qt:
                    emit_S(kt + 1)
                if kt == qt:
                    @blk.vector
                    def _(v, kt=kt, sb=sb):
                        kb.wait(v, st_toks[kt])
                        kb.chain(v, "V")
                        ev = v.tensor_tensor(kb.banks[sb].ap()[:, 0:128],
                                             kb.banks[sb].ap()[:, 0:128],
                                             C.negbig, AL.add)
                        st_toks[kt] = kb.inc(ev, "V")

                @blk.scalar
                def _(s, kt=kt, sb=sb, h=h):
                    kb.wait(s, st_toks[kt])
                    if ("pt2", sb) in kfree:
                        kb.wait(s, kfree[("pt2", sb)])
                    kb.chain(s, "S")
                    ev = s.activation(PT2[:, sb], kb.banks[sb].ap()[:, 0:128],
                                      AF.Exp, bias=NMC)
                    kb.bank_free[sb] = kb.inc(ev, "S")
                    tks["pt"] = kb.tok("S")

                @blk.tensor
                def _(t, h=h, kt=kt, qt=qt, sb=sb, cbk=cbk):
                    kb.wait(t, tks["pt"])
                    if kt == 0:
                        kb.use_bank(t, cbk)
                    mm = t.matmul(kb.banks[cbk].ap()[0:65, 0:128],
                                  VT3[:, kt, 65 * h : 65 * h + 65], PT2[:, sb],
                                  start=(kt == 0), stop=(kt == qt))
                    tks["ctx"] = kb.inc(mm, "T")
                    kfree[("pt2", sb)] = tks["ctx"]

            @blk.vector
            def _(v, cbk=cbk, qt=qt):
                kb.wait(v, tks["ctx"])
                kb.chain(v, "V")
                v.reciprocal(GRW[:, 2, 0:128], kb.banks[cbk].ap()[64:65, 0:128])
                tks["dr"] = kb.inc(KB.nop(v), "V")

            @blk.tensor
            def _(t, qt=qt):
                kb.wait(t, tks["dr"])
                kb.use_bank(t, 4 + qt % 2)
                tks["bc"] = kb.inc(
                    t.matmul(kb.banks[4 + qt % 2].ap()[0:64, 0:128],
                             C.ones_row[0:1, 0:64],
                             GRW[:, 2, 0:128], start=True, stop=True), "T")

            @blk.vector
            def _(v, h=h, hp=hp, cb=cb, qcol=qcol, cbk=cbk, qt=qt):
                kb.wait(v, tks["bc"])
                kb.chain(v, "V")
                ev = v.tensor_copy(BC2[:, 0:128],
                                   kb.banks[4 + qt % 2].ap()[0:64, 0:128])
                kb.bank_free[4 + qt % 2] = kb.inc(ev, "V")
                kb.chain(v, "V")
                ev = v.tensor_tensor(CTX1[hp : hp + 64, cb, qcol],
                                     kb.banks[cbk].ap()[0:64, 0:128], BC2[:, 0:128],
                                     AL.mult)
                kb.bank_free[cbk] = kb.inc(ev, "V")
        kfree[("qsc", hb)] = kfree[("sc_part", hb)]
    p3_done = kb.tok("V")
    kb.free("VT", "QSC", "KSC", "PT2", "BC2")

    kb.region("R3", 8 * T)
    kb.region("WPJ", 2 * 1024)
    kb.region("R2B", 2 * 512)
    kb.region("RES3", 8 * 512)
    kb.region("SQ7", 512)
    R3 = kb.view("R3").rearrange("p (c t) -> p c t", c=8)
    WPJ = kb.view("WPJ").rearrange("p (b k c) -> p b k c", b=2, k=8)
    R2B = kb.view("R2B").rearrange("p (b t) -> p b t", b=2)
    RES3 = kb.view("RES3").rearrange("p (c t) -> p c t", c=8)
    SQ7 = kb.view("SQ7")
    GRW2 = kb.view("GRW").rearrange("p (b t) -> p b t", b=3)[0:1]
    p3t = kb.tok("T")
    p3s = kb.tok("S")

    for nt in range(4):
        ncol = slice(512 * nt, 512 * nt + 512)
        last_st = [None]
        for m in range(8):
            wb = m % 2
            bo = 4 + wb

            @blk.sync
            def _(s, m=m, wb=wb, nt=nt):
                if ("wpj", wb) in wfree:
                    kb.wait(s, wfree[("wpj", wb)])
                kb.wait(s, p3t)
                tks["wp"] = kb.dma(s, "WM%d" % wb, WPJ[:, wb],
                                   dr["wproj_t"][m].rearrange("k p c -> p k c"))

            @blk.gpsimd
            def _(g, m=m, nt=nt):
                if ("r2b", m % 2) in wfree:
                    kb.wait(g, wfree[("r2b", m % 2)])
                kb.wait(g, p3t, p3s)
                tks["r2b"] = kb.dma(g, "RB%d" % (m % 2), R2B[:, m % 2],
                                    dr["r2_sp"][m][:, 512 * nt : 512 * nt + 512])

            @blk.tensor
            def _(t, m=m, nt=nt, wb=wb, bo=bo, p3_done=p3_done):
                kb.wait(t, tks["wp"], p3_done)
                kb.use_bank(t, bo)
                for k in range(8):
                    mm = t.matmul(kb.banks[bo].ap()[:, :], WPJ[:, wb, k, :],
                                  CTX1[:, k, ncol], start=(k == 0), stop=(k == 7))
                tks["opj"] = kb.inc(mm, "T")
                wfree[("wpj", wb)] = tks["opj"]

            @blk.vector
            def _(v, m=m, bo=bo):
                kb.wait(v, tks["opj"], tks["r2b"])
                if ("res3", m) in wfree:
                    kb.wait(v, wfree[("res3", m)])
                kb.chain(v, "V")
                ev = v.tensor_tensor(RES3[:, m, :], kb.banks[bo].ap()[:, :],
                                     R2B[:, m % 2], AL.add)
                kb.bank_free[bo] = kb.inc(ev, "V")
                wfree[("r2b", m % 2)] = kb.bank_free[bo]
                kb.chain(v, "V")
                kb.wait(v, last_st[0])
                v.tensor_tensor(SQ7, RES3[:, m, :], RES3[:, m, :], AL.mult)
                tks["sq"] = kb.inc(KB.nop(v), "V")

            @blk.tensor
            def _(t, m=m):
                if m == 0:
                    kb.use_bank(t, 6, 7)
                kb.wait(t, kb.tok("V"))
                t.matmul(kb.banks[6].ap()[0:1, :], C.ones_col, RES3[:, m, :],
                         start=(m == 0), stop=(m == 7))
                mm = t.matmul(kb.banks[7].ap()[0:1, :], C.ones_col, SQ7,
                              start=(m == 0), stop=(m == 7))
                tks["lnst"] = last_st[0] = kb.inc(mm, "T")

        @blk.vector
        def _(v):
            kb.wait(v, tks["lnst"])
            kb.chain(v, "V")
            v.tensor_scalar(GRW2[:, 0, 0:512], kb.banks[6].ap()[0:1, :],
                            1.0 / 1024.0, None, AL.mult)
            kb.bank_free[6] = kb.inc(KB.nop(v), "V")
            kb.chain(v, "V")
            v.tensor_scalar(GRW2[:, 1, 0:512], kb.banks[7].ap()[0:1, :],
                            1.0 / 1024.0, None, AL.mult)
            kb.bank_free[7] = kb.inc(KB.nop(v), "V")
            kb.chain(v, "V")
            v.tensor_tensor(GRW2[:, 2, 0:512], GRW2[:, 0, 0:512],
                            GRW2[:, 0, 0:512], AL.mult)
            kb.inc(KB.nop(v), "V")
            kb.chain(v, "V")
            v.tensor_tensor(GRW2[:, 1, 0:512], GRW2[:, 1, 0:512],
                            GRW2[:, 2, 0:512], AL.subtract)
            tks["var"] = kb.inc(KB.nop(v), "V")

        @blk.scalar
        def _(s):
            kb.wait(s, tks["var"])
            kb.chain(s, "S")
            kb.inc(s.activation(GRW2[:, 2, 0:512], GRW2[:, 1, 0:512], AF.Sqrt,
                                bias=C.epscol[0:1, 0:1]), "S")
        s_sdc4 = kb.tok("S")

        @blk.vector
        def _(v):
            kb.wait(v, s_sdc4)
            kb.chain(v, "V")
            v.reciprocal(GRW2[:, 1, 0:512], GRW2[:, 2, 0:512])
            tks["rstd"] = kb.inc(KB.nop(v), "V")

        @blk.tensor
        def _(t):
            kb.wait(t, tks["rstd"])
            kb.use_bank(t, 6, 7)
            tks["bmu"] = kb.inc(
                t.matmul(kb.banks[6].ap()[:, :], C.ones_row, GRW2[:, 0, 0:512],
                         start=True, stop=True), "T")
            tks["brs"] = kb.inc(
                t.matmul(kb.banks[7].ap()[:, :], C.ones_row, GRW2[:, 1, 0:512],
                         start=True, stop=True), "T")

        for m in range(8):
            @blk.vector
            def _(v, nt=nt, m=m):
                kb.wait(v, tks["bmu"], tks["brs"])
                kb.chain(v, "V")
                v.tensor_tensor(R3[:, m, ncol], RES3[:, m, :],
                                kb.banks[6].ap()[:, :], AL.subtract)
                kb.inc(KB.nop(v), "V")
                kb.chain(v, "V")
                v.tensor_tensor(R3[:, m, ncol], R3[:, m, ncol],
                                kb.banks[7].ap()[:, :], AL.mult)
                ev = kb.inc(KB.nop(v), "V")
                wfree[("res3", m)] = ev
        kb.bank_free[6] = kb.tok("V")
        kb.bank_free[7] = kb.tok("V")

    globals()["_DBG_C_REGIONS"] = dict(kb.regions)
    kb.free("CTX1", "GRW", "CSM", "WPJ", "R2B", "RES3", "SQ7")



BF16 = mybir.dt.bfloat16


def _phase_D(nc, blk, kb, dr, C, out_fm):
    R3 = kb.view("R3").rearrange("p (c t) -> p c t", c=8)
    kb.free("QPOS")
    kb.region("RWS", 64)
    kb.region("LPT", 128)
    kb.region("SQ8", 512)
    kb.region("PRB", 16)
    big = kb.big.ap()
    RWS = kb.view("RWS").rearrange("p (k c) -> p k c", k=8)
    LPT = kb.view("LPT").rearrange("p (s e) -> p s e", e=8)
    SQ8 = kb.view("SQ8")
    PRB = kb.view("PRB")

    tks = {}
    wfree = {}

    @blk.sync
    def _(s):
        tks["rw"] = kb.dma(s, "LC0", RWS, dr["rw_t"][0].rearrange("k p c -> p k c"))
    rw_tok = tks["rw"]

    for tt in range(16):
        tcol = slice(128 * tt, 128 * tt + 128)

        @blk.tensor
        def _(t, tt=tt, tcol=tcol, rw_tok=rw_tok):
            kb.wait(t, rw_tok)
            bo = tt % 2
            kb.use_bank(t, bo)
            for k in range(8):
                mm = t.matmul(kb.banks[bo].ap()[:, 0:8], R3[:, k, tcol], RWS[:, k],
                              start=(k == 0), stop=(k == 7))
            tks["lg"] = kb.inc(mm, "T")

        @blk.vector
        def _(v, tt=tt):
            bo = tt % 2
            kb.wait(v, tks["lg"])
            kb.chain(v, "V")
            v.reduce_max(PRB[:, 0:1], kb.banks[bo].ap()[:, 0:8], axis=AX)
            kb.inc(KB.nop(v), "V")
            kb.chain(v, "V")
            v.tensor_scalar(PRB[:, 1:2], PRB[:, 0:1], -1.0, None, AL.mult)
            tks["ngm"] = kb.inc(KB.nop(v), "V")

        @blk.scalar
        def _(s, tt=tt):
            bo = tt % 2
            kb.wait(s, tks["ngm"])
            kb.chain(s, "S")
            ev = s.activation(LPT[:, tt, :], kb.banks[bo].ap()[:, 0:8], AF.Exp,
                              bias=PRB[:, 1:2])
            kb.bank_free[bo] = kb.inc(ev, "S")
            tks["pe"] = kb.tok("S")

        @blk.vector
        def _(v, tt=tt):
            kb.wait(v, tks["pe"])
            steps = [
                lambda: v.reduce_sum(PRB[:, 2:3], LPT[:, tt, :], axis=AX),
                lambda: v.reciprocal(PRB[:, 3:4], PRB[:, 2:3]),
                lambda: v.tensor_scalar(LPT[:, tt, :], LPT[:, tt, :],
                                        PRB[:, 3:4], None, AL.mult),
                lambda: v.reduce_max(PRB[:, 4:5], LPT[:, tt, :], axis=AX),
                lambda: v.tensor_scalar(SQ8[:, 0:8], LPT[:, tt, :], PRB[:, 4:5],
                                        None, AL.is_equal),
                lambda: v.scalar_tensor_tensor(SQ8[:, 8:16], SQ8[:, 0:8],
                                               -1.0e30, LPT[:, tt, :], AL.mult,
                                               AL.add),
                lambda: v.reduce_max(PRB[:, 6:7], SQ8[:, 8:16], axis=AX),
                lambda: v.tensor_scalar(SQ8[:, 0:8], LPT[:, tt, :], PRB[:, 6:7],
                                        None, AL.is_ge),
                lambda: v.tensor_tensor(LPT[:, tt, :], LPT[:, tt, :],
                                        SQ8[:, 0:8], AL.mult),
            ]
            for f in steps:
                kb.chain(v, "V")
                f()
                kb.inc(KB.nop(v), "V")
    lp_done = kb.tok("V")

    @blk.gpsimd
    def _(g):
        kb.wait(g, lp_done)
        for c in range(8):
            tks["r3s"] = kb.dma(g, "RS%d" % (c % 2), dr["r3_sp"][c], R3[:, c, :])
    r3_sp_done = (kb.tok("RS0"), kb.tok("RS1"))
    kb.free("R3")
    kb.region("OA", 8 * 1024)
    kb.region("GEH", 11 * 512)
    kb.region("GEL", 11 * 512)
    kb.region("GST", 512)
    kb.region("R3H", 8 * 512)
    kb.region("R3L", 8 * 512)
    kb.region("R3T", 2 * 1024)
    kb.region("W1S", 2 * 1024)
    kb.region("W1H", 2 * 1024)
    kb.region("W1L", 2 * 1024)
    kb.region("W2S", 11 * 128)
    kb.region("W2H", 2 * 11 * 64)
    kb.region("W2L", 2 * 11 * 64)
    kb.region("LPB", 1024)
    kb.region("LPR1", 1024)
    kb.region("TY1", 512)
    kb.region("DRW", 3 * 512)
    kb.region("RRS", 2 * 512)
    OA = kb.view("OA").rearrange("p (c t) -> p c t", c=8)
    GEH = kb.view("GEH").bitcast(BF16).rearrange("p (f t) -> p f t", f=11)
    GEL = kb.view("GEL").bitcast(BF16).rearrange("p (f t) -> p f t", f=11)
    GST = kb.view("GST")
    R3H = kb.view("R3H").bitcast(BF16).rearrange("p (c t) -> p c t", c=8)
    R3L = kb.view("R3L").bitcast(BF16).rearrange("p (c t) -> p c t", c=8)
    R3T = kb.view("R3T").rearrange("p (b t) -> p b t", b=2)
    W1S = kb.view("W1S").rearrange("p (f k c) -> p f k c", f=2, k=8)
    W1H = kb.view("W1H").bitcast(BF16).rearrange("p (b f k c) -> p b f k c",
                                                 b=2, f=2, k=8)
    W1L = kb.view("W1L").bitcast(BF16).rearrange("p (b f k c) -> p b f k c",
                                                 b=2, f=2, k=8)
    W2S = kb.view("W2S").rearrange("p (k c) -> p k c", k=11)
    W2H = kb.view("W2H").bitcast(BF16).rearrange("p (b k c) -> p b k c", b=2, k=11)
    W2L = kb.view("W2L").bitcast(BF16).rearrange("p (b k c) -> p b k c", b=2, k=11)
    LPB = kb.view("LPB").rearrange("p (b t) -> p b t", b=2)
    LPR1 = kb.view("LPR1")[0:1, :]
    TY1 = kb.view("TY1")
    DRW = kb.view("DRW").rearrange("p (b t) -> p b t", b=3)[0:1]
    RRS = kb.view("RRS").rearrange("p (b t) -> p b t", b=2)

    ofree = {}
    for th in range(2):
        hcol = slice(1024 * th, 1024 * th + 1024)

        for c in range(8):
            cb = c % 2

            @blk.gpsimd
            def _(g, th=th, c=c, cb=cb):
                kb.wait(g, *r3_sp_done)
                if ("r3t", cb) in wfree:
                    kb.wait(g, wfree[("r3t", cb)])
                tks["r3t"] = kb.dma(g, "RT%d" % cb, R3T[:, cb, :],
                                    dr["r3_sp"][c][:, hcol])

            @blk.vector
            def _(v, th=th, c=c, cb=cb):
                kb.wait(v, tks["r3t"])
                if ("r3hl", c) in wfree:
                    kb.wait(v, wfree[("r3hl", c)])
                kb.chain(v, "V")
                v.tensor_copy(R3H[:, c, :], R3T[:, cb, :])
                kb.inc(KB.nop(v), "V")
                kb.chain(v, "V")
                ev = v.tensor_tensor(R3L[:, c, :], R3T[:, cb, :], R3H[:, c, :],
                                     AL.subtract)
                wfree[("r3t", cb)] = kb.inc(ev, "V")
                tks["r3hl"] = wfree[("r3t", cb)]
        r3hl_done = kb.tok("V")

        for e in range(8):
            @blk.gpsimd
            def _(g, th=th, e=e):
                kb.wait(g, lp_done)
                kb.wait(g, *r3_sp_done)
                if "lpr" in wfree:
                    kb.wait(g, wfree["lpr"])
                for ti in range(8):
                    tt = 8 * th + ti
                    tks["lpr"] = kb.dma(g, "LPX",
                                        LPR1[:, 128 * ti : 128 * ti + 128],
                                        LPT[:, tt, e : e + 1], slow=True)

            for half in range(2):
                @blk.tensor
                def _(t, half=half):
                    kb.wait(t, tks["lpr"])
                    kb.use_bank(t, 6)
                    tks["lpb"] = kb.inc(
                        t.matmul(kb.banks[6].ap()[:, :], C.ones_row,
                                 LPR1[:, 512 * half : 512 * half + 512],
                                 start=True, stop=True), "T")

                @blk.vector
                def _(v, half=half):
                    kb.wait(v, tks["lpb"])
                    if ("lpbf", half) in wfree:
                        kb.wait(v, wfree[("lpbf", half)])
                    kb.chain(v, "V")
                    ev = v.tensor_copy(LPB[:, half], kb.banks[6].ap()[:, :])
                    kb.bank_free[6] = kb.inc(ev, "V")
                    tks["lpbe"] = kb.bank_free[6]
                    wfree["lpr"] = tks["lpbe"]
            lpb_tok = tks["lpbe"]

            w1tok = {}

            def w1_fetch(fi, e=e):
                wb = fi % 2

                @blk.sync
                def _(s, e=e, fi=fi, wb=wb):
                    if ("w1s",) in wfree:
                        kb.wait(s, wfree[("w1s",)])
                    else:
                        kb.wait(s, *r3_sp_done)
                        kb.wait(s, kb.tok("T"))
                    tks["w1a"] = kb.dma(s, "WM0", W1S[:, 0],
                                        dr["w1_t"][e][fi].rearrange("k p c -> p k c"))
                    tks["w1"] = kb.dma(s, "WM1", W1S[:, 1],
                                       dr["w1_t"][e][fi + 11]
                                       .rearrange("k p c -> p k c"))

                @blk.vector
                def _(v, fi=fi, wb=wb):
                    kb.wait(v, tks["w1a"], tks["w1"])
                    if ("w1hl", wb) in wfree:
                        kb.wait(v, wfree[("w1hl", wb)])
                    for f2 in range(2):
                        kb.chain(v, "V")
                        v.tensor_copy(W1H[:, wb, f2], W1S[:, f2])
                        kb.inc(KB.nop(v), "V")
                        kb.chain(v, "V")
                        ev = v.tensor_tensor(W1L[:, wb, f2], W1S[:, f2],
                                             W1H[:, wb, f2], AL.subtract)
                        kb.inc(ev, "V")
                    wfree[("w1s",)] = kb.tok("V")
                    w1tok[fi] = wfree[("w1s",)]

            w1_fetch(0)
            for fi in range(11):
                wb = fi % 2
                if fi + 1 < 11:
                    w1_fetch(fi + 1)
                tks["w1hl"] = w1tok[fi]

                for nt2 in range(2):
                    col = slice(512 * nt2, 512 * nt2 + 512)
                    b1 = (2 * fi + nt2) % 2
                    b2 = 2 + b1

                    @blk.tensor
                    def _(t, th=th, e=e, fi=fi, nt2=nt2, wb=wb, b1=b1, b2=b2,
                          col=col, lpb_tok=lpb_tok, r3hl_done=r3hl_done):
                        kb.wait(t, tks["w1hl"], lpb_tok, r3hl_done)
                        if ("ge", fi) in ofree:
                            kb.wait(t, ofree[("ge", fi)])
                        kb.use_bank(t, b1)
                        idx = 0
                        for wop, xop in ((0, 0), (0, 1), (1, 0)):
                            for k in range(8):
                                xs = (R3H if xop == 0 else R3L)[:, k, col]
                                mm = t.matmul(kb.banks[b1].ap()[:, :],
                                              W1H[:, wb, 0, k, :] if wop == 0
                                              else W1L[:, wb, 0, k, :],
                                              xs, start=(idx == 0),
                                              stop=(idx == 23))
                                idx += 1
                        tks["y1"] = kb.inc(mm, "T")
                        kb.use_bank(t, b2)
                        idx = 0
                        for wop, xop in ((0, 0), (0, 1), (1, 0)):
                            for k in range(8):
                                xs = (R3H if xop == 0 else R3L)[:, k, col]
                                mm = t.matmul(kb.banks[b2].ap()[:, :],
                                              W1H[:, wb, 1, k, :] if wop == 0
                                              else W1L[:, wb, 1, k, :],
                                              xs, start=(idx == 0),
                                              stop=(idx == 23))
                                idx += 1
                        tks["y2"] = kb.inc(mm, "T")
                        if nt2 == 1:
                            tks["w1done%d" % wb] = tks["y2"]
                            wfree[("w1hl", wb)] = tks["y2"]

                    @blk.scalar
                    def _(s, b1=b1):
                        kb.wait(s, tks["y1"])
                        if "ty1" in wfree:
                            kb.wait(s, wfree["ty1"])
                        kb.wait(s, wfree.get("r4prev"))
                        kb.chain(s, "S")
                        ev = s.activation(TY1, kb.banks[b1].ap()[:, :], AF.Silu)
                        kb.bank_free[b1] = kb.inc(ev, "S")
                        tks["ty1"] = kb.tok("S")

                    @blk.vector
                    def _(v, fi=fi, nt2=nt2, b2=b2):
                        kb.wait(v, tks["y2"], tks["ty1"])
                        if "gst" in wfree:
                            kb.wait(v, wfree["gst"])
                        kb.chain(v, "V")
                        ev = v.tensor_tensor(GST, kb.banks[b2].ap()[:, :],
                                             LPB[:, nt2], AL.mult)
                        kb.bank_free[b2] = kb.inc(ev, "V")
                        kb.chain(v, "V")
                        v.tensor_tensor(GST, GST, TY1, AL.mult)
                        wfree["ty1"] = kb.inc(KB.nop(v), "V")
                        gcol = slice(512 * nt2, 512 * nt2 + 512)
                        kb.chain(v, "V")
                        v.tensor_copy(GEH[:, fi, gcol], GST)
                        kb.inc(KB.nop(v), "V")
                        kb.chain(v, "V")
                        ev = v.tensor_tensor(GEL[:, fi, gcol], GST,
                                             GEH[:, fi, gcol], AL.subtract)
                        wfree["gst"] = kb.inc(ev, "V")
                        tks["g"] = wfree["gst"]
            g_done = kb.tok("V")
            wfree[("lpbf", 0)] = g_done
            wfree[("lpbf", 1)] = g_done

            w2tok = {}

            def w2_fetch(m, e=e):
                wb = m % 2

                @blk.sync
                def _(s, e=e, m=m, wb=wb):
                    if ("w2s",) in wfree:
                        kb.wait(s, wfree[("w2s",)])
                    else:
                        kb.wait(s, *r3_sp_done)
                    tks["w2"] = kb.dma(s, "UC0", W2S,
                                       dr["w2_t"][e][m].rearrange("k p c -> p k c"))

                @blk.vector
                def _(v, m=m, wb=wb):
                    kb.wait(v, tks["w2"])
                    if ("w2hl", wb) in wfree:
                        kb.wait(v, wfree[("w2hl", wb)])
                    kb.chain(v, "V")
                    v.tensor_copy(W2H[:, wb], W2S)
                    kb.inc(KB.nop(v), "V")
                    kb.chain(v, "V")
                    ev = v.tensor_tensor(W2L[:, wb], W2S, W2H[:, wb], AL.subtract)
                    wfree[("w2s",)] = kb.inc(ev, "V")
                    w2tok[m] = wfree[("w2s",)]

            w2_fetch(0)
            for m in range(8):
                wb = m % 2
                bo = 4 + wb
                if m + 1 < 8:
                    w2_fetch(m + 1)
                tks["w2hl"] = w2tok[m]

                for nt2 in range(2):
                    gcol = slice(512 * nt2, 512 * nt2 + 512)

                    @blk.tensor
                    def _(t, m=m, nt2=nt2, wb=wb, bo=bo, gcol=gcol, g_done=g_done):
                        kb.wait(t, tks["w2hl"], g_done)
                        kb.use_bank(t, bo)
                        idx = 0
                        for wop, xop in ((0, 0), (0, 1), (1, 0)):
                            for kf in range(11):
                                gs = (GEH if xop == 0 else GEL)[:, kf, gcol]
                                mm = t.matmul(kb.banks[bo].ap()[:, :],
                                              W2H[:, wb, kf, :] if wop == 0
                                              else W2L[:, wb, kf, :],
                                              gs, start=(idx == 0),
                                              stop=(idx == 32))
                                idx += 1
                        tks["o"] = kb.inc(mm, "T")
                        if nt2 == 1:
                            wfree[("w2hl", wb)] = tks["o"]

                    @blk.vector
                    def _(v, e=e, m=m, nt2=nt2, bo=bo):
                        kb.wait(v, tks["o"])
                        kb.chain(v, "V")
                        ocol = slice(512 * nt2, 512 * nt2 + 512)
                        if e == 0:
                            ev = v.tensor_copy(OA[:, m, ocol],
                                               kb.banks[bo].ap()[:, :])
                        else:
                            ev = v.tensor_tensor(OA[:, m, ocol], OA[:, m, ocol],
                                                 kb.banks[bo].ap()[:, :], AL.add)
                        kb.bank_free[bo] = kb.inc(ev, "V")
                        tks["oacc"] = kb.bank_free[bo]
            if e == 7:
                o_done = kb.tok("V")
            for fi in range(11):
                ofree[("ge", fi)] = tks["o"]

        for nt2 in range(2):
            col = slice(1024 * th + 512 * nt2, 1024 * th + 512 * nt2 + 512)
            ocol = slice(512 * nt2, 512 * nt2 + 512)
            last_st = [None]
            for m in range(8):
                @blk.gpsimd
                def _(g, th=th, m=m, col=col):
                    kb.wait(g, *r3_sp_done)
                    if ("rrs", m % 2) in wfree:
                        kb.wait(g, wfree[("rrs", m % 2)])
                    tks["rrs"] = kb.dma(g, "RT%d" % (m % 2), RRS[:, m % 2],
                                        dr["r3_sp"][m][:, col])

                @blk.vector
                def _(v, th=th, m=m, nt2=nt2, ocol=ocol, o_done=o_done):
                    kb.wait(v, o_done, tks["rrs"])
                    kb.chain(v, "V")
                    v.tensor_tensor(RRS[:, m % 2], OA[:, m, ocol], RRS[:, m % 2],
                                    AL.add)
                    tks["res"] = kb.inc(KB.nop(v), "V")
                    kb.chain(v, "V")
                    kb.wait(v, last_st[0])
                    v.tensor_tensor(SQ8, RRS[:, m % 2], RRS[:, m % 2], AL.mult)
                    tks["sq"] = kb.inc(KB.nop(v), "V")
                    kb.chain(v, "V")
                    v.tensor_copy(OA[:, m, ocol], RRS[:, m % 2])
                    wfree[("rrs2", m % 2)] = kb.inc(KB.nop(v), "V")

                @blk.tensor
                def _(t, m=m):
                    if m == 0:
                        kb.use_bank(t, 6, 7)
                    kb.wait(t, kb.tok("V"))
                    t.matmul(kb.banks[6].ap()[0:1, :], C.ones_col, RRS[:, m % 2],
                             start=(m == 0), stop=(m == 7))
                    mm = t.matmul(kb.banks[7].ap()[0:1, :], C.ones_col, SQ8,
                                  start=(m == 0), stop=(m == 7))
                    tks["lnst"] = last_st[0] = kb.inc(mm, "T")
                    wfree[("rrs", m % 2)] = tks["lnst"]

            @blk.vector
            def _(v):
                kb.wait(v, tks["lnst"])
                kb.chain(v, "V")
                v.tensor_scalar(DRW[:, 0, :], kb.banks[6].ap()[0:1, :],
                                1.0 / 1024.0, None, AL.mult)
                kb.bank_free[6] = kb.inc(KB.nop(v), "V")
                kb.chain(v, "V")
                v.tensor_scalar(DRW[:, 1, :], kb.banks[7].ap()[0:1, :],
                                1.0 / 1024.0, None, AL.mult)
                kb.bank_free[7] = kb.inc(KB.nop(v), "V")
                kb.chain(v, "V")
                v.tensor_tensor(DRW[:, 2, :], DRW[:, 0, :], DRW[:, 0, :], AL.mult)
                kb.inc(KB.nop(v), "V")
                kb.chain(v, "V")
                v.tensor_tensor(DRW[:, 1, :], DRW[:, 1, :], DRW[:, 2, :],
                                AL.subtract)
                tks["var"] = kb.inc(KB.nop(v), "V")

            @blk.scalar
            def _(s):
                kb.wait(s, tks["var"])
                kb.chain(s, "S")
                kb.inc(s.activation(DRW[:, 2, :], DRW[:, 1, :], AF.Sqrt,
                                    bias=C.epscol[0:1, 0:1]), "S")
            s_sdd = kb.tok("S")

            @blk.vector
            def _(v):
                kb.wait(v, s_sdd)
                kb.chain(v, "V")
                v.reciprocal(DRW[:, 1, :], DRW[:, 2, :])
                tks["rstd"] = kb.inc(KB.nop(v), "V")

            @blk.tensor
            def _(t):
                kb.wait(t, tks["rstd"])
                kb.use_bank(t, 6, 7)
                tks["bmu"] = kb.inc(
                    t.matmul(kb.banks[6].ap()[:, :], C.ones_row, DRW[:, 0, :],
                             start=True, stop=True), "T")
                tks["brs"] = kb.inc(
                    t.matmul(kb.banks[7].ap()[:, :], C.ones_row, DRW[:, 1, :],
                             start=True, stop=True), "T")

            for m in range(8):
                @blk.vector
                def _(v, m=m, nt2=nt2, ocol=ocol):
                    kb.wait(v, tks["bmu"], tks["brs"])
                    kb.wait(v, wfree.get("r4prev"))
                    kb.chain(v, "V")
                    v.tensor_tensor(TY1, OA[:, m, ocol], kb.banks[6].ap()[:, :],
                                    AL.subtract)
                    kb.inc(KB.nop(v), "V")
                    kb.chain(v, "V")
                    v.tensor_tensor(TY1, TY1, kb.banks[7].ap()[:, :], AL.mult)
                    tks["r4"] = kb.inc(KB.nop(v), "V")

                @blk.gpsimd
                def _(g, th=th, m=m, nt2=nt2, col=col):
                    kb.wait(g, tks["r4"])
                    wfree["r4prev"] = kb.dma(g, "GO%d" % (m % 2),
                                             out_fm[m][:, col], TY1)
            kb.bank_free[6] = kb.tok("V")
            kb.bank_free[7] = kb.tok("V")
        for fi in range(11):
            ofree[("ge", fi)] = kb.tok("V")
        wfree[("r3hl", 0)] = kb.tok("T")
        for c in range(8):
            wfree[("r3hl", c)] = wfree[("r3hl", 0)]

    @blk.gpsimd
    def _(g):
        kb.chain(g, "GO0")
        kb.chain(g, "GO1")


# version 21 (final)
# speedup vs baseline: 1.0739x; 1.0739x over previous
import math
from contextlib import ExitStack

import numpy as np

import concourse.bass as bass
import concourse.mybir as mybir
from concourse.bass_utils import run_bass_kernel_spmd

F32 = mybir.dt.float32
AL = mybir.AluOpType
AF = mybir.ActivationFunctionType
AX = mybir.AxisListType.X

T = 2048
CL = 128
NCH = T // CL
EPS = 1e-5



def _tiles(w, kp=128, mp=128):
    K, M = w.shape
    return np.ascontiguousarray(
        w.reshape(K // kp, kp, M // mp, mp).transpose(2, 0, 1, 3)
    )


def _alibi_slopes(n):
    def pow2(m):
        start = 2 ** (-(2 ** (-(math.log2(m) - 3))))
        return [start * start**i for i in range(m)]
    if math.log2(n).is_integer():
        return pow2(n)
    c = 2 ** math.floor(math.log2(n))
    return pow2(c) + _alibi_slopes(2 * c)[0::2][: n - c]


def _prep_inputs(inp):
    g = {}
    f32 = lambda a: np.ascontiguousarray(np.asarray(a, dtype=np.float32))

    g["c_ones_row"] = np.ones((1, 128), np.float32)
    g["c_ones_col"] = np.ones((128, 1), np.float32)
    g["c_ident"] = np.eye(128, dtype=np.float32)
    g["c_triu"] = np.triu(np.ones((128, 128), np.float32))
    g["c_negbig"] = np.where(
        np.arange(128)[:, None] <= np.arange(128)[None, :], 0.0, -1.0e30
    ).astype(np.float32)
    g["c_qpos"] = np.arange(T, dtype=np.float32).reshape(1, T)
    g["c_epscol"] = np.full((128, 1), 1e-5, np.float32)
    g["c_q2row"] = np.stack([np.ones(T, np.float32),
                             np.arange(T, dtype=np.float32)])
    e2t = np.zeros((2, 128), np.float32)
    e2t[0, :64] = 1.0
    e2t[1, 64:] = 1.0
    g["c_e2t"] = e2t
    g["qrw2_col"] = np.ascontiguousarray(
        np.tile(f32(inp["L1_attn_q_rms_w"]), 2).reshape(128, 1))
    g["krw2_col"] = np.ascontiguousarray(
        np.tile(f32(inp["L1_attn_k_rms_w"]), 2).reshape(128, 1))
    e2 = np.zeros((128, 2), np.float32)
    e2[:64, 0] = 1.0
    e2[64:, 1] = 1.0
    g["c_ehh"] = e2
    ew = np.zeros((8, 128), np.float32)
    for i in range(8):
        ew[i, i * 16 : (i + 1) * 16] = 1.0
    g["c_ew"] = ew

    g["wx_t"] = _tiles(f32(inp["L0_attn_Wx"]).T)
    g["wz_t"] = _tiles(f32(inp["L0_attn_Wz"]).T)
    g["wb_t"] = _tiles(f32(inp["L0_attn_Wb"]).T)
    g["wc_t"] = _tiles(f32(inp["L0_attn_Wc"]).T)
    g["wdt_t"] = _tiles(f32(inp["L0_attn_Wdt"]).T, mp=8)
    cw = f32(inp["L0_attn_conv_w"])[:, 0, :]
    g["convw_fm"] = np.ascontiguousarray(cw.reshape(16, 128, 4).transpose(1, 0, 2))
    g["convb_fm"] = np.ascontiguousarray(f32(inp["L0_attn_conv_b"]).reshape(16, 128).T)
    g["bdt_col"] = f32(inp["L0_attn_b_dt"]).reshape(8, 1)
    g["loga_col"] = f32(inp["L0_attn_log_a"]).reshape(8, 1)
    g["d_fm"] = np.ascontiguousarray(
        np.broadcast_to(np.repeat(f32(inp["L0_attn_d_param"]), 2)[None, :], (128, 16))
    )
    g["gnw_fm"] = np.ascontiguousarray(f32(inp["L0_attn_gn_w"]).reshape(16, 128).T)
    g["gnb_fm"] = np.ascontiguousarray(f32(inp["L0_attn_gn_b"]).reshape(16, 128).T)
    g["wout_t"] = _tiles(f32(inp["L0_attn_Wout"]).T)

    g["gw1_t"] = _tiles(f32(inp["L0_ffn_gw1"]).T)
    g["gb1_fm"] = np.ascontiguousarray(f32(inp["L0_ffn_gb1"]).reshape(2, 128).T)
    g["gw2_t"] = _tiles(f32(inp["L0_ffn_gw2"]).T, mp=8)
    g["gb2_row"] = f32(inp["L0_ffn_gb2"]).reshape(1, 8)
    for p in ("q", "k", "v", "o"):
        g[p + "wm_t"] = _tiles(f32(inp[f"L0_ffn_{p}Wm"]).T)
        V = f32(inp[f"L0_ffn_{p}V"]).reshape(128, 1024)
        g[p + "vcat_t"] = _tiles(V.T)
        U = f32(inp[f"L0_ffn_{p}U"]).transpose(0, 2, 1).reshape(128, 1024)
        g[p + "ucat_t"] = _tiles(U)
        g[p + "s_col"] = f32(inp[f"L0_ffn_{p}S"]).reshape(128, 1)
        g[p + "b_fm"] = np.ascontiguousarray(f32(inp[f"L0_ffn_{p}b"]).reshape(8, 128).T)

    g["wq_t"] = _tiles(f32(inp["L1_attn_Wq"]))
    g["wk_t"] = _tiles(f32(inp["L1_attn_Wk"]))
    wv = f32(inp["L1_attn_Wv"])
    g["wv_m"] = np.ascontiguousarray(wv.reshape(8, 128, 2, 512).transpose(0, 2, 1, 3))
    g["wproj_t"] = _tiles(f32(inp["L1_attn_Wproj"]))
    g["qrw_col"] = f32(inp["L1_attn_q_rms_w"]).reshape(64, 1)
    g["krw_col"] = f32(inp["L1_attn_k_rms_w"]).reshape(64, 1)
    g["omega_row"] = f32(inp["L1_attn_omega"]).reshape(1, 16)
    g["u16"] = f32(inp["L1_attn_u"])
    g["slopes_row"] = np.asarray(_alibi_slopes(16), np.float32).reshape(1, 16)

    g["rw_t"] = _tiles(f32(inp["L1_ffn_router_weight"]), mp=8)
    w1 = f32(inp["L1_ffn_w1"])
    g["w1_t"] = np.ascontiguousarray(
        w1.reshape(8, 8, 128, 22, 128).transpose(0, 3, 1, 2, 4)
    )
    w2 = f32(inp["L1_ffn_w2"])
    g["w2_t"] = np.ascontiguousarray(
        w2.reshape(8, 11, 128, 8, 128).transpose(0, 3, 1, 2, 4)
    )
    return g


_IN_SHAPES = [
    ("c_ones_row", (1, 128)), ("c_ones_col", (128, 1)), ("c_ident", (128, 128)),
    ("c_triu", (128, 128)), ("c_negbig", (128, 128)), ("c_qpos", (1, T)),
    ("c_ew", (8, 128)), ("c_ehh", (128, 2)), ("c_epscol", (128, 1)),
    ("c_e2t", (2, 128)), ("qrw2_col", (128, 1)), ("krw2_col", (128, 1)),
    ("c_q2row", (2, T)),
    ("wx_t", (16, 8, 128, 128)), ("wz_t", (16, 8, 128, 128)),
    ("wb_t", (4, 16, 128, 128)), ("wc_t", (4, 16, 128, 128)),
    ("wdt_t", (1, 16, 128, 8)),
    ("convw_fm", (128, 16, 4)), ("convb_fm", (128, 16)),
    ("bdt_col", (8, 1)), ("loga_col", (8, 1)), ("d_fm", (128, 16)),
    ("gnw_fm", (128, 16)), ("gnb_fm", (128, 16)),
    ("wout_t", (8, 16, 128, 128)),
    ("gw1_t", (2, 8, 128, 128)), ("gb1_fm", (128, 2)),
    ("gw2_t", (1, 2, 128, 8)), ("gb2_row", (1, 8)),
    ("qwm_t", (8, 8, 128, 128)), ("qvcat_t", (1, 8, 128, 128)),
    ("qucat_t", (8, 1, 128, 128)), ("qs_col", (128, 1)), ("qb_fm", (128, 8)),
    ("kwm_t", (8, 8, 128, 128)), ("kvcat_t", (1, 8, 128, 128)),
    ("kucat_t", (8, 1, 128, 128)), ("ks_col", (128, 1)), ("kb_fm", (128, 8)),
    ("vwm_t", (8, 8, 128, 128)), ("vvcat_t", (1, 8, 128, 128)),
    ("vucat_t", (8, 1, 128, 128)), ("vs_col", (128, 1)), ("vb_fm", (128, 8)),
    ("owm_t", (8, 8, 128, 128)), ("ovcat_t", (1, 8, 128, 128)),
    ("oucat_t", (8, 1, 128, 128)), ("os_col", (128, 1)), ("ob_fm", (128, 8)),
    ("wq_t", (8, 8, 128, 128)), ("wk_t", (8, 8, 128, 128)),
    ("wv_m", (8, 2, 128, 512)), ("wproj_t", (8, 8, 128, 128)),
    ("qrw_col", (64, 1)), ("krw_col", (64, 1)),
    ("omega_row", (1, 16)), ("u16", (16, 64)), ("slopes_row", (1, 16)),
    ("rw_t", (1, 8, 128, 8)),
    ("w1_t", (8, 22, 8, 128, 128)), ("w2_t", (8, 8, 11, 128, 128)),
]




class _FakeInstr:
    __slots__ = ("incs",)

    def __init__(self):
        self.incs = []

    def then_inc(self, sem, val=1, **kw):
        self.incs.append((sem, val))
        return self


class _EngProxy:
    def __init__(self, rec, kind):
        self._rec = rec
        self._kind = kind

    def __getattr__(self, attr):
        rec, kind = self._rec, self._kind

        def call(*a, **k):
            fake = _FakeInstr()
            rec[kind].append((attr, a, k, fake))
            return fake

        return call


class _FakeBlk:

    def __init__(self):
        self.rec = {k: [] for k in ("tensor", "vector", "scalar", "gpsimd",
                                    "sync")}
        for k in self.rec:
            setattr(self, k, self._mk(k))

    def _mk(self, kind):
        def section(fn):
            fn(_EngProxy(self.rec, kind))
        return section

    def flush(self, blk):
        for kind in ("sync", "gpsimd", "tensor", "vector", "scalar"):
            q = self.rec[kind]
            if not q:
                continue

            @getattr(blk, kind)
            def _(h, q=q):
                for attr, a, k, fake in q:
                    instr = getattr(h, attr)(*a, **k)
                    for sem, val in fake.incs:
                        instr.then_inc(sem, val)
            self.rec[kind] = []


class KB:
    def __init__(self, nc, stack, sbuf_els):
        self.nc = nc
        self.stack = stack
        self.sems = {}
        self.big = stack.enter_context(nc.sbuf_tensor("BIG", [128, sbuf_els], F32))
        self.banks = [
            stack.enter_context(nc.psum_tensor(f"bank{i}", [128, 512], F32))
            for i in range(8)
        ]
        self.bank_free = [None] * 8
        self.sbuf_els = sbuf_els
        self.regions = {}

    def region(self, name, els):
        assert name not in self.regions, name
        taken = sorted(self.regions.values())
        off = 0
        for a, b in taken:
            if off + els <= a:
                break
            off = max(off, b)
        if off + els > self.sbuf_els:
            print("MAP at overflow of", name, els)
            for n, (a, b) in sorted(self.regions.items(), key=lambda x: x[1]):
                print(f"  {a:6d}..{b:6d}  {n}")
            raise AssertionError(f"SBUF overflow: {name} @{off}+{els}")
        self.regions[name] = (off, off + els)
        return off

    def free(self, *names):
        for n in names:
            del self.regions[n]

    def view(self, name):
        a, b = self.regions[name]
        return self.big.ap()[:, a:b]

    def sem(self, name):
        if name not in self.sems:
            self.sems[name] = [self.stack.enter_context(self.nc.semaphore(name)), 0]
        return self.sems[name]

    def inc(self, instr, name, k=1):
        s = self.sem(name)
        instr.then_inc(s[0], k)
        s[1] += k
        return (name, s[1])

    def tok(self, name):
        return (name, self.sem(name)[1])

    def wait(self, e, *toks):
        for tk in toks:
            if tk is None:
                continue
            name, v = tk
            if v > 0:
                e.wait_ge(self.sems[name][0], v)

    def chain(self, e, name):
        s = self.sem(name)
        if s[1] > 0:
            e.wait_ge(s[0], s[1])

    def dma(self, e, sem, out, in_, slow=False):
        self.chain(e, sem)
        return self.inc(
            e.dma_start(out=out, in_=in_, allow_slow_non_contiguous=slow),
            sem, 16)

    @staticmethod
    def nop(e):
        if isinstance(e, _EngProxy):
            return e.engine_nop() if e._kind in ("vector", "gpsimd") else e.nop()
        f = getattr(e, "engine_nop", None)
        return f() if f is not None else e.nop()

    def use_bank(self, e, *idxs):
        for i in idxs:
            if self.bank_free[i] is not None:
                self.wait(e, self.bank_free[i])
                self.bank_free[i] = None




def build_program():
    nc = bass.Bass(trn_type="TRN2", target_bir_lowering=False)
    dr = {}
    dr["x0fm"] = nc.dram_tensor("x0fm", [8, 128, T], F32, kind="ExternalInput").ap()
    for nm, shp in _IN_SHAPES:
        dr[nm] = nc.dram_tensor(nm, list(shp), F32, kind="ExternalInput").ap()
    for nm, shp in [
        ("xc_sp", (16, 128, T)), ("z_sp", (16, 128, T)), ("y_sp", (16, 128, T)),
        ("q0_sp", (8, 128, T)), ("k0_sp", (8, 128, T)), ("r2_sp", (8, 128, T)),
        ("q1_sp", (8, 128, T)), ("k1_sp", (8, 128, T)), ("r3_sp", (8, 128, T)), ("r1_sp", (8, 128, T)),
    ]:
        dr[nm] = nc.dram_tensor(nm, list(shp), F32).ap()
    out_fm = nc.dram_tensor("out_fm", [8, 128, T], F32, kind="ExternalOutput").ap()
    dr["out_fm"] = out_fm

    with ExitStack() as stack:
        kb = KB(nc, stack, sbuf_els=52480)
        with nc.Block() as realblk:
            blk = _FakeBlk()
            C = _load_consts(nc, blk, kb, dr)
            _phase_A1(nc, blk, kb, dr, C)
            _barrier(blk, kb)
            kb.free("XF")
            _phase_A2a(nc, blk, kb, dr, C)
            _barrier(blk, kb)
            _phase_A2b(nc, blk, kb, dr, C)
            _barrier(blk, kb)
            kb.free("BB", "CC", "DT8")
            _phase_A3(nc, blk, kb, dr, C)
            _barrier(blk, kb)
            _phase_B(nc, blk, kb, dr, C)
            _barrier(blk, kb)
            _phase_C(nc, blk, kb, dr, C)
            _barrier(blk, kb)
            _phase_D(nc, blk, kb, dr, C, out_fm)
            blk.flush(realblk)
    return nc


def _preload_set(nc, blk, kb, name):
    from concourse.hw_specs import get_activation_tables
    import concourse.mybir as _mb
    idx = list(get_activation_tables(nc.m.arch)).index(name)

    @blk.scalar
    def _(s):
        s.add_instruction(_mb.InstLoadActFuncSet(
            name=nc.get_next_instruction_name(), act_func_set_id=idx,
            ins=[], outs=[]))


def _barrier(blk, kb):
    if not hasattr(kb, "_bar_snap"):
        kb._bar_snap = {}
    toks = []
    for name, (h, cnt) in kb.sems.items():
        if cnt > 0 and kb._bar_snap.get(name) != cnt:
            toks.append((name, cnt))
            kb._bar_snap[name] = cnt
    for e in ["tensor", "vector", "scalar", "gpsimd", "sync"]:
        @getattr(blk, e)
        def _(h):
            kb.wait(h, *toks)
    kb.bank_free = [None] * 8


class _C:
    pass


def _load_consts(nc, blk, kb, dr):
    C = _C()
    kb.region("XF", 8 * T)
    kb.region("CONST", 768)
    kb.region("QPOS", T)
    big = kb.big.ap()
    co = kb.regions["CONST"][0]
    C.ones_row = big[0:1, co : co + 128]
    C.ones_col = big[:, co + 128 : co + 129]
    C.ident = big[:, co + 144 : co + 272]
    C.triu = big[:, co + 272 : co + 400]
    C.negbig = big[:, co + 400 : co + 528]
    C.d_fm = big[:, co + 528 : co + 544]
    C.gnw_fm = big[:, co + 544 : co + 560]
    C.gnb_fm = big[:, co + 560 : co + 576]
    C.convw = big[:, co + 576 : co + 640].rearrange("p (c k) -> p c k", k=4)
    C.convb = big[:, co + 640 : co + 656]
    C.bdt_col = big[0:8, co + 656 : co + 657]
    C.loga_col = big[0:8, co + 657 : co + 658]
    C.a_col = big[0:8, co + 658 : co + 659]
    C.epscol = big[:, co + 660 : co + 661]
    qo = kb.regions["QPOS"][0]
    C.qpos_row = big[0:1, qo : qo + T]
    C.XF = kb.view("XF").rearrange("p (c t) -> p c t", t=T)

    @blk.sync
    def _(s):
        items = [(C.XF[:, c, :], None, c) for c in range(8)] + [
            (ap, nm, None) for ap, nm in [
            (C.ones_row, "c_ones_row"), (C.ones_col, "c_ones_col"),
            (C.ident, "c_ident"), (C.triu, "c_triu"), (C.negbig, "c_negbig"),
            (C.qpos_row, "c_qpos"), (C.d_fm, "d_fm"), (C.gnw_fm, "gnw_fm"),
            (C.gnb_fm, "gnb_fm"), (C.convw, "convw_fm"), (C.convb, "convb_fm"),
            (C.bdt_col, "bdt_col"), (C.loga_col, "loga_col"),
            (C.epscol, "c_epscol"),
        ]]
        for j, (ap, nm, c) in enumerate(items):
            src_ap = dr["x0fm"][c] if nm is None else dr[nm]
            kb.dma(s, "LC%d" % (j % 2), ap, src_ap)
    C.lw0 = (kb.tok("LC0"), kb.tok("LC1"))

    @blk.scalar
    def _(s):
        kb.wait(s, *C.lw0)
        kb.inc(s.activation(C.a_col, C.loga_col, AF.Exp), "S")
    sa = kb.tok("S")

    @blk.vector
    def _(v):
        kb.wait(v, sa)
        kb.inc(v.tensor_scalar_mul(C.a_col, C.a_col, -1.0), "ACOL")
    return C




def _phase_A1(nc, blk, kb, dr, C):
    kb.region("WA", 2 * 1024)
    kb.region("WZ", 2 * 1024)
    kb.region("CS", 2 * 515)
    kb.region("CT", 2 * 512)
    kb.region("XCB", 2 * 512)
    kb.region("ZB", 2 * 512)
    WA = kb.view("WA").rearrange("p (b k c) -> p b k c", b=2, k=8)
    WZ = kb.view("WZ").rearrange("p (b k c) -> p b k c", b=2, k=8)
    CS = kb.view("CS").rearrange("p (b t) -> p b t", b=2)
    CT = kb.view("CT").rearrange("p (b t) -> p b t", b=2)
    XCB = kb.view("XCB").rearrange("p (b t) -> p b t", b=2)
    ZB = kb.view("ZB").rearrange("p (b t) -> p b t", b=2)
    prev_silu = [None]
    spillx, spillz = {}, {}
    a1free = {}

    for m in range(16):
        mb = m % 2

        wtoks = {}

        @blk.sync
        def _(s, m=m, mb=mb):
            if m >= 2:
                kb.wait(s, *a1free[m - 2])
            wtoks["x"] = kb.dma(s, "WX%d" % mb, WA[:, mb],
                                dr["wx_t"][m].rearrange("k p c -> p k c"))
            wtoks["z"] = kb.dma(s, "WZ%d" % mb, WZ[:, mb],
                                dr["wz_t"][m].rearrange("k p c -> p k c"))

        for nt in range(4):
            i = 4 * m + nt
            tb = i % 2
            bx = [0, 2][nt % 2]
            bz = [4, 6][nt % 2]
            tks = {}

            @blk.tensor
            def _(t, m=m, nt=nt, bx=bx, bz=bz, mb=mb):
                kb.wait(t, wtoks["x"], wtoks["z"], *C.lw0)
                kb.use_bank(t, bx)
                for k in range(8):
                    mm = t.matmul(kb.banks[bx].ap()[:, :], WA[:, mb, k, :],
                                  C.XF[:, k, 512 * nt : 512 * nt + 512],
                                  start=(k == 0), stop=(k == 7))
                tks["xz"] = kb.inc(mm, "T")
                kb.use_bank(t, bz)
                for k in range(8):
                    mm = t.matmul(kb.banks[bz].ap()[:, :], WZ[:, mb, k, :],
                                  C.XF[:, k, 512 * nt : 512 * nt + 512],
                                  start=(k == 0), stop=(k == 7))
                tks["z"] = kb.inc(mm, "T")
                if nt == 3:
                    a1free[m] = (tks["xz"], tks["z"])

            @blk.vector
            def _(v, m=m, nt=nt, i=i, tb=tb, bx=bx):
                kb.wait(v, tks["xz"])
                kb.chain(v, "V")
                if i >= 2:
                    kb.wait(v, kb.tok("Vcs%d" % tb))
                if nt == 0:
                    v.memset(CS[:, tb, 0:3], 0.0)
                else:
                    v.tensor_copy(CS[:, tb, 0:3], CS[:, 1 - tb, 512:515])
                kb.inc(KB.nop(v), "V")
                kb.chain(v, "V")
                ev = v.tensor_copy(CS[:, tb, 3:515], kb.banks[bx].ap()[:, :])
                kb.bank_free[bx] = kb.inc(ev, "V")
                kb.chain(v, "V")
                kb.wait(v, prev_silu[0])
                v.tensor_scalar(CT[:, tb], CS[:, tb, 0:512],
                                C.convw[:, m, 0:1], None, AL.mult)
                kb.inc(KB.nop(v), "V")
                kb.chain(v, "V")
                v.scalar_tensor_tensor(CT[:, 1 - tb], CS[:, tb, 1:513],
                                       C.convw[:, m, 1:2], CT[:, tb], AL.mult, AL.add)
                kb.inc(KB.nop(v), "V")
                kb.chain(v, "V")
                v.scalar_tensor_tensor(CT[:, tb], CS[:, tb, 2:514],
                                       C.convw[:, m, 2:3], CT[:, 1 - tb], AL.mult, AL.add)
                kb.inc(KB.nop(v), "V")
                kb.chain(v, "V")
                cv = v.scalar_tensor_tensor(CT[:, 1 - tb], CS[:, tb, 3:515],
                                            C.convw[:, m, 3:4], CT[:, tb], AL.mult, AL.add)
                kb.inc(cv, "Vcs%d" % tb)
                kb.chain(v, "Vcs%d" % tb)
                tks["conv"] = kb.inc(KB.nop(v), "V")

            @blk.scalar
            def _(s, m=m, nt=nt, i=i, tb=tb, bz=bz):
                kb.wait(s, tks["conv"])
                if i >= 2:
                    kb.wait(s, spillx.get(tb), spillz.get(tb))
                kb.chain(s, "S")
                k1 = s.activation(XCB[:, tb], CT[:, 1 - tb], AF.Silu,
                                  bias=C.convb[:, m : m + 1])
                prev_silu[0] = kb.inc(k1, "S")
                kb.wait(s, tks["z"])
                kb.chain(s, "S")
                k2 = s.activation(ZB[:, tb], kb.banks[bz].ap()[:, :], AF.Silu)
                kb.bank_free[bz] = kb.inc(k2, "S")
            s_now = kb.tok("S")

            @blk.gpsimd
            def _(g, m=m, nt=nt, tb=tb, s_now=s_now):
                kb.wait(g, s_now)
                spillx[tb] = kb.dma(g, "SX%d" % tb,
                                    dr["xc_sp"][m][:, 512 * nt : 512 * nt + 512],
                                    XCB[:, tb])
                spillz[tb] = kb.dma(g, "SZ%d" % tb,
                                    dr["z_sp"][m][:, 512 * nt : 512 * nt + 512],
                                    ZB[:, tb])

    kb.free("WA", "WZ", "CS", "CT", "XCB", "ZB")




def _phase_A2a(nc, blk, kb, dr, C):
    kb.region("BB", 4 * T)
    kb.region("CC", 4 * T)
    kb.region("DT8", T)
    kb.region("XCN", 2 * 16 * 512)
    kb.region("WST", 2 * 16 * 128)
    BB = kb.view("BB").rearrange("p (c t) -> p c t", t=T)
    CC = kb.view("CC").rearrange("p (c t) -> p c t", t=T)
    DT8 = kb.view("DT8")[0:8, :]
    XCN = kb.view("XCN").rearrange("p (b k t) -> p b k t", b=2, k=16)
    WST = kb.view("WST").rearrange("p (b k c) -> p b k c", b=2, k=16)
    C.BB, C.CC, C.DT8 = BB, CC, DT8

    groups = [("wb_t", 0), ("wb_t", 1), ("wb_t", 2), ("wb_t", 3),
              ("wc_t", 0), ("wc_t", 1), ("wc_t", 2), ("wc_t", 3),
              ("wdt_t", 0)]
    a2wfree, a2xfree = {}, {}

    for nt in range(4):
        xb = nt % 2

        gtk = {}

        @blk.gpsimd
        def _(g, nt=nt, xb=xb):
            if xb in a2xfree:
                kb.wait(g, a2xfree[xb])
            gtk["x"] = kb.dma(g, "GXN%d" % xb, XCN[:, xb],
                              dr["xc_sp"][:, :, 512 * nt : 512 * nt + 512]
                              .rearrange("k p t -> p k t"))
        gd_now = gtk["x"]

        for gi, (wnm, m) in enumerate(groups):
            wb = gi % 2
            bank = gi % 2
            tks = {}
            @blk.sync
            def _(s, wnm=wnm, m=m, wb=wb, gi=gi, nt=nt):
                if (wb in a2wfree):
                    kb.wait(s, a2wfree[wb])
                mp = dr[wnm].shape[3]
                tks["w"] = kb.dma(s, "WS%d" % wb, WST[:, wb, :, 0:mp],
                                  dr[wnm][m].rearrange("k p c -> p k c"))

            @blk.tensor
            def _(t, wnm=wnm, wb=wb, bank=bank, gi=gi, xb=xb, gd_now=gd_now):
                kb.wait(t, tks["w"], gd_now)
                kb.use_bank(t, bank)
                mp = dr[wnm].shape[3]
                for k in range(16):
                    mm = t.matmul(kb.banks[bank].ap()[0:mp, :], WST[:, wb, k, 0:mp],
                                  XCN[:, xb, k, :], start=(k == 0), stop=(k == 15))
                tks["mm"] = kb.inc(mm, "T")
                a2wfree[wb] = tks["mm"]
                if gi == len(groups) - 1:
                    a2xfree[xb] = tks["mm"]

            if wnm == "wdt_t":
                @blk.scalar
                def _(s, nt=nt, bank=bank):
                    kb.wait(s, tks["mm"])
                    kb.chain(s, "S")
                    ev = s.activation(DT8[:, 512 * nt : 512 * nt + 512],
                                      kb.banks[bank].ap()[0:8, :], AF.Exp,
                                      bias=C.bdt_col)
                    kb.bank_free[bank] = kb.inc(ev, "S")
            else:
                @blk.vector
                def _(v, wnm=wnm, m=m, nt=nt, bank=bank):
                    kb.wait(v, tks["mm"])
                    kb.chain(v, "V")
                    dst = BB if wnm == "wb_t" else CC
                    ev = v.tensor_copy(dst[:, m, 512 * nt : 512 * nt + 512],
                                       kb.banks[bank].ap()[:, :])
                    kb.bank_free[bank] = kb.inc(ev, "V")

    @blk.scalar
    def _(s):
        kb.wait(s, kb.tok("S"), kb.tok("V"))
        kb.chain(s, "S")
        kb.inc(s.activation(DT8, DT8, AF.Ln, bias=1.0), "S")
    kb.free("XCN", "WST")




def _phase_A2b(nc, blk, kb, dr, C):
    kb.region("LC8", T)
    kb.region("EPR", 2 * T)
    kb.region("EMR", 2 * T)
    kb.region("EP8", T)
    kb.region("EM8", T)
    kb.region("XCT", 2 * 256)
    kb.region("BP", 2 * 128)
    kb.region("CP", 2 * 128)
    kb.region("BTM", 2 * 64)
    kb.region("GMT", 2 * 128)
    kb.region("XTM", 2 * 256)
    kb.region("SST", 4 * 256)
    kb.region("EC", 2)
    kb.region("YEV", 2 * 256)
    kb.region("STAT", 64)
    kb.region("TMP1", 1)
    kb.region("SQT", 128)
    kb.region("TMPS", 2)

    LC8 = kb.view("LC8")[0:8, :]
    EPR = kb.view("EPR")[0:1, :].rearrange("p (b t) -> p b t", b=2)
    EMR = kb.view("EMR")[0:1, :].rearrange("p (b t) -> p b t", b=2)
    EP8 = kb.view("EP8")[0:8, :]
    EM8 = kb.view("EM8")[0:8, :]
    XCT = kb.view("XCT").rearrange("p (b t) -> p b t", b=2)
    BP = kb.view("BP").rearrange("p (b t) -> p b t", b=2)[0:64]
    CP = kb.view("CP").rearrange("p (b t) -> p b t", b=2)[0:64]
    BTM = kb.view("BTM").rearrange("p (b t) -> p b t", b=2)
    GMT = kb.view("GMT").rearrange("p (b t) -> p b t", b=2)
    XTM = kb.view("XTM").rearrange("p (b t) -> p b t", b=2)
    SST = kb.view("SST").rearrange("p (b t) -> p b t", b=4)[0:64]
    EC = kb.view("EC").rearrange("p (b t) -> p b t", b=2)[0:64]
    YEV = kb.view("YEV").rearrange("p (b t) -> p b t", b=2)
    STAT = kb.view("STAT").rearrange("p (b t) -> p b t", b=4)
    TMP1 = kb.view("TMP1")
    TMPS = kb.view("TMPS")
    SQT = kb.view("SQT")
    DT8 = C.DT8

    @blk.vector
    def _(v):
        kb.wait(v, kb.tok("S"), kb.tok("ACOL"))
        kb.chain(v, "V")
        v.tensor_scalar(LC8, DT8, C.a_col, None, AL.mult)
        kb.inc(KB.nop(v), "V")
        ones8 = C.ones_col[0:8, 0:1]
        for l in range(NCH):
            kb.chain(v, "V")
            sl = slice(CL * l, CL * l + CL)
            v.tensor_tensor_scan(LC8[:, sl], ones8.broadcast_to([8, CL]),
                                 LC8[:, sl], 0.0, AL.mult, AL.add)
            kb.inc(KB.nop(v), "V")
    v_lc = kb.tok("V")

    @blk.scalar
    def _(s):
        kb.wait(s, v_lc)
        kb.chain(s, "S")
        kb.inc(s.activation(EP8, LC8, AF.Exp), "S")
        kb.chain(s, "S")
        kb.inc(s.activation(EM8, LC8, AF.Exp, scale=-1.0), "S")
    s_e8 = kb.tok("S")

    @blk.vector
    def _(v):
        kb.wait(v, s_e8)
        kb.chain(v, "V")
        kb.inc(v.tensor_tensor(EM8, EM8, DT8, AL.mult), "V")
        for sb_ in range(4):
            kb.chain(v, "V")
            v.memset(STAT[:, sb_, :], 0.0)
            kb.inc(KB.nop(v), "V")
    v_em = kb.tok("V")

    BB, CC = C.BB, C.CC
    spilly, schfree = {}, {}
    stat_done = {}
    g_rows = {}

    def a2b_head_rows(h):
        hb = h % 2

        @blk.gpsimd
        def _(g, h=h, hb=hb):
            kb.wait(g, v_em)
            if hb in schfree:
                kb.wait(g, schfree[hb])
            kb.dma(g, "GR%d" % hb, EPR[:, hb, :], EP8[h : h + 1, :])
            g_rows[hb] = kb.dma(g, "GR%d" % hb, EMR[:, hb, :], EM8[h : h + 1, :])

    def a2b_iter(h, l):
        hb = h % 2
        cb = h // 2
        hp = 64 * (h % 2)
        bA, bB, bC, bD = 4 * hb, 4 * hb + 1, 4 * hb + 2, 4 * hb + 3
        sb_old, sb_new = 2 * hb + l % 2, 2 * hb + 1 - l % 2
        tcol = slice(CL * l, CL * l + CL)
        tks = {}

        @blk.gpsimd
        def _(g, h=h, l=l, hb=hb):
            if h >= 2 or l >= 1:
                kb.wait(g, kb.tok("XCF%d" % hb))
            tks["xc"] = kb.dma(g, "GC%d" % hb,
                               XCT[:, hb].rearrange("p (c t) -> p c t", c=2),
                               dr["xc_sp"][2 * h : 2 * h + 2, :, CL * l : CL * l + CL]
                               .rearrange("c p t -> p c t"))
        g_xc = tks["xc"]

        @blk.tensor
        def _(t, l=l, hb=hb, bA=bA):
            kb.wait(t, g_rows[hb])
            kb.use_bank(t, bA)
            t.matmul(kb.banks[bA].ap()[:, 0:CL], C.ones_row, EMR[:, hb, tcol],
                     start=True, stop=True)
            t.matmul(kb.banks[bA].ap()[:, CL : 2 * CL], C.ones_row,
                     EPR[:, hb, tcol], start=True, stop=True)
            mm = t.matmul(kb.banks[bA].ap()[0:64, 2 * CL : 2 * CL + 1],
                          C.ones_row[0:1, 0:64],
                          EPR[:, hb, CL * l + CL - 1 : CL * l + CL],
                          start=True, stop=True)
            tks["bc"] = kb.inc(mm, "T")
            if l == NCH - 1:
                schfree[hb] = tks["bc"]

        @blk.vector
        def _(v, l=l, cb=cb, hp=hp, hb=hb, bA=bA):
            if l == 0:
                kb.chain(v, "V")
                v.memset(SST[:, 2 * hb, :], 0.0)
                kb.inc(KB.nop(v), "V")
            kb.wait(v, tks["bc"])
            kb.chain(v, "V")
            v.tensor_tensor(BP[:, hb], BB[hp : hp + 64, cb, tcol],
                            kb.banks[bA].ap()[hp : hp + 64, 0:CL], AL.mult)
            kb.inc(KB.nop(v), "V")
            kb.chain(v, "V")
            v.tensor_tensor(CP[:, hb], CC[hp : hp + 64, cb, tcol],
                            kb.banks[bA].ap()[hp : hp + 64, CL : 2 * CL], AL.mult)
            kb.inc(KB.nop(v), "V")
            kb.chain(v, "V")
            ev = v.tensor_copy(EC[:, hb], kb.banks[bA].ap()[0:64, 2 * CL : 2 * CL + 1])
            kb.bank_free[bA] = kb.inc(ev, "V")
            tks["bpcp"] = kb.bank_free[bA]

        @blk.tensor
        def _(t, hb=hb, bB=bB, bC=bC, g_xc=g_xc):
            kb.wait(t, tks["bpcp"])
            kb.use_bank(t, bB)
            tks["tbp"] = kb.inc(
                t.transpose(kb.banks[bB].ap()[:, 0:64], BP[:, hb],
                            C.ident[0:64, 0:64]), "T")
            kb.wait(t, g_xc)
            kb.use_bank(t, bC)
            t.transpose(kb.banks[bC].ap()[:, 0:128], XCT[:, hb, 0:128], C.ident)
            tks["tx"] = kb.inc(
                t.transpose(kb.banks[bC].ap()[:, 128:256], XCT[:, hb, 128:256],
                            C.ident), "T")
            mm = t.matmul(kb.banks[bB].ap()[:, 64 : 64 + CL], BP[:, hb], CP[:, hb],
                          start=True, stop=True)
            tks["g"] = kb.inc(mm, "T")

        @blk.vector
        def _(v, hb=hb, bB=bB, bC=bC):
            kb.wait(v, tks["tbp"])
            kb.chain(v, "V")
            kb.inc(v.tensor_copy(BTM[:, hb], kb.banks[bB].ap()[:, 0:64]), "V")
            kb.wait(v, tks["tx"])
            kb.chain(v, "V")
            ev = v.tensor_copy(XTM[:, hb], kb.banks[bC].ap()[:, 0:256])
            kb.bank_free[bC] = kb.inc(ev, "V")
            kb.wait(v, tks["g"])
            kb.chain(v, "V")
            ev = v.tensor_tensor(GMT[:, hb], kb.banks[bB].ap()[:, 64 : 64 + CL],
                                 C.triu, AL.mult)
            kb.bank_free[bB] = kb.inc(ev, "V")
            tks["gmt"] = kb.bank_free[bB]

        @blk.tensor
        def _(t, hb=hb, sb_old=sb_old, bD=bD):
            kb.wait(t, tks["gmt"])
            kb.use_bank(t, bD)
            t.matmul(kb.banks[bD].ap()[:, 0:128], XTM[:, hb, 0:128], GMT[:, hb],
                     start=True, stop=False)
            t.matmul(kb.banks[bD].ap()[:, 0:128], SST[:, sb_old, 0:128], CP[:, hb],
                     start=False, stop=True)
            t.matmul(kb.banks[bD].ap()[:, 128:256], XTM[:, hb, 128:256], GMT[:, hb],
                     start=True, stop=False)
            mm = t.matmul(kb.banks[bD].ap()[:, 128:256], SST[:, sb_old, 128:256],
                          CP[:, hb], start=False, stop=True)
            tks["y"] = kb.inc(mm, "T")
            mm = t.matmul(kb.banks[bD].ap()[0:64, 256:512], BTM[:, hb], XTM[:, hb],
                          start=True, stop=True)
            tks["ds"] = kb.inc(mm, "T")

        @blk.vector
        def _(v, hb=hb, sb_old=sb_old, sb_new=sb_new, bD=bD):
            kb.wait(v, tks["ds"])
            kb.chain(v, "V")
            v.tensor_tensor(SST[:, sb_new, :], SST[:, sb_old, :],
                            kb.banks[bD].ap()[0:64, 256:512], AL.add)
            kb.inc(KB.nop(v), "V")
            kb.chain(v, "V")
            v.tensor_scalar(SST[:, sb_new, :], SST[:, sb_new, :], EC[:, hb], None,
                            AL.mult)
            kb.inc(KB.nop(v), "V")

        @blk.vector
        def _(v, h=h, l=l, hb=hb, bD=bD):
            kb.wait(v, tks["y"])
            if spilly.get(hb) is not None:
                kb.wait(v, spilly[hb])
                kb.wait(v, stat_done[hb])
            kb.chain(v, "V")
            v.scalar_tensor_tensor(YEV[:, hb, 0:128], XCT[:, hb, 0:128],
                                   C.d_fm[:, 2 * h : 2 * h + 1],
                                   kb.banks[bD].ap()[:, 0:128], AL.mult, AL.add)
            kb.inc(KB.nop(v), "V")
            kb.chain(v, "V")
            ev = v.scalar_tensor_tensor(YEV[:, hb, 128:256], XCT[:, hb, 128:256],
                                        C.d_fm[:, 2 * h + 1 : 2 * h + 2],
                                        kb.banks[bD].ap()[:, 128:256], AL.mult,
                                        AL.add)
            kb.bank_free[bD] = kb.inc(ev, "V")
            kb.chain(v, "V")
            kb.inc(KB.nop(v), "XCF%d" % hb)
            tks["yev"] = kb.tok("V")

        @blk.scalar
        def _(s, h=h, l=l, hb=hb):
            kb.wait(s, tks["yev"])
            par, pav = l % 2, 1 - l % 2
            for half in range(2):
                c = 2 * h + half
                sl = slice(128 * half, 128 * half + 128)
                kb.chain(s, "S")
                kb.inc(s.activation(SQT, YEV[:, hb, sl], AF.Square,
                                    accum_out=TMPS[:, 0:1]), "S")
                kb.chain(s, "S")
                kb.inc(s.activation(STAT[:, 2 * par + 1, c : c + 1],
                                    TMPS[:, 0:1], AF.Identity,
                                    bias=STAT[:, 2 * pav + 1, c : c + 1]), "S")
                kb.chain(s, "S")
                kb.inc(s.activation(SQT, YEV[:, hb, sl], AF.Identity,
                                    accum_out=TMPS[:, 1:2]), "S")
                kb.chain(s, "S")
                kb.inc(s.activation(STAT[:, 2 * par, c : c + 1],
                                    TMPS[:, 1:2], AF.Identity,
                                    bias=STAT[:, 2 * pav, c : c + 1]), "S")
            stat_done[hb] = kb.tok("S")

        @blk.gpsimd
        def _(g, h=h, l=l, hb=hb):
            kb.wait(g, tks["yev"])
            spilly[hb] = kb.dma(
                g, "GY%d" % hb,
                dr["y_sp"][2 * h : 2 * h + 2, :, CL * l : CL * l + CL]
                .rearrange("c p t -> p c t"),
                YEV[:, hb].rearrange("p (c t) -> p c t", c=2))

    for pj in range(4):
        a2b_head_rows(2 * pj)
        a2b_head_rows(2 * pj + 1)
        for l in range(NCH):
            a2b_iter(2 * pj, l)
            a2b_iter(2 * pj + 1, l)

    kb.free("TMPS")
    kb.free("LC8", "EPR", "EMR", "EP8", "EM8", "XCT", "BP", "CP", "BTM", "GMT",
            "XTM", "EC", "YEV", "TMP1", "SQT", "SST")




def _phase_A3(nc, blk, kb, dr, C):
    kb.region("R1", 8 * T)
    kb.region("GBUF", 16 * 512)
    kb.region("YZP", 4 * 512)
    kb.region("X0B", 2 * 512)
    kb.region("RES", 8 * 512)
    kb.region("WOT", 2 * 16 * 128)
    kb.region("MRS", 96)
    kb.region("ROWS", 3 * 512)
    kb.region("SQ3", 512)

    R1 = kb.view("R1").rearrange("p (c t) -> p c t", t=T)
    GB = kb.view("GBUF").rearrange("p (k t) -> p k t", k=16)
    YZ = kb.view("YZP").rearrange("p (b t) -> p b t", b=4)
    X0B = kb.view("X0B").rearrange("p (b t) -> p b t", b=2)
    RES = kb.view("RES").rearrange("p (c t) -> p c t", c=8)
    WOT = kb.view("WOT").rearrange("p (b k c) -> p b k c", b=2, k=16)
    MRS = kb.view("MRS")
    MUB = MRS[:, 0:8]
    RSB = MRS[:, 8:16]
    ROWS = kb.view("ROWS").rearrange("p (b t) -> p b t", b=3)
    SQ3 = kb.view("SQ3")
    STATf = kb.view("STAT")[:, 32:64]

    tks = {}

    @blk.tensor
    def _(t):
        kb.wait(t, kb.tok("V"), kb.tok("S"))
        kb.use_bank(t, 0)
        mm = t.matmul(kb.banks[0].ap()[0:1, 0:32], C.ones_col, STATf,
                      start=True, stop=True)
        tks["tsum"] = kb.inc(mm, "T")

    @blk.vector
    def _(v):
        kb.wait(v, tks["tsum"])
        N = float(T * 256)
        kb.chain(v, "V")
        ev = v.tensor_copy(MRS[0:1, 16:48], kb.banks[0].ap()[0:1, 0:32])
        kb.bank_free[0] = kb.inc(ev, "V")
        p2 = lambda o: MRS[0:1, o : o + 16].rearrange("p (h b) -> p h b", b=2)
        o1 = lambda o: MRS[0:1, o : o + 8].rearrange("p (h b) -> p h b", b=1)
        kb.chain(v, "V")
        v.tensor_tensor(o1(48), p2(16)[:, :, 0:1], p2(16)[:, :, 1:2], AL.add)
        kb.inc(KB.nop(v), "V")
        kb.chain(v, "V")
        v.tensor_tensor(o1(56), p2(32)[:, :, 0:1], p2(32)[:, :, 1:2], AL.add)
        kb.inc(KB.nop(v), "V")
        kb.chain(v, "V")
        v.tensor_scalar(MRS[0:1, 64:72], MRS[0:1, 48:56], 1.0 / N, None, AL.mult)
        kb.inc(KB.nop(v), "V")
        kb.chain(v, "V")
        v.tensor_scalar(MRS[0:1, 72:80], MRS[0:1, 56:64], 1.0 / N, None, AL.mult)
        kb.inc(KB.nop(v), "V")
        kb.chain(v, "V")
        v.tensor_tensor(MRS[0:1, 80:88], MRS[0:1, 64:72], MRS[0:1, 64:72], AL.mult)
        kb.inc(KB.nop(v), "V")
        kb.chain(v, "V")
        v.tensor_tensor(MRS[0:1, 72:80], MRS[0:1, 72:80], MRS[0:1, 80:88],
                        AL.subtract)
        tks["var"] = kb.inc(KB.nop(v), "V")

    @blk.scalar
    def _(s):
        kb.wait(s, tks["var"])
        kb.chain(s, "S")
        kb.inc(s.activation(MRS[0:1, 80:88], MRS[0:1, 72:80], AF.Sqrt, bias=C.epscol[0:1, 0:1]), "S")
    s_sd0 = kb.tok("S")

    @blk.vector
    def _(v):
        kb.wait(v, s_sd0)
        kb.chain(v, "V")
        v.reciprocal(MRS[0:1, 88:96], MRS[0:1, 80:88])
        tks["rstd"] = kb.inc(KB.nop(v), "V")

    @blk.tensor
    def _(t):
        kb.wait(t, tks["rstd"])
        kb.use_bank(t, 0)
        t.matmul(kb.banks[0].ap()[:, 0:8], C.ones_row, MRS[0:1, 64:72],
                 start=True, stop=True)
        mm = t.matmul(kb.banks[0].ap()[:, 8:16], C.ones_row, MRS[0:1, 88:96],
                      start=True, stop=True)
        tks["bcmu"] = kb.inc(mm, "T")

    @blk.vector
    def _(v):
        kb.wait(v, tks["bcmu"])
        kb.chain(v, "V")
        ev = v.tensor_copy(MRS[:, 0:16], kb.banks[0].ap()[:, 0:16])
        kb.bank_free[0] = kb.inc(ev, "V")
    v_mrs = kb.tok("V")

    for nt in range(4):
        ntc = slice(512 * nt, 512 * nt + 512)
        for k in range(16):
            j = 16 * nt + k
            pb = j % 2
            tks2 = {}

            @blk.gpsimd
            def _(g, nt=nt, k=k, j=j, pb=pb):
                if j >= 2:
                    kb.wait(g, kb.tok("A3G%d" % pb))
                tks2["y"] = kb.dma(g, "YL%d" % pb, YZ[:, 2 * pb],
                                   dr["y_sp"][k][:, ntc])
                tks2["z"] = kb.dma(g, "ZL%d" % pb, YZ[:, 2 * pb + 1],
                                   dr["z_sp"][k][:, ntc])

            @blk.vector
            def _(v, nt=nt, k=k, pb=pb):
                kb.wait(v, tks2["y"], tks2["z"], v_mrs)
                kb.chain(v, "V")
                hk = k // 2
                v.tensor_scalar(GB[:, k, :], YZ[:, 2 * pb],
                                MUB[:, hk : hk + 1], RSB[:, hk : hk + 1],
                                AL.subtract, AL.mult)
                kb.inc(KB.nop(v), "V")
                kb.chain(v, "V")
                v.tensor_scalar(GB[:, k, :], GB[:, k, :],
                                C.gnw_fm[:, k : k + 1], C.gnb_fm[:, k : k + 1],
                                AL.mult, AL.add)
                kb.inc(KB.nop(v), "V")
                kb.chain(v, "V")
                v.tensor_tensor(GB[:, k, :], GB[:, k, :], YZ[:, 2 * pb + 1], AL.mult)
                kb.inc(KB.nop(v), "V")
                kb.chain(v, "V")
                kb.inc(KB.nop(v), "A3G%d" % pb)
        v_g = kb.tok("V")

        last_st = [None]
        a3wfree = getattr(kb, "_a3wfree", {})
        kb._a3wfree = a3wfree
        for m in range(8):
            wb = m % 2
            bank = 1 + m % 2
            tks3 = {}

            @blk.sync
            def _(s, nt=nt, m=m, wb=wb):
                if wb in a3wfree:
                    kb.wait(s, a3wfree[wb])
                tks3["w"] = kb.dma(s, "WO%d" % wb, WOT[:, wb],
                                   dr["wout_t"][m].rearrange("k p c -> p k c"))

            @blk.gpsimd
            def _(g, nt=nt, m=m):
                if 8 * nt + m >= 2:
                    kb.wait(g, kb.tok("A3X%d" % (m % 2)))
                tks3["x0"] = kb.dma(g, "XL%d" % (m % 2), X0B[:, m % 2],
                                    dr["x0fm"][m][:, ntc])

            @blk.tensor
            def _(t, m=m, wb=wb, bank=bank):
                kb.wait(t, tks3["w"], v_g)
                kb.use_bank(t, bank)
                for k in range(16):
                    mm = t.matmul(kb.banks[bank].ap()[:, :], WOT[:, wb, k, :],
                                  GB[:, k, :], start=(k == 0), stop=(k == 15))
                tks3["mm"] = kb.inc(mm, "T")
                a3wfree[wb] = tks3["mm"]

            @blk.vector
            def _(v, m=m, bank=bank):
                kb.wait(v, tks3["mm"], tks3["x0"])
                kb.chain(v, "V")
                ev = v.tensor_tensor(RES[:, m, :], kb.banks[bank].ap()[:, :],
                                     X0B[:, m % 2], AL.add)
                kb.bank_free[bank] = kb.inc(ev, "V")
                kb.chain(v, "V")
                kb.inc(KB.nop(v), "A3X%d" % (m % 2))
                kb.chain(v, "V")
                kb.wait(v, last_st[0])
                v.tensor_tensor(SQ3, RES[:, m, :], RES[:, m, :], AL.mult)
                tks3["sq"] = kb.inc(KB.nop(v), "V")

            @blk.tensor
            def _(t, m=m):
                if m == 0:
                    kb.use_bank(t, 3, 4)
                kb.wait(t, kb.tok("V"))
                t.matmul(kb.banks[3].ap()[0:1, :], C.ones_col, RES[:, m, :],
                         start=(m == 0), stop=(m == 7))
                mm = t.matmul(kb.banks[4].ap()[0:1, :], C.ones_col, SQ3,
                              start=(m == 0), stop=(m == 7))
                tks3["st"] = last_st[0] = kb.inc(mm, "T")
        t_st = kb.tok("T")

        tks4 = {}

        @blk.vector
        def _(v, nt=nt):
            kb.wait(v, t_st)
            kb.chain(v, "V")
            v.tensor_scalar(ROWS[0:1, 0, :], kb.banks[3].ap()[0:1, :],
                            1.0 / 1024.0, None, AL.mult)
            kb.bank_free[3] = kb.inc(KB.nop(v), "V")
            kb.chain(v, "V")
            v.tensor_scalar(ROWS[0:1, 1, :], kb.banks[4].ap()[0:1, :],
                            1.0 / 1024.0, None, AL.mult)
            kb.bank_free[4] = kb.inc(KB.nop(v), "V")
            kb.chain(v, "V")
            v.tensor_tensor(ROWS[0:1, 2, :], ROWS[0:1, 0, :], ROWS[0:1, 0, :],
                            AL.mult)
            kb.inc(KB.nop(v), "V")
            kb.chain(v, "V")
            v.tensor_tensor(ROWS[0:1, 1, :], ROWS[0:1, 1, :], ROWS[0:1, 2, :],
                            AL.subtract)
            tks4["var"] = kb.inc(KB.nop(v), "V")

        @blk.scalar
        def _(s):
            kb.wait(s, tks4["var"])
            kb.chain(s, "S")
            kb.inc(s.activation(ROWS[0:1, 2, :], ROWS[0:1, 1, :], AF.Sqrt,
                                bias=C.epscol[0:1, 0:1]), "S")
        s_sd2b = kb.tok("S")

        @blk.vector
        def _(v):
            kb.wait(v, s_sd2b)
            kb.chain(v, "V")
            v.reciprocal(ROWS[0:1, 1, :], ROWS[0:1, 2, :])
            tks4["rstd"] = kb.inc(KB.nop(v), "V")

        @blk.tensor
        def _(t):
            kb.wait(t, tks4["rstd"])
            kb.use_bank(t, 5, 6)
            mm = t.matmul(kb.banks[5].ap()[:, :], C.ones_row, ROWS[0:1, 0, :],
                          start=True, stop=True)
            tks4["bmu"] = kb.inc(mm, "T")
            mm = t.matmul(kb.banks[6].ap()[:, :], C.ones_row, ROWS[0:1, 1, :],
                          start=True, stop=True)
            tks4["brs"] = kb.inc(mm, "T")

        @blk.vector
        def _(v, nt=nt):
            kb.wait(v, tks4["bmu"], tks4["brs"])
            for m in range(8):
                kb.chain(v, "V")
                v.tensor_tensor(R1[:, m, ntc], RES[:, m, :], kb.banks[5].ap()[:, :],
                                AL.subtract)
                kb.inc(KB.nop(v), "V")
                kb.chain(v, "V")
                v.tensor_tensor(R1[:, m, ntc], R1[:, m, ntc],
                                kb.banks[6].ap()[:, :], AL.mult)
                kb.inc(KB.nop(v), "V")
            kb.bank_free[5] = kb.tok("V")
            kb.bank_free[6] = kb.tok("V")

    kb.free("GBUF", "YZP", "X0B", "RES", "WOT", "MRS", "ROWS", "SQ3", "STAT")




def kernel(**inputs):
    g = _prep_inputs(inputs)
    x = np.asarray(inputs["L0_attn_input_tensor"], dtype=np.float32)
    nc = build_program()
    in_maps = []
    for b in range(8):
        m = dict(g)
        m["x0fm"] = np.ascontiguousarray(x[b].T).reshape(8, 128, T)
        in_maps.append(m)
    res = run_bass_kernel_spmd(nc, in_maps, core_ids=list(range(8)))
    outs = []
    for b in range(8):
        o = np.asarray(res.results[b]["out_fm"]).reshape(1024, T)
        outs.append(np.ascontiguousarray(o.T))
    return np.stack(outs).astype(np.float32)




def _phase_B(nc, blk, kb, dr, C):
    R1 = kb.view("R1").rearrange("p (c t) -> p c t", t=T)
    kb.region("BSM", 192)
    kb.region("BBIAS", 32)
    kb.region("XVS", 2048)
    kb.region("VCST", 1024)
    kb.region("WMST", 2 * 1024)
    kb.region("UCST", 2 * 128)
    kb.region("QKB", 2 * 512)
    kb.region("SQB", 512)
    kb.region("SQ4", 512)
    kb.region("VT", 16 * 1040)
    kb.region("VF", 512)

    big = kb.big.ap()
    sm = kb.regions["BSM"][0]
    PL = big[:, sm : sm + 8]
    H2 = big[:, sm + 8 : sm + 10]
    LG8 = big[0:8, sm + 10 : sm + 11]
    WCOL = big[0:8, sm + 11 : sm + 12]
    SCL = {p: big[:, sm + 12 + i : sm + 13 + i] for i, p in enumerate("qkvo")}
    NEGMB = big[:, sm + 16 : sm + 32]
    MAXSQ = {"q": big[0:2, sm + 32 : sm + 40], "k": big[0:2, sm + 40 : sm + 48]}
    LROW = big[0:1, sm + 48 : sm + 56]
    E1R = big[0:1, sm + 56 : sm + 64]
    L2R = big[0:1, sm + 64 : sm + 72]
    MKR = big[0:1, sm + 72 : sm + 80]
    PMR = big[0:1, sm + 80 : sm + 88]
    BR = big[0:1, sm + 88 : sm + 96]
    MROW = {"q": big[0:1, sm + 96 : sm + 112], "k": big[0:1, sm + 112 : sm + 128]}
    NMROW = big[0:1, sm + 128 : sm + 144]
    MX8 = big[0:8, sm + 144 : sm + 146]
    SRAW = {p: big[:, sm + 146 + i : sm + 147 + i] for i, p in enumerate("qkvo")}
    GB2R = big[0:1, sm + 152 : sm + 160]
    EHH = big[:, sm + 160 : sm + 162]
    GB1 = big[:, sm + 162 : sm + 164]
    bb = kb.regions["BBIAS"][0]
    PB = {p: big[:, bb + 8 * i : bb + 8 * i + 8] for i, p in enumerate("qkvo")}
    XVS = kb.view("XVS")
    VCST = kb.view("VCST").rearrange("p (k c) -> p k c", k=8)
    WMST = kb.view("WMST").rearrange("p (b k c) -> p b k c", b=2, k=8)
    UCST = kb.view("UCST").rearrange("p (b c) -> p b c", b=2)
    QKB = kb.view("QKB").rearrange("p (b t) -> p b t", b=2)
    SQB = kb.view("SQB")
    SQ4 = kb.view("SQ4")
    VT3 = kb.view("VT").rearrange("p (s c) -> p s c", c=1040)
    VF = kb.view("VF")

    kb.region("GRT", 2048 + 128 + 16)
    gr = kb.regions["GRT"][0]
    GW1 = big[:, gr : gr + 2048].rearrange("p (m k c) -> p m k c", m=2, k=8)
    EWc = big[0:8, gr + 2048 : gr + 2176]
    GW2 = big[:, gr + 2176 : gr + 2192].rearrange("p (k c) -> p k c", k=2)

    tks = {}

    @blk.sync
    def _(s):
        for mi in range(2):
            kb.dma(s, "LC0", GW1[:, mi], dr["gw1_t"][mi].rearrange("k p c -> p k c"))
        kb.dma(s, "LC1", EWc, dr["c_ew"])
        kb.dma(s, "LC0", GW2, dr["gw2_t"][0].rearrange("k p c -> p k c"))
        kb.dma(s, "LC1", GB1, dr["gb1_fm"])
        kb.dma(s, "LC0", GB2R, dr["gb2_row"])
        kb.dma(s, "LC1", EHH, dr["c_ehh"])
        for i, p in enumerate("qkvo"):
            kb.dma(s, "LC0", PB[p], dr[p + "b_fm"])
            kb.dma(s, "LC1", SRAW[p], dr[p + "s_col"])
    lcend = (kb.tok("LC0"), kb.tok("LC1"))

    @blk.vector
    def _(v):
        kb.wait(v, *lcend)
        kb.chain(v, "V")
        v.reduce_sum(PL.rearrange("p (a b) -> p a b", b=1), R1, axis=AX)
        tks["pl"] = kb.inc(KB.nop(v), "V")

    @blk.tensor
    def _(t):
        kb.wait(t, tks["pl"])
        kb.use_bank(t, 0)
        for mi in range(2):
            for k in range(8):
                mm = t.matmul(kb.banks[0].ap()[:, mi : mi + 1], GW1[:, mi, k, :],
                              PL[:, k : k + 1], start=(k == 0), stop=(k == 7))
        tks["h"] = kb.inc(mm, "T")

    @blk.scalar
    def _(s):
        kb.wait(s, tks["h"])
        for mi in range(2):
            kb.chain(s, "S")
            ev = s.activation(H2[:, mi : mi + 1], kb.banks[0].ap()[:, mi : mi + 1],
                              AF.Relu, bias=GB1[:, mi : mi + 1], scale=1.0 / T)
            kb.bank_free[0] = kb.inc(ev, "S")
    tks["h2"] = kb.tok("S")

    @blk.tensor
    def _(t):
        kb.wait(t, tks["h2"])
        kb.use_bank(t, 1)
        for k in range(2):
            mm = t.matmul(kb.banks[1].ap()[0:8, 0:1], GW2[:, k, :],
                          H2[:, k : k + 1], start=(k == 0), stop=(k == 1))
        tks["lg"] = kb.inc(mm, "T")

    @blk.vector
    def _(v):
        kb.wait(v, tks["lg"])
        kb.chain(v, "V")
        ev = v.tensor_copy(LG8, kb.banks[1].ap()[0:8, 0:1])
        kb.bank_free[1] = kb.inc(ev, "V")
        tks["lg8"] = kb.bank_free[1]

    @blk.gpsimd
    def _(g):
        kb.wait(g, tks["lg8"])
        tks["lgt"] = kb.dma(g, "TRX", LROW.rearrange("p (a c) -> p a c", a=8),
                            LG8)

    @blk.vector
    def _(v):
        kb.wait(v, tks["lgt"])
        kb.chain(v, "V")
        ev = v.tensor_tensor(LROW, LROW, GB2R, AL.add)
        kb.inc(ev, "V")
        for op in range(5):
            kb.chain(v, "V")
            if op == 0:
                v.reduce_max(BR[:, 0:1], LROW, axis=AX)
            elif op == 1:
                v.tensor_scalar(E1R, LROW, BR[:, 0:1], None, AL.is_equal)
            elif op == 2:
                v.scalar_tensor_tensor(L2R, E1R, -1.0e30, LROW, AL.mult, AL.add)
            elif op == 3:
                v.reduce_max(BR[:, 1:2], L2R, axis=AX)
            elif op == 4:
                v.tensor_scalar(MKR, LROW, BR[:, 1:2], None, AL.is_ge)
            kb.inc(KB.nop(v), "V")
        kb.chain(v, "V")
        v.tensor_scalar(BR[:, 3:4], BR[:, 0:1], -1.0, None, AL.mult)
        tks["brm"] = kb.inc(KB.nop(v), "V")

    @blk.scalar
    def _(s):
        kb.wait(s, tks["brm"])
        kb.chain(s, "S")
        kb.inc(s.activation(L2R, LROW, AF.Exp, bias=BR[:, 3:4]), "S")
    tks["pexp"] = kb.tok("S")

    @blk.vector
    def _(v):
        kb.wait(v, tks["pexp"])
        for op in range(4):
            kb.chain(v, "V")
            if op == 0:
                v.tensor_tensor(PMR, L2R, MKR, AL.mult)
            elif op == 1:
                v.reduce_sum(BR[:, 2:3], PMR, axis=AX)
            elif op == 2:
                v.reciprocal(BR[:, 4:5], BR[:, 2:3])
            elif op == 3:
                v.tensor_scalar(PMR, PMR, BR[:, 4:5], None, AL.mult)
            kb.inc(KB.nop(v), "V")
        tks["wrow"] = kb.tok("V")

    @blk.gpsimd
    def _(g):
        kb.wait(g, tks["wrow"])
        tks["wcol"] = kb.dma(g, "TRX", WCOL,
                             PMR.rearrange("p (a c) -> p a c", a=8))

    @blk.tensor
    def _(t):
        kb.wait(t, tks["wcol"])
        kb.use_bank(t, 4)
        tks["w128"] = kb.inc(
            t.matmul(kb.banks[4].ap()[:, 0:1], EWc, WCOL, start=True, stop=True), "T")

    @blk.vector
    def _(v):
        kb.wait(v, tks["w128"])
        for p in "qkvo":
            kb.chain(v, "V")
            kb.inc(v.tensor_tensor(SCL[p], SRAW[p], kb.banks[4].ap()[:, 0:1],
                                   AL.mult), "V")
        kb.bank_free[4] = kb.tok("V")
        kb.chain(v, "V")
        v.memset(MAXSQ["q"], 0.0)
        kb.inc(KB.nop(v), "V")
        kb.chain(v, "V")
        v.memset(MAXSQ["k"], 0.0)
        kb.inc(KB.nop(v), "V")
        kb.chain(v, "V")
        v.memset(VT3.rearrange("p s (h d) -> p s h d", d=65)[:, :, :, 64:65], 1.0)
        kb.inc(KB.nop(v), "V")
    b1_done = kb.tok("V")
    kb.free("GRT")

    wfree = {}
    for p in ("q", "k", "v"):
        @blk.sync
        def _(s, p=p):
            if "vc" in wfree:
                kb.wait(s, wfree["vc"])
            tks["vc"] = kb.dma(s, "VC", VCST,
                               dr[p + "vcat_t"][0].rearrange("k p c -> p k c"))

        for nt in range(4):
            @blk.tensor
            def _(t, p=p, nt=nt):
                kb.wait(t, tks["vc"], b1_done)
                kb.use_bank(t, 0)
                for k in range(8):
                    mm = t.matmul(kb.banks[0].ap()[:, :], VCST[:, k, :],
                                  R1[:, k, 512 * nt : 512 * nt + 512],
                                  start=(k == 0), stop=(k == 7))
                tks["xv"] = kb.inc(mm, "T")
                if nt == 3:
                    wfree["vc"] = tks["xv"]

            @blk.vector
            def _(v, p=p, nt=nt):
                kb.wait(v, tks["xv"])
                kb.chain(v, "V")
                ev = v.tensor_scalar(XVS[:, 512 * nt : 512 * nt + 512],
                                     kb.banks[0].ap()[:, :], SCL[p], None, AL.mult)
                kb.bank_free[0] = kb.inc(ev, "V")
        xvs_done = kb.tok("V")

        for oc in range(8):
            wb = oc % 2

            @blk.sync
            def _(s, p=p, oc=oc, wb=wb):
                if ("wm", wb) in wfree:
                    kb.wait(s, wfree[("wm", wb)])
                tks["wm"] = kb.dma(s, "WM%d" % wb, WMST[:, wb],
                                   dr[p + "wm_t"][oc].rearrange("k p c -> p k c"))
                if ("uc", wb) in wfree:
                    kb.wait(s, wfree[("uc", wb)])
                tks["uc"] = kb.dma(s, "UC%d" % wb, UCST[:, wb],
                                   dr[p + "ucat_t"][oc][0])

            for nt in range(4):
                bo = 1 + nt % 2

                @blk.tensor
                def _(t, p=p, oc=oc, nt=nt, wb=wb, bo=bo, xvs_done=xvs_done):
                    kb.wait(t, tks["wm"], tks["uc"], xvs_done)
                    kb.use_bank(t, bo)
                    for k in range(8):
                        t.matmul(kb.banks[bo].ap()[:, :], WMST[:, wb, k, :],
                                 R1[:, k, 512 * nt : 512 * nt + 512],
                                 start=(k == 0), stop=False)
                    mm = t.matmul(kb.banks[bo].ap()[:, :], UCST[:, wb],
                                  XVS[:, 512 * nt : 512 * nt + 512],
                                  start=False, stop=True)
                    tks["pr"] = kb.inc(mm, "T")
                    if nt == 3:
                        wfree[("wm", wb)] = tks["pr"]
                        wfree[("uc", wb)] = tks["pr"]

                if p in ("q", "k"):
                    spname = {"q": "q0_sp", "k": "k0_sp"}[p]

                    @blk.vector
                    def _(v, p=p, oc=oc, nt=nt, bo=bo):
                        kb.wait(v, tks["pr"])
                        if ("qkb", nt % 2) in wfree:
                            kb.wait(v, wfree[("qkb", nt % 2)])
                        kb.chain(v, "V")
                        if p == "q":
                            ev = v.tensor_scalar(QKB[:, nt % 2],
                                                 kb.banks[bo].ap()[:, :],
                                                 PB[p][:, oc : oc + 1], 0.125,
                                                 AL.add, AL.mult)
                        else:
                            ev = v.tensor_scalar(QKB[:, nt % 2],
                                                 kb.banks[bo].ap()[:, :],
                                                 PB[p][:, oc : oc + 1], None, AL.add)
                        kb.bank_free[bo] = kb.inc(ev, "V")
                        tks["qkev"] = kb.bank_free[bo]

                    @blk.scalar
                    def _(s, nt=nt):
                        kb.wait(s, tks["qkev"])
                        if "sqb" in wfree:
                            kb.wait(s, wfree["sqb"])
                        kb.chain(s, "S")
                        kb.inc(s.activation(SQB, QKB[:, nt % 2], AF.Square), "S")
                        tks["sq"] = kb.tok("S")

                    @blk.tensor
                    def _(t):
                        kb.wait(t, tks["sq"])
                        kb.use_bank(t, 3)
                        mm = t.matmul(kb.banks[3].ap()[0:2, :], EHH, SQB,
                                      start=True, stop=True)
                        tks["ss"] = kb.inc(mm, "T")
                        wfree["sqb"] = tks["ss"]

                    @blk.vector
                    def _(v, p=p, oc=oc):
                        kb.wait(v, tks["ss"])
                        kb.chain(v, "V")
                        ev = v.tensor_copy(SQ4[0:2, 0:512], kb.banks[3].ap()[0:2, :])
                        kb.bank_free[3] = kb.inc(ev, "V")
                        kb.chain(v, "V")
                        v.reduce_max(MX8[0:2, 0:1], SQ4[0:2, 0:512], axis=AX)
                        kb.inc(KB.nop(v), "V")
                        kb.chain(v, "V")
                        v.tensor_tensor(MAXSQ[p][:, oc : oc + 1],
                                        MAXSQ[p][:, oc : oc + 1], MX8[0:2, 0:1],
                                        AL.max)
                        kb.inc(KB.nop(v), "V")

                    @blk.gpsimd
                    def _(g, oc=oc, nt=nt, spname=spname):
                        kb.wait(g, tks["qkev"])
                        wfree[("qkb", nt % 2)] = kb.dma(
                            g, "QS%d" % (nt % 2),
                            dr[spname][oc][:, 512 * nt : 512 * nt + 512],
                            QKB[:, nt % 2])
                else:
                    @blk.vector
                    def _(v, oc=oc, nt=nt, bo=bo):
                        kb.wait(v, tks["pr"])
                        if "vf" in wfree:
                            kb.wait(v, wfree["vf"])
                        kb.chain(v, "V")
                        ev = v.tensor_scalar(VF, kb.banks[bo].ap()[:, :],
                                             PB["v"][:, oc : oc + 1], None, AL.add)
                        kb.bank_free[bo] = kb.inc(ev, "V")
                        tks["vf"] = kb.bank_free[bo]

                    for sub in range(4):
                        @blk.tensor
                        def _(t, sub=sub):
                            kb.wait(t, tks["vf"])
                            kb.use_bank(t, 6)
                            tks["vt"] = kb.inc(
                                t.transpose(kb.banks[6].ap()[:, 0:128],
                                            VF[:, 128 * sub : 128 * sub + 128],
                                            C.ident), "T")

                        @blk.vector
                        def _(v, oc=oc, nt=nt, sub=sub):
                            kb.wait(v, tks["vt"])
                            tt = 4 * nt + sub
                            h2o = oc // 2
                            kb.chain(v, "V")
                            dst = VT3[:, tt, :].rearrange("c (h d) -> c h d", d=65)
                            ev = v.tensor_copy(
                                dst[:, 2 * oc : 2 * oc + 2, 0:64],
                                kb.banks[6].ap()[:, 0:128].rearrange(
                                    "p (h d) -> p h d", d=64))
                            kb.bank_free[6] = kb.inc(ev, "V")
                            if sub == 3:
                                wfree["vf"] = kb.bank_free[6]
        if p == "v":
            b2v_done = kb.tok("V")

    @blk.gpsimd
    def _(g):
        kb.wait(g, b2v_done)
        for c in range(8):
            tks["r1s"] = kb.dma(g, "RS%d" % (c % 2), dr["r1_sp"][c], R1[:, c, :])
    r1_spilled = (kb.tok("RS0"), kb.tok("RS1"))

    @blk.gpsimd
    def _(g):
        kb.wait(g, kb.tok("V"))
        for pp in ("q", "k"):
            for half in range(2):
                tks["mrow"] = kb.dma(
                    g, "MQ",
                    MROW[pp].rearrange("p (c h) -> p c h", c=8)[:, :, half],
                    MAXSQ[pp][half : half + 1, :], slow=True)

    @blk.scalar
    def _(s):
        kb.wait(s, tks["mrow"])
        kb.chain(s, "S")
        kb.inc(s.activation(MROW["q"], MROW["q"], AF.Sqrt), "S")
        kb.chain(s, "S")
        kb.inc(s.activation(MROW["k"], MROW["k"], AF.Sqrt), "S")
    tks["msq"] = kb.tok("S")

    @blk.vector
    def _(v):
        kb.wait(v, tks["msq"])
        kb.chain(v, "V")
        v.tensor_tensor(NMROW, MROW["q"], MROW["k"], AL.mult)
        kb.inc(KB.nop(v), "V")
        kb.chain(v, "V")
        v.tensor_scalar(NMROW, NMROW, -1.0, None, AL.mult)
        tks["nm"] = kb.inc(KB.nop(v), "V")

    @blk.tensor
    def _(t):
        kb.wait(t, tks["nm"])
        kb.use_bank(t, 0)
        tks["nmb"] = kb.inc(
            t.matmul(kb.banks[0].ap()[:, 0:16], C.ones_row, NMROW,
                     start=True, stop=True), "T")

    @blk.vector
    def _(v):
        kb.wait(v, tks["nmb"])
        kb.chain(v, "V")
        ev = v.tensor_copy(NEGMB, kb.banks[0].ap()[:, 0:16])
        kb.bank_free[0] = kb.inc(ev, "V")
    negmb_done = kb.tok("V")

    kb.free("R1", "XVS", "VCST", "QKB", "SQB")
    _phase_B34(nc, blk, kb, dr, C, dict(
        VT3=VT3, CTX_done=None, NEGMB=NEGMB, SCL=SCL, PB=PB,
        negmb_done=negmb_done, r1_spilled=r1_spilled,
        wm0=wfree.get(("wm", 0)), wm1=wfree.get(("wm", 1)),
        uc0=wfree.get(("uc", 0)), uc1=wfree.get(("uc", 1)),
        vc=wfree.get("vc")))
    kb.free("BSM", "BBIAS", "WMST", "UCST", "VT", "VF", "SQ4")


def _phase_B34(nc, blk, kb, dr, C, E):
    VT3, NEGMB, SCL, PB = E["VT3"], E["NEGMB"], E["SCL"], E["PB"]
    kb.region("CTXQ", 8 * 512)
    kb.region("KH", 2 * 2048)
    kb.region("QH", 2 * 512)
    kb.region("PT", 2 * 512)
    kb.region("BCV", 512)
    kb.region("XVO", 512)
    kb.region("OVC", 1024)
    kb.region("RES2", 8 * 512)
    kb.region("ROW3", 3 * 512)
    kb.region("SQ5", 512)
    kb.region("R1B", 2 * 512)
    CTXQ = kb.view("CTXQ").rearrange("p (c t) -> p c t", c=8)
    KH = kb.view("KH").rearrange("p (b t) -> p b t", b=2)[0:64]
    QH = kb.view("QH").rearrange("p (b t) -> p b t", b=2)[0:64]
    PT = kb.view("PT").rearrange("p (b t) -> p b t", b=2)
    BCV = kb.view("BCV")[0:64, :]
    XVO = kb.view("XVO")
    OVC = kb.view("OVC").rearrange("p (k c) -> p k c", k=8)
    RES2 = kb.view("RES2").rearrange("p (c t) -> p c t", c=8)
    ROW3 = kb.view("ROW3").rearrange("p (b t) -> p b t", b=3)
    SQ5 = kb.view("SQ5")
    R1B = kb.view("R1B").rearrange("p (b t) -> p b t", b=2)
    WMST = kb.view("WMST").rearrange("p (b k c) -> p b k c", b=2, k=8)
    UCST = kb.view("UCST").rearrange("p (b c) -> p b c", b=2)

    tks = {}
    wfree = {("owm", 0): E["wm0"], ("owm", 1): E["wm1"],
             ("ouc", 0): E["uc0"], ("ouc", 1): E["uc1"]}
    wfree = {k: v for k, v in wfree.items() if v is not None}
    kfree = {}

    b2t = kb.tok("T")
    b2s = kb.tok("S")
    b2qs = (kb.tok("QS0"), kb.tok("QS1"))

    @blk.sync
    def _(s):
        if E["vc"] is not None:
            kb.wait(s, E["vc"])
        kb.wait(s, b2t)
        tks["ovc"] = kb.dma(s, "VC", OVC,
                            dr["ovcat_t"][0].rearrange("k p c -> p k c"))

    for qt in range(4):
        qcol = slice(512 * qt, 512 * qt + 512)
        for h in range(16):
            hb = h % 2
            ih = 16 * qt + h

            @blk.gpsimd
            def _(g, qt=qt, h=h, hb=hb, ih=ih):
                if ih == 0:
                    kb.wait(g, b2t, b2s, *b2qs)
                if ("kh", hb) in kfree:
                    kb.wait(g, kfree[("kh", hb)])
                tks["kh"] = kb.dma(g, "KH%d" % hb, KH[:, hb, :],
                                   dr["k0_sp"][h // 2][64 * (h % 2) : 64 * (h % 2) + 64, :])
                if ("qh", hb) in kfree:
                    kb.wait(g, kfree[("qh", hb)])
                tks["qh"] = kb.dma(g, "QH%d" % hb, QH[:, hb, :],
                                   dr["q0_sp"][h // 2][64 * (h % 2) : 64 * (h % 2) + 64,
                                                      512 * qt : 512 * qt + 512])

            cbk = 2 + h % 2
            st_toks = {}

            def emit_S(kt, h=h, hb=hb, qt=qt):
                sb = kt % 2

                @blk.tensor
                def _(t):
                    kb.wait(t, tks["kh"], tks["qh"], E["negmb_done"])
                    kb.use_bank(t, sb)
                    mm = t.matmul(kb.banks[sb].ap()[:, :],
                                  KH[:, hb, 128 * kt : 128 * kt + 128],
                                  QH[:, hb, :], start=True, stop=True)
                    st_toks[kt] = kb.inc(mm, "T")
                    if kt >= 14:
                        kfree[("kh", hb)] = st_toks[kt]
                    if kt == 15:
                        kfree[("qh", hb)] = st_toks[kt]

            emit_S(0)
            for kt in range(16):
                sb = kt % 2
                if kt + 1 <= 15:
                    emit_S(kt + 1)

                @blk.scalar
                def _(s, h=h, kt=kt, sb=sb):
                    kb.wait(s, st_toks[kt])
                    if ("pt", sb) in kfree:
                        kb.wait(s, kfree[("pt", sb)])
                    kb.chain(s, "S")
                    ev = s.activation(PT[:, sb], kb.banks[sb].ap()[:, :], AF.Exp,
                                      bias=NEGMB[:, h : h + 1])
                    kb.bank_free[sb] = kb.inc(ev, "S")
                    tks["pt"] = kb.tok("S")

                @blk.tensor
                def _(t, h=h, kt=kt, sb=sb, cbk=cbk):
                    kb.wait(t, tks["pt"])
                    if kt == 0:
                        kb.use_bank(t, cbk)
                    mm = t.matmul(kb.banks[cbk].ap()[0:65, :],
                                  VT3[:, kt, 65 * h : 65 * h + 65], PT[:, sb],
                                  start=(kt == 0), stop=(kt == 15))
                    tks["ctx"] = kb.inc(mm, "T")
                    kfree[("pt", sb)] = tks["ctx"]

            @blk.vector
            def _(v, h=h, cbk=cbk):
                kb.wait(v, tks["ctx"])
                kb.chain(v, "V")
                v.reciprocal(ROW3[0:1, 2, :], kb.banks[cbk].ap()[64:65, :])
                tks["dr"] = kb.inc(KB.nop(v), "V")

            @blk.tensor
            def _(t, h=h):
                kb.wait(t, tks["dr"])
                kb.use_bank(t, 6)
                tks["bc"] = kb.inc(
                    t.matmul(kb.banks[6].ap()[0:64, :], C.ones_row[0:1, 0:64],
                             ROW3[0:1, 2, :], start=True, stop=True), "T")

            @blk.vector
            def _(v, qt=qt, h=h, cbk=cbk):
                kb.wait(v, tks["bc"])
                kb.chain(v, "V")
                ev = v.tensor_copy(BCV, kb.banks[6].ap()[0:64, :])
                kb.bank_free[6] = kb.inc(ev, "V")
                kb.chain(v, "V")
                if "ctxq_free" in wfree:
                    kb.wait(v, wfree["ctxq_free"])
                hp = 64 * (h % 2)
                ev = v.tensor_tensor(CTXQ[hp : hp + 64, h // 2, :],
                                     kb.banks[cbk].ap()[0:64, :], BCV, AL.mult)
                kb.bank_free[cbk] = kb.inc(ev, "V")
        ctx_done = kb.tok("V")

        @blk.tensor
        def _(t, qt=qt):
            kb.wait(t, tks["ovc"], ctx_done)
            kb.use_bank(t, 0)
            for k in range(8):
                mm = t.matmul(kb.banks[0].ap()[:, :], OVC[:, k, :], CTXQ[:, k, :],
                              start=(k == 0), stop=(k == 7))
            tks["xvo"] = kb.inc(mm, "T")

        @blk.vector
        def _(v):
            kb.wait(v, tks["xvo"])
            kb.chain(v, "V")
            ev = v.tensor_scalar(XVO, kb.banks[0].ap()[:, :], SCL["o"], None,
                                 AL.mult)
            kb.bank_free[0] = kb.inc(ev, "V")
        xvo_done = kb.tok("V")

        last_st = [None]
        for oc in range(8):
            wb = oc % 2
            bo = 4 + wb

            @blk.sync
            def _(s, qt=qt, oc=oc, wb=wb):
                if ("owm", wb) in wfree:
                    kb.wait(s, wfree[("owm", wb)])
                tks["owm"] = kb.dma(s, "WM%d" % wb, WMST[:, wb],
                                    dr["owm_t"][oc].rearrange("k p c -> p k c"))
                if ("ouc", wb) in wfree:
                    kb.wait(s, wfree[("ouc", wb)])
                tks["ouc"] = kb.dma(s, "UC%d" % wb, UCST[:, wb],
                                    dr["oucat_t"][oc][0])

            @blk.gpsimd
            def _(g, qt=qt, oc=oc):
                if ("r1b", oc % 2) in wfree:
                    kb.wait(g, wfree[("r1b", oc % 2)])
                kb.wait(g, *E["r1_spilled"])
                tks["r1b"] = kb.dma(g, "RB%d" % (oc % 2), R1B[:, oc % 2],
                                    dr["r1_sp"][oc][:, 512 * qt : 512 * qt + 512])

            @blk.tensor
            def _(t, qt=qt, oc=oc, wb=wb, bo=bo, xvo_done=xvo_done):
                kb.wait(t, tks["owm"], tks["ouc"], xvo_done)
                kb.use_bank(t, bo)
                for k in range(8):
                    t.matmul(kb.banks[bo].ap()[:, :], WMST[:, wb, k, :],
                             CTXQ[:, k, :], start=(k == 0), stop=False)
                mm = t.matmul(kb.banks[bo].ap()[:, :], UCST[:, wb], XVO,
                              start=False, stop=True)
                tks["opr"] = kb.inc(mm, "T")
                wfree[("owm", wb)] = tks["opr"]
                wfree[("ouc", wb)] = tks["opr"]
                if oc == 7:
                    wfree["ctxq_free"] = tks["opr"]

            @blk.vector
            def _(v, oc=oc, bo=bo):
                kb.wait(v, tks["opr"], tks["r1b"])
                if ("r2s", oc) in wfree:
                    kb.wait(v, wfree[("r2s", oc)])
                kb.chain(v, "V")
                ev = v.scalar_tensor_tensor(RES2[:, oc, :], kb.banks[bo].ap()[:, :],
                                            PB["o"][:, oc : oc + 1], R1B[:, oc % 2],
                                            AL.add, AL.add)
                kb.bank_free[bo] = kb.inc(ev, "V")
                wfree[("r1b", oc % 2)] = kb.bank_free[bo]
                kb.chain(v, "V")
                kb.wait(v, last_st[0])
                v.tensor_tensor(SQ5, RES2[:, oc, :], RES2[:, oc, :], AL.mult)
                tks["sq"] = kb.inc(KB.nop(v), "V")

            @blk.tensor
            def _(t, oc=oc):
                if oc == 0:
                    kb.use_bank(t, 6, 7)
                kb.wait(t, kb.tok("V"))
                t.matmul(kb.banks[6].ap()[0:1, :], C.ones_col, RES2[:, oc, :],
                         start=(oc == 0), stop=(oc == 7))
                mm = t.matmul(kb.banks[7].ap()[0:1, :], C.ones_col, SQ5,
                              start=(oc == 0), stop=(oc == 7))
                tks["lnst"] = last_st[0] = kb.inc(mm, "T")

        @blk.vector
        def _(v):
            kb.wait(v, tks["lnst"])
            kb.chain(v, "V")
            v.tensor_scalar(ROW3[0:1, 0, :], kb.banks[6].ap()[0:1, :],
                            1.0 / 1024.0, None, AL.mult)
            kb.bank_free[6] = kb.inc(KB.nop(v), "V")
            kb.chain(v, "V")
            v.tensor_scalar(ROW3[0:1, 1, :], kb.banks[7].ap()[0:1, :],
                            1.0 / 1024.0, None, AL.mult)
            kb.bank_free[7] = kb.inc(KB.nop(v), "V")
            kb.chain(v, "V")
            v.tensor_tensor(ROW3[0:1, 2, :], ROW3[0:1, 0, :], ROW3[0:1, 0, :],
                            AL.mult)
            kb.inc(KB.nop(v), "V")
            kb.chain(v, "V")
            v.tensor_tensor(ROW3[0:1, 1, :], ROW3[0:1, 1, :], ROW3[0:1, 2, :],
                            AL.subtract)
            tks["var"] = kb.inc(KB.nop(v), "V")

        @blk.scalar
        def _(s):
            kb.wait(s, tks["var"])
            kb.chain(s, "S")
            kb.inc(s.activation(ROW3[0:1, 2, :], ROW3[0:1, 1, :], AF.Sqrt,
                                bias=C.epscol[0:1, 0:1]), "S")
        s_sdb4 = kb.tok("S")

        @blk.vector
        def _(v):
            kb.wait(v, s_sdb4)
            kb.chain(v, "V")
            v.reciprocal(ROW3[0:1, 1, :], ROW3[0:1, 2, :])
            tks["rstd"] = kb.inc(KB.nop(v), "V")

        @blk.tensor
        def _(t):
            kb.wait(t, tks["rstd"])
            kb.use_bank(t, 6, 7)
            tks["bmu"] = kb.inc(
                t.matmul(kb.banks[6].ap()[:, :], C.ones_row, ROW3[0:1, 0, :],
                         start=True, stop=True), "T")
            tks["brs"] = kb.inc(
                t.matmul(kb.banks[7].ap()[:, :], C.ones_row, ROW3[0:1, 1, :],
                         start=True, stop=True), "T")

        for oc in range(8):
            @blk.vector
            def _(v, qt=qt, oc=oc):
                kb.wait(v, tks["bmu"], tks["brs"])
                if ("r2s", oc) in wfree:
                    kb.wait(v, wfree[("r2s", oc)])
                kb.chain(v, "V")
                v.tensor_tensor(RES2[:, oc, :], RES2[:, oc, :],
                                kb.banks[6].ap()[:, :], AL.subtract)
                kb.inc(KB.nop(v), "V")
                kb.chain(v, "V")
                v.tensor_tensor(RES2[:, oc, :], RES2[:, oc, :],
                                kb.banks[7].ap()[:, :], AL.mult)
                tks["r2"] = kb.inc(KB.nop(v), "V")

            @blk.gpsimd
            def _(g, qt=qt, oc=oc):
                kb.wait(g, tks["r2"])
                wfree[("r2s", oc)] = kb.dma(
                    g, "R2%d" % (oc % 2), dr["r2_sp"][oc][:, 512 * qt : 512 * qt + 512],
                    RES2[:, oc, :])
        kb.bank_free[6] = kb.tok("V")
        kb.bank_free[7] = kb.tok("V")
    r2_done = [kb.tok("R20"), kb.tok("R21")]
    kb._r2_done = r2_done

    kb.free("CTXQ", "KH", "QH", "PT", "BCV", "XVO", "OVC", "RES2", "ROW3",
            "SQ5", "R1B")




def _phase_C(nc, blk, kb, dr, C):
    kb.region("CSM", 128)
    kb.region("VT", 16 * 1040)
    kb.region("R2N", 2 * 8 * 512)
    kb.region("WVM", 8 * 2 * 512)
    kb.region("WQK", 2 * 1024)
    kb.region("QKE", 2 * 512)
    kb.region("CP1", 136)
    kb.region("SQP", 2 * 512)
    kb.region("SDR", 2 * 512)

    big = kb.big.ap()
    VT3 = kb.view("VT").rearrange("p (s c) -> p s c", c=1040)
    R2N = kb.view("R2N").rearrange("p (b k t) -> p b k t", b=2, k=8)
    WVM = kb.view("WVM").rearrange("p (k o t) -> p k o t", k=8, o=2)
    WQK = kb.view("WQK").rearrange("p (b k c) -> p b k c", b=2, k=8)
    QKE = kb.view("QKE").rearrange("p (b t) -> p b t", b=2)
    cs = kb.regions["CSM"][0]
    UNT = big[:, cs : cs + 16]
    U16B = big[0:16, cs + 16 : cs + 80]
    UN1 = big[0:16, cs + 80 : cs + 81]
    OMEF = big[0:1, cs + 81 : cs + 97]
    NMC = big[:, cs + 97 : cs + 98]
    QWS = big[0:64, cs + 98 : cs + 99]
    KWS = big[0:64, cs + 99 : cs + 100]
    WROW = big[0:1, cs + 100 : cs + 116]
    cp = kb.regions["CP1"][0]
    EHHC = big[:, cp : cp + 2]
    E2TC = big[0:2, cp + 2 : cp + 130]
    QW2 = big[:, cp + 130 : cp + 131]
    KW2 = big[:, cp + 131 : cp + 132]
    SQP = kb.view("SQP").rearrange("p (b t) -> p b t", b=2)
    SDR = kb.view("SDR").rearrange("p (b t) -> p b t", b=2)[0:2]
    tks = {}
    wfree = {}

    @blk.sync
    def _(s):
        kb.wait(s, *kb._r2_done)
        for k in range(8):
            tks["wv"] = kb.dma(s, "LC0", WVM[:, k],
                               dr["wv_m"][k].rearrange("o p c -> p o c"))
        kb.dma(s, "LC1", EHHC, dr["c_ehh"])
        kb.dma(s, "LC0", E2TC, dr["c_e2t"])
        kb.dma(s, "LC1", QW2, dr["qrw2_col"])
        kb.dma(s, "LC0", KW2, dr["krw2_col"])
    wv_done = tks["wv"]
    cp1_done = (kb.tok("LC0"), kb.tok("LC1"))

    tk2 = {}
    emitted = set()

    def _qst(gi):
        nt, rem = divmod(gi, 16)
        pj, oc = divmod(rem, 8)
        return (nt, pj, oc, gi % 2,
                "wq_t" if pj == 0 else "wk_t",
                "q1_sp" if pj == 0 else "k1_sp")

    def emit_wload(gi):
        nt, pj, oc, wb, wt, sp = _qst(gi)

        @blk.sync
        def _(s, gi=gi, oc=oc, wb=wb, wt=wt):
            if ("wqk", wb) in wfree:
                kb.wait(s, wfree[("wqk", wb)])
            tk2[("w", gi)] = kb.dma(s, "WM%d" % wb, WQK[:, wb],
                                    dr[wt][oc].rearrange("k p c -> p k c"))

    def emit_proj(gi, r2n_tok):
        nt, pj, oc, wb, wt, sp = _qst(gi)
        nb = nt % 2

        @blk.tensor
        def _(t, gi=gi, nb=nb, wb=wb, pj=pj, oc=oc, r2n_tok=r2n_tok):
            kb.wait(t, tk2[("w", gi)], r2n_tok)
            kb.use_bank(t, wb)
            for k in range(8):
                mm = t.matmul(kb.banks[wb].ap()[:, :], WQK[:, wb, k, :],
                              R2N[:, nb, k, :], start=(k == 0), stop=(k == 7))
            tk2[("pr", gi)] = kb.inc(mm, "T")
            wfree[("wqk", wb)] = tk2[("pr", gi)]
            if pj == 1 and oc == 7:
                wfree[("r2n_qk", nb)] = tk2[("pr", gi)]

    def emit_square(gi):
        wb = gi % 2

        @blk.scalar
        def _(s, gi=gi, wb=wb):
            kb.wait(s, tk2[("pr", gi)])
            if ("sqp", wb) in wfree:
                kb.wait(s, wfree[("sqp", wb)])
            kb.chain(s, "S")
            kb.inc(s.activation(SQP[:, wb], kb.banks[wb].ap()[:, :], AF.Square),
                   "S")
            tk2[("sqp", gi)] = kb.tok("S")

    def emit_stat(gi):
        if ("sqp", gi) not in tk2 or ("stat", gi) in emitted:
            return
        emitted.add(("stat", gi))
        wb = gi % 2

        @blk.tensor
        def _(t, gi=gi, wb=wb):
            kb.wait(t, tk2[("sqp", gi)])
            if gi < 2:
                kb.wait(t, *cp1_done)
            kb.use_bank(t, 6)
            mm = t.matmul(kb.banks[6].ap()[0:2, :], EHHC, SQP[:, wb],
                          start=True, stop=True)
            tk2[("stat", gi)] = kb.inc(mm, "T")
            wfree[("sqp", wb)] = tk2[("stat", gi)]

    def emit_sqrt(gi):
        if ("stat", gi) not in tk2 or ("sqrt", gi) in emitted:
            return
        emitted.add(("sqrt", gi))
        wb = gi % 2

        @blk.scalar
        def _(s, gi=gi, wb=wb):
            kb.wait(s, tk2[("stat", gi)])
            if ("sdr", wb) in wfree:
                kb.wait(s, wfree[("sdr", wb)])
            kb.chain(s, "S")
            ev = s.activation(SDR[:, wb], kb.banks[6].ap()[0:2, :], AF.Sqrt,
                              bias=C.epscol[0:2, 0:1], scale=1.0 / 64.0)
            kb.bank_free[6] = kb.inc(ev, "S")
            tk2[("sd", gi)] = kb.tok("S")

    def emit_recip(gi):
        if ("sd", gi) not in tk2 or ("rcp", gi) in emitted:
            return
        emitted.add(("rcp", gi))
        wb = gi % 2

        @blk.vector
        def _(v, gi=gi, wb=wb):
            kb.wait(v, tk2[("sd", gi)])
            kb.chain(v, "V")
            v.reciprocal(SDR[:, wb], SDR[:, wb])
            tk2[("rcp", gi)] = kb.inc(KB.nop(v), "V")

    def emit_bcast(gi):
        if ("rcp", gi) not in tk2 or ("bc", gi) in emitted:
            return
        emitted.add(("bc", gi))
        wb = gi % 2

        @blk.tensor
        def _(t, gi=gi, wb=wb):
            kb.wait(t, tk2[("rcp", gi)])
            if gi < 1:
                kb.wait(t, *cp1_done)
            kb.use_bank(t, 7)
            mm = t.matmul(kb.banks[7].ap()[:, :], E2TC, SDR[:, wb],
                          start=True, stop=True)
            tk2[("rbc", gi)] = kb.inc(mm, "T")
            wfree[("sdr", wb)] = tk2[("rbc", gi)]

    def emit_evacA(gi):
        if ("sqp", gi) not in tk2 or ("evA", gi) in emitted:
            return
        emitted.add(("evA", gi))
        nt, pj, oc, wb, wt, sp = _qst(gi)

        @blk.vector
        def _(v, gi=gi, wb=wb, pj=pj):
            kb.wait(v, tk2[("sqp", gi)])
            if gi < 2:
                kb.wait(v, *cp1_done)
            if ("qke", wb) in wfree:
                kb.wait(v, wfree[("qke", wb)])
            kb.chain(v, "V")
            if pj == 0:
                ev = v.tensor_scalar(QKE[:, wb], kb.banks[wb].ap()[:, :],
                                     QW2, 0.125, AL.mult, AL.mult)
            else:
                ev = v.tensor_scalar(QKE[:, wb], kb.banks[wb].ap()[:, :],
                                     KW2, None, AL.mult)
            kb.bank_free[wb] = kb.inc(ev, "V")

    def emit_evacB(gi):
        if ("rbc", gi) not in tk2 or ("evB", gi) in emitted:
            return
        emitted.add(("evB", gi))
        wb = gi % 2

        @blk.vector
        def _(v, gi=gi, wb=wb):
            kb.wait(v, tk2[("rbc", gi)])
            kb.chain(v, "V")
            ev = v.tensor_tensor(QKE[:, wb], QKE[:, wb],
                                 kb.banks[7].ap()[:, :], AL.mult)
            kb.bank_free[7] = kb.inc(ev, "V")
            tk2[("evB", gi)] = kb.bank_free[7]

    def emit_spill(gi):
        if ("evB", gi) not in tk2 or ("sp", gi) in emitted:
            return
        emitted.add(("sp", gi))
        nt, pj, oc, wb, wt, sp = _qst(gi)

        @blk.gpsimd
        def _(g, gi=gi, nt=nt, oc=oc, wb=wb, sp=sp):
            kb.wait(g, tk2[("evB", gi)])
            wfree[("qke", wb)] = kb.dma(
                g, "QS%d" % wb, dr[sp][oc][:, 512 * nt : 512 * nt + 512],
                QKE[:, wb])

    for nt in range(4):
        nb = nt % 2

        @blk.gpsimd
        def _(g, nt=nt, nb=nb):
            kb.wait(g, *kb._r2_done)
            if ("r2n", nb) in wfree:
                kb.wait(g, wfree[("r2n", nb)])
            tks["r2n"] = kb.dma(g, "RN%d" % nb, R2N[:, nb],
                                dr["r2_sp"][:, :, 512 * nt : 512 * nt + 512]
                                .rearrange("k p t -> p k t"))
        r2n_tok = tks["r2n"]

        for local in range(16):
            gi = nt * 16 + local
            emit_wload(gi)
            emit_stat(gi - 2)
            emit_bcast(gi - 3)
            emit_proj(gi, r2n_tok)
            emit_sqrt(gi - 2)
            emit_square(gi)
            emit_recip(gi - 2)
            emit_evacB(gi - 3)
            emit_spill(gi - 3)
            emit_evacA(gi - 1)
        for d in range(3):
            gi = nt * 16 + 16 + d
            emit_stat(gi - 2)
            emit_bcast(gi - 3)
            emit_sqrt(gi - 2)
            emit_recip(gi - 2)
            emit_evacB(gi - 3)
            emit_spill(gi - 3)
            emit_evacA(gi - 1)

        for sub in range(4):
            tt = 4 * nt + sub
            for oc2 in range(2):
                bo = 2 + oc2

                @blk.tensor
                def _(t, nb=nb, sub=sub, oc2=oc2, bo=bo, r2n_tok=r2n_tok, wv_done=wv_done):
                    kb.wait(t, wv_done, r2n_tok)
                    kb.use_bank(t, bo)
                    for k in range(8):
                        mm = t.matmul(kb.banks[bo].ap()[:, :],
                                      R2N[:, nb, k, 128 * sub : 128 * sub + 128],
                                      WVM[:, k, oc2], start=(k == 0), stop=(k == 7))
                    tks["v"] = kb.inc(mm, "T")
                    if sub == 3 and oc2 == 1:
                        wfree[("r2n_v", nb)] = tks["v"]

                @blk.vector
                def _(v, tt=tt, oc2=oc2, bo=bo):
                    kb.wait(v, tks["v"])
                    kb.chain(v, "V")
                    dst = VT3[:, tt, 520 * oc2 : 520 * oc2 + 520].rearrange(
                        "c (h d) -> c h d", d=65)
                    ev = v.tensor_copy(dst[:, :, 0:64],
                                       kb.banks[bo].ap()[:, :].rearrange(
                                           "p (h d) -> p h d", d=64))
                    kb.bank_free[bo] = kb.inc(ev, "V")

        def _mk(nb=nb):
            a = wfree.get(("r2n_qk", nb))
            b = wfree.get(("r2n_v", nb))
            wfree[("r2n", nb)] = b if (a is None or (b and b[1] > a[1])) else a
        _mk()

    @blk.vector
    def _(v):
        kb.chain(v, "V")
        v.memset(VT3.rearrange("p s (h d) -> p s h d", d=65)[:, :, :, 64:65], 1.0)
        kb.inc(KB.nop(v), "V")
    p1_done = kb.tok("V")

    @blk.sync
    def _(s):
        kb.dma(s, "LC1", U16B, dr["u16"])
        kb.dma(s, "LC0", OMEF, dr["omega_row"])
        kb.dma(s, "LC1", WROW[0:1, 0:16], dr["slopes_row"])
        kb.dma(s, "LC0", QWS, dr["qrw_col"])
        tks["c2"] = kb.dma(s, "LC1", KWS, dr["krw_col"])
    c2 = (kb.tok("LC0"), kb.tok("LC1"))

    kb.region("U2S", 64)
    U2S = kb.view("U2S")[0:16, :]

    @blk.vector
    def _(v):
        kb.wait(v, *c2)
        kb.chain(v, "V")
        v.tensor_tensor(U2S, U16B, U16B, AL.mult)
        kb.inc(KB.nop(v), "V")
        kb.chain(v, "V")
        v.reduce_sum(UN1, U2S, axis=AX)
        tks["un1"] = kb.inc(KB.nop(v), "V")

    @blk.scalar
    def _(s):
        kb.wait(s, tks["un1"])
        kb.chain(s, "S")
        kb.inc(s.activation(UN1, UN1, AF.Sqrt), "S")
    s_un = kb.tok("S")

    @blk.vector
    def _(v):
        kb.wait(v, s_un)
        kb.chain(v, "V")
        v.tensor_scalar(UN1, UN1, 1e-6, None, AL.max)
        kb.inc(KB.nop(v), "V")
        kb.chain(v, "V")
        v.reciprocal(UN1, UN1)
        kb.inc(KB.nop(v), "V")
        kb.chain(v, "V")
        v.tensor_scalar(U16B, U16B, UN1, None, AL.mult)
        tks["un"] = kb.inc(KB.nop(v), "V")

    @blk.gpsimd
    def _(g):
        kb.wait(g, tks["un"])
        for hh in range(16):
            tks["unt"] = kb.dma(g, "TRX", UNT[0:64, hh : hh + 1],
                                U16B[hh : hh + 1, :]
                                .rearrange("p (a c) -> p a c", a=64), slow=True)

    @blk.scalar
    def _(s):
        kb.wait(s, *c2)
        kb.chain(s, "S")
        kb.inc(s.activation(OMEF, OMEF, AF.Exp), "S")
        kb.chain(s, "S")
        kb.inc(s.activation(OMEF, OMEF, AF.Ln, bias=1.0), "S")
    s_om = kb.tok("S")

    @blk.vector
    def _(v):
        kb.wait(v, s_om)
        kb.chain(v, "V")
        v.tensor_tensor(OMEF, OMEF, WROW[0:1, 0:16], AL.mult)
        kb.inc(KB.nop(v), "V")
        tks["om"] = kb.tok("V")

    kb.region("CR2", 192)
    CR2 = kb.view("CR2")[0:1, :]

    @blk.gpsimd
    def _(g):
        kb.wait(g, *c2)
        kb.dma(g, "TRX", CR2[:, 0:64].rearrange("p (a c) -> p a c", a=64), QWS,
               slow=True)
        tks["wr2"] = kb.dma(g, "TRX",
                            CR2[:, 64:128].rearrange("p (a c) -> p a c", a=64),
                            KWS, slow=True)

    @blk.vector
    def _(v):
        kb.wait(v, tks["wr2"])
        kb.chain(v, "V")
        v.reduce_max(CR2[:, 128:129], CR2[:, 0:64], axis=AX,
                     apply_absolute_value=True)
        kb.inc(KB.nop(v), "V")
        kb.chain(v, "V")
        v.reduce_max(CR2[:, 129:130], CR2[:, 64:128], axis=AX,
                     apply_absolute_value=True)
        kb.inc(KB.nop(v), "V")
        kb.chain(v, "V")
        v.tensor_tensor(CR2[:, 130:131], CR2[:, 128:129], CR2[:, 129:130], AL.mult)
        kb.inc(KB.nop(v), "V")
        kb.chain(v, "V")
        v.tensor_scalar(CR2[:, 130:131], CR2[:, 130:131], -8.0, None, AL.mult)
        kb.inc(KB.nop(v), "V")
        tks["negm"] = kb.tok("V")

    @blk.tensor
    def _(t):
        kb.wait(t, tks["negm"])
        kb.use_bank(t, 3)
        tks["nmb"] = kb.inc(
            t.matmul(kb.banks[3].ap()[:, 0:1], C.ones_row, CR2[:, 130:131],
                     start=True, stop=True), "T")

    @blk.vector
    def _(v):
        kb.wait(v, tks["nmb"])
        kb.chain(v, "V")
        ev = v.tensor_copy(NMC, kb.banks[3].ap()[:, 0:1])
        kb.bank_free[3] = kb.inc(ev, "V")
    p2_done = kb.tok("V")

    kb.free("R2N", "WVM", "WQK", "QKE", "U2S")
    kb.free("CR2", "CP1", "SQP", "SDR")

    kb.region("CTX1", 8 * T)
    kb.region("QSC", 2 * T)
    kb.region("KSC", 2 * T)
    kb.region("PT2", 2 * 128)
    kb.region("BC2", 128)
    kb.region("GRW", 3 * T)
    QSC = kb.view("QSC").rearrange("p (b t) -> p b t", b=2)
    KSC = kb.view("KSC").rearrange("p (b t) -> p b t", b=2)
    CTX1 = kb.view("CTX1").rearrange("p (c t) -> p c t", c=8)
    PT2 = kb.view("PT2").rearrange("p (b t) -> p b t", b=2)
    BC2 = kb.view("BC2")[0:64, :]
    GRW = kb.view("GRW").rearrange("p (b t) -> p b t", b=3)[0:1]

    p3guard = tuple(kb.tok(n) for n in
                    ("QS0", "QS1", "WM0", "WM1", "RN0", "RN1", "LC0", "LC1",
                     "T", "V", "S", "TRX", "KE0", "KE1"))

    @blk.sync
    def _(s):
        kb.wait(s, *p3guard)
        for b in range(2):
            tks["q2"] = kb.dma(s, "LC0", QSC[64:66, b, :], dr["c_q2row"])
    q2_done = tks["q2"]

    kfree = {}
    for h in range(16):
        hb = h % 2
        hp = 64 * (h % 2)
        cb = h // 2

        @blk.gpsimd
        def _(g, h=h, hb=hb, hp=hp, cb=cb):
            kb.wait(g, p1_done, p2_done, q2_done)
            if h == 0:
                kb.wait(g, *p3guard)
            if ("qsc", hb) in kfree:
                kb.wait(g, kfree[("qsc", hb)])
            tks["scq"] = kb.dma(g, "KH%d" % hb, QSC[0:64, hb, :],
                                dr["q1_sp"][cb][hp : hp + 64, :])
            tks["sc"] = kb.dma(g, "QH%d" % hb, KSC[0:64, hb, :],
                               dr["k1_sp"][cb][hp : hp + 64, :])
        sc_tok = tks["sc"]
        scq_tok = tks["scq"]

        tks["rmsd"] = sc_tok

        for tl in range(4):
            tcol = slice(512 * tl, 512 * tl + 512)
            gbk = 4 + tl % 2

            @blk.tensor
            def _(t, h=h, hb=hb, tcol=tcol, gbk=gbk):
                kb.wait(t, tks["rmsd"], scq_tok, tks["unt"])
                kb.use_bank(t, gbk)
                mm = t.matmul(kb.banks[gbk].ap()[0:1, :], UNT[0:64, h : h + 1],
                              KSC[0:64, hb, tcol], start=True, stop=True)
                tks["gl"] = kb.inc(mm, "T")

            @blk.scalar
            def _(s, tl=tl, tcol=tcol, gbk=gbk):
                kb.wait(s, tks["gl"])
                kb.chain(s, "S")
                ev = s.activation(GRW[:, 0, tcol], kb.banks[gbk].ap()[0:1, :],
                                  AF.Exp, scale=0.125)
                kb.bank_free[gbk] = kb.inc(ev, "S")

        @blk.scalar
        def _(s):
            kb.chain(s, "S")
            kb.inc(s.activation(GRW[:, 0, :], GRW[:, 0, :], AF.Ln, bias=1.0), "S")
        s_gate = kb.tok("S")

        @blk.vector
        def _(v, h=h):
            kb.wait(v, s_gate, tks["om"])
            kb.chain(v, "V")
            v.scalar_tensor_tensor(GRW[:, 1, :], GRW[:, 0, :],
                                   OMEF[0:1, h : h + 1], C.qpos_row,
                                   AL.mult, AL.mult)
            kb.inc(KB.nop(v), "V")
            kb.chain(v, "V")
            v.tensor_scalar(GRW[:, 2, :], GRW[:, 0, :], OMEF[0:1, h : h + 1],
                            -1.0, AL.mult, AL.mult)
            tks["bg"] = kb.inc(KB.nop(v), "V")

        @blk.gpsimd
        def _(g, hb=hb):
            kb.wait(g, tks["bg"])
            kb.dma(g, "KE%d" % hb, KSC[64:65, hb, :], GRW[:, 1, :])
            tks["ke"] = kb.dma(g, "KE%d" % hb, KSC[65:66, hb, :], GRW[:, 2, :])

        for qt in range(16):
            qcol = slice(128 * qt, 128 * qt + 128)
            cbk = 2 + qt % 2
            st_toks = {}

            def emit_S(kt, qt=qt, h=h, hb=hb, qcol=qcol):
                sb = kt % 2

                @blk.tensor
                def _(t):
                    kb.wait(t, tks["ke"])
                    kb.use_bank(t, sb)
                    mm = t.matmul(kb.banks[sb].ap()[:, 0:128],
                                  KSC[0:66, hb, 128 * kt : 128 * kt + 128],
                                  QSC[0:66, hb, qcol], start=True, stop=True)
                    st_toks[kt] = kb.inc(mm, "T")
                    if kt >= qt - 1:
                        kfree[("sc_part", hb)] = st_toks[kt]

            emit_S(0)
            for kt in range(qt + 1):
                sb = kt % 2
                if kt + 1 <= qt:
                    emit_S(kt + 1)
                if kt == qt:
                    @blk.vector
                    def _(v, kt=kt, sb=sb):
                        kb.wait(v, st_toks[kt])
                        kb.chain(v, "V")
                        ev = v.tensor_tensor(kb.banks[sb].ap()[:, 0:128],
                                             kb.banks[sb].ap()[:, 0:128],
                                             C.negbig, AL.add)
                        st_toks[kt] = kb.inc(ev, "V")

                @blk.scalar
                def _(s, kt=kt, sb=sb, h=h):
                    kb.wait(s, st_toks[kt])
                    if ("pt2", sb) in kfree:
                        kb.wait(s, kfree[("pt2", sb)])
                    kb.chain(s, "S")
                    ev = s.activation(PT2[:, sb], kb.banks[sb].ap()[:, 0:128],
                                      AF.Exp, bias=NMC)
                    kb.bank_free[sb] = kb.inc(ev, "S")
                    tks["pt"] = kb.tok("S")

                @blk.tensor
                def _(t, h=h, kt=kt, qt=qt, sb=sb, cbk=cbk):
                    kb.wait(t, tks["pt"])
                    if kt == 0:
                        kb.use_bank(t, cbk)
                    mm = t.matmul(kb.banks[cbk].ap()[0:65, 0:128],
                                  VT3[:, kt, 65 * h : 65 * h + 65], PT2[:, sb],
                                  start=(kt == 0), stop=(kt == qt))
                    tks["ctx"] = kb.inc(mm, "T")
                    kfree[("pt2", sb)] = tks["ctx"]

            @blk.vector
            def _(v, cbk=cbk, qt=qt):
                kb.wait(v, tks["ctx"])
                kb.chain(v, "V")
                v.reciprocal(GRW[:, 2, 0:128], kb.banks[cbk].ap()[64:65, 0:128])
                tks["dr"] = kb.inc(KB.nop(v), "V")

            @blk.tensor
            def _(t, qt=qt):
                kb.wait(t, tks["dr"])
                kb.use_bank(t, 4 + qt % 2)
                tks["bc"] = kb.inc(
                    t.matmul(kb.banks[4 + qt % 2].ap()[0:64, 0:128],
                             C.ones_row[0:1, 0:64],
                             GRW[:, 2, 0:128], start=True, stop=True), "T")

            @blk.vector
            def _(v, h=h, hp=hp, cb=cb, qcol=qcol, cbk=cbk, qt=qt):
                kb.wait(v, tks["bc"])
                kb.chain(v, "V")
                ev = v.tensor_copy(BC2[:, 0:128],
                                   kb.banks[4 + qt % 2].ap()[0:64, 0:128])
                kb.bank_free[4 + qt % 2] = kb.inc(ev, "V")
                kb.chain(v, "V")
                ev = v.tensor_tensor(CTX1[hp : hp + 64, cb, qcol],
                                     kb.banks[cbk].ap()[0:64, 0:128], BC2[:, 0:128],
                                     AL.mult)
                kb.bank_free[cbk] = kb.inc(ev, "V")
        kfree[("qsc", hb)] = kfree[("sc_part", hb)]
    p3_done = kb.tok("V")
    kb.free("VT", "QSC", "KSC", "PT2", "BC2")

    kb.region("R3", 8 * T)
    kb.region("WPJ", 2 * 1024)
    kb.region("R2B", 2 * 512)
    kb.region("RES3", 8 * 512)
    kb.region("SQ7", 512)
    R3 = kb.view("R3").rearrange("p (c t) -> p c t", c=8)
    WPJ = kb.view("WPJ").rearrange("p (b k c) -> p b k c", b=2, k=8)
    R2B = kb.view("R2B").rearrange("p (b t) -> p b t", b=2)
    RES3 = kb.view("RES3").rearrange("p (c t) -> p c t", c=8)
    SQ7 = kb.view("SQ7")
    GRW2 = kb.view("GRW").rearrange("p (b t) -> p b t", b=3)[0:1]
    p3t = kb.tok("T")
    p3s = kb.tok("S")

    for nt in range(4):
        ncol = slice(512 * nt, 512 * nt + 512)
        last_st = [None]
        for m in range(8):
            wb = m % 2
            bo = 4 + wb

            @blk.sync
            def _(s, m=m, wb=wb, nt=nt):
                if ("wpj", wb) in wfree:
                    kb.wait(s, wfree[("wpj", wb)])
                kb.wait(s, p3t)
                tks["wp"] = kb.dma(s, "WM%d" % wb, WPJ[:, wb],
                                   dr["wproj_t"][m].rearrange("k p c -> p k c"))

            @blk.gpsimd
            def _(g, m=m, nt=nt):
                if ("r2b", m % 2) in wfree:
                    kb.wait(g, wfree[("r2b", m % 2)])
                kb.wait(g, p3t, p3s)
                tks["r2b"] = kb.dma(g, "RB%d" % (m % 2), R2B[:, m % 2],
                                    dr["r2_sp"][m][:, 512 * nt : 512 * nt + 512])

            @blk.tensor
            def _(t, m=m, nt=nt, wb=wb, bo=bo, p3_done=p3_done):
                kb.wait(t, tks["wp"], p3_done)
                kb.use_bank(t, bo)
                for k in range(8):
                    mm = t.matmul(kb.banks[bo].ap()[:, :], WPJ[:, wb, k, :],
                                  CTX1[:, k, ncol], start=(k == 0), stop=(k == 7))
                tks["opj"] = kb.inc(mm, "T")
                wfree[("wpj", wb)] = tks["opj"]

            @blk.vector
            def _(v, m=m, bo=bo):
                kb.wait(v, tks["opj"], tks["r2b"])
                if ("res3", m) in wfree:
                    kb.wait(v, wfree[("res3", m)])
                kb.chain(v, "V")
                ev = v.tensor_tensor(RES3[:, m, :], kb.banks[bo].ap()[:, :],
                                     R2B[:, m % 2], AL.add)
                kb.bank_free[bo] = kb.inc(ev, "V")
                wfree[("r2b", m % 2)] = kb.bank_free[bo]
                kb.chain(v, "V")
                kb.wait(v, last_st[0])
                v.tensor_tensor(SQ7, RES3[:, m, :], RES3[:, m, :], AL.mult)
                tks["sq"] = kb.inc(KB.nop(v), "V")

            @blk.tensor
            def _(t, m=m):
                if m == 0:
                    kb.use_bank(t, 6, 7)
                kb.wait(t, kb.tok("V"))
                t.matmul(kb.banks[6].ap()[0:1, :], C.ones_col, RES3[:, m, :],
                         start=(m == 0), stop=(m == 7))
                mm = t.matmul(kb.banks[7].ap()[0:1, :], C.ones_col, SQ7,
                              start=(m == 0), stop=(m == 7))
                tks["lnst"] = last_st[0] = kb.inc(mm, "T")

        @blk.vector
        def _(v):
            kb.wait(v, tks["lnst"])
            kb.chain(v, "V")
            v.tensor_scalar(GRW2[:, 0, 0:512], kb.banks[6].ap()[0:1, :],
                            1.0 / 1024.0, None, AL.mult)
            kb.bank_free[6] = kb.inc(KB.nop(v), "V")
            kb.chain(v, "V")
            v.tensor_scalar(GRW2[:, 1, 0:512], kb.banks[7].ap()[0:1, :],
                            1.0 / 1024.0, None, AL.mult)
            kb.bank_free[7] = kb.inc(KB.nop(v), "V")
            kb.chain(v, "V")
            v.tensor_tensor(GRW2[:, 2, 0:512], GRW2[:, 0, 0:512],
                            GRW2[:, 0, 0:512], AL.mult)
            kb.inc(KB.nop(v), "V")
            kb.chain(v, "V")
            v.tensor_tensor(GRW2[:, 1, 0:512], GRW2[:, 1, 0:512],
                            GRW2[:, 2, 0:512], AL.subtract)
            tks["var"] = kb.inc(KB.nop(v), "V")

        @blk.scalar
        def _(s):
            kb.wait(s, tks["var"])
            kb.chain(s, "S")
            kb.inc(s.activation(GRW2[:, 2, 0:512], GRW2[:, 1, 0:512], AF.Sqrt,
                                bias=C.epscol[0:1, 0:1]), "S")
        s_sdc4 = kb.tok("S")

        @blk.vector
        def _(v):
            kb.wait(v, s_sdc4)
            kb.chain(v, "V")
            v.reciprocal(GRW2[:, 1, 0:512], GRW2[:, 2, 0:512])
            tks["rstd"] = kb.inc(KB.nop(v), "V")

        @blk.tensor
        def _(t):
            kb.wait(t, tks["rstd"])
            kb.use_bank(t, 6, 7)
            tks["bmu"] = kb.inc(
                t.matmul(kb.banks[6].ap()[:, :], C.ones_row, GRW2[:, 0, 0:512],
                         start=True, stop=True), "T")
            tks["brs"] = kb.inc(
                t.matmul(kb.banks[7].ap()[:, :], C.ones_row, GRW2[:, 1, 0:512],
                         start=True, stop=True), "T")

        for m in range(8):
            @blk.vector
            def _(v, nt=nt, m=m):
                kb.wait(v, tks["bmu"], tks["brs"])
                kb.chain(v, "V")
                v.tensor_tensor(R3[:, m, ncol], RES3[:, m, :],
                                kb.banks[6].ap()[:, :], AL.subtract)
                kb.inc(KB.nop(v), "V")
                kb.chain(v, "V")
                v.tensor_tensor(R3[:, m, ncol], R3[:, m, ncol],
                                kb.banks[7].ap()[:, :], AL.mult)
                ev = kb.inc(KB.nop(v), "V")
                wfree[("res3", m)] = ev
        kb.bank_free[6] = kb.tok("V")
        kb.bank_free[7] = kb.tok("V")

    globals()["_DBG_C_REGIONS"] = dict(kb.regions)
    kb.free("CTX1", "GRW", "CSM", "WPJ", "R2B", "RES3", "SQ7")



BF16 = mybir.dt.bfloat16


def _phase_D(nc, blk, kb, dr, C, out_fm):
    R3 = kb.view("R3").rearrange("p (c t) -> p c t", c=8)
    kb.free("QPOS")
    kb.region("RWS", 64)
    kb.region("LPT", 128)
    kb.region("SQ8", 512)
    kb.region("PRB", 16)
    big = kb.big.ap()
    RWS = kb.view("RWS").rearrange("p (k c) -> p k c", k=8)
    LPT = kb.view("LPT").rearrange("p (s e) -> p s e", e=8)
    SQ8 = kb.view("SQ8")
    PRB = kb.view("PRB")

    tks = {}
    wfree = {}

    @blk.sync
    def _(s):
        tks["rw"] = kb.dma(s, "LC0", RWS, dr["rw_t"][0].rearrange("k p c -> p k c"))
    rw_tok = tks["rw"]

    for tt in range(16):
        tcol = slice(128 * tt, 128 * tt + 128)

        @blk.tensor
        def _(t, tt=tt, tcol=tcol, rw_tok=rw_tok):
            kb.wait(t, rw_tok)
            bo = tt % 2
            kb.use_bank(t, bo)
            for k in range(8):
                mm = t.matmul(kb.banks[bo].ap()[:, 0:8], R3[:, k, tcol], RWS[:, k],
                              start=(k == 0), stop=(k == 7))
            tks["lg"] = kb.inc(mm, "T")

        @blk.vector
        def _(v, tt=tt):
            bo = tt % 2
            kb.wait(v, tks["lg"])
            kb.chain(v, "V")
            v.reduce_max(PRB[:, 0:1], kb.banks[bo].ap()[:, 0:8], axis=AX)
            kb.inc(KB.nop(v), "V")
            kb.chain(v, "V")
            v.tensor_scalar(PRB[:, 1:2], PRB[:, 0:1], -1.0, None, AL.mult)
            tks["ngm"] = kb.inc(KB.nop(v), "V")

        @blk.scalar
        def _(s, tt=tt):
            bo = tt % 2
            kb.wait(s, tks["ngm"])
            kb.chain(s, "S")
            ev = s.activation(LPT[:, tt, :], kb.banks[bo].ap()[:, 0:8], AF.Exp,
                              bias=PRB[:, 1:2])
            kb.bank_free[bo] = kb.inc(ev, "S")
            tks["pe"] = kb.tok("S")

        @blk.vector
        def _(v, tt=tt):
            kb.wait(v, tks["pe"])
            steps = [
                lambda: v.reduce_sum(PRB[:, 2:3], LPT[:, tt, :], axis=AX),
                lambda: v.reciprocal(PRB[:, 3:4], PRB[:, 2:3]),
                lambda: v.tensor_scalar(LPT[:, tt, :], LPT[:, tt, :],
                                        PRB[:, 3:4], None, AL.mult),
                lambda: v.reduce_max(PRB[:, 4:5], LPT[:, tt, :], axis=AX),
                lambda: v.tensor_scalar(SQ8[:, 0:8], LPT[:, tt, :], PRB[:, 4:5],
                                        None, AL.is_equal),
                lambda: v.scalar_tensor_tensor(SQ8[:, 8:16], SQ8[:, 0:8],
                                               -1.0e30, LPT[:, tt, :], AL.mult,
                                               AL.add),
                lambda: v.reduce_max(PRB[:, 6:7], SQ8[:, 8:16], axis=AX),
                lambda: v.tensor_scalar(SQ8[:, 0:8], LPT[:, tt, :], PRB[:, 6:7],
                                        None, AL.is_ge),
                lambda: v.tensor_tensor(LPT[:, tt, :], LPT[:, tt, :],
                                        SQ8[:, 0:8], AL.mult),
            ]
            for f in steps:
                kb.chain(v, "V")
                f()
                kb.inc(KB.nop(v), "V")
    lp_done = kb.tok("V")

    @blk.gpsimd
    def _(g):
        kb.wait(g, lp_done)
        for c in range(8):
            tks["r3s"] = kb.dma(g, "RS%d" % (c % 2), dr["r3_sp"][c], R3[:, c, :])
    r3_sp_done = (kb.tok("RS0"), kb.tok("RS1"))
    kb.free("R3")
    kb.region("OA", 8 * 1024)
    kb.region("GEH", 11 * 512)
    kb.region("GEL", 11 * 512)
    kb.region("GST", 512)
    kb.region("R3H", 8 * 512)
    kb.region("R3L", 8 * 512)
    kb.region("R3T", 2 * 1024)
    kb.region("W1S", 2 * 1024)
    kb.region("W1H", 2 * 1024)
    kb.region("W1L", 2 * 1024)
    kb.region("W2S", 11 * 128)
    kb.region("W2H", 2 * 11 * 64)
    kb.region("W2L", 2 * 11 * 64)
    kb.region("LPB", 1024)
    kb.region("LPR1", 1024)
    kb.region("TY1", 512)
    kb.region("DRW", 3 * 512)
    kb.region("RRS", 2 * 512)
    OA = kb.view("OA").rearrange("p (c t) -> p c t", c=8)
    GEH = kb.view("GEH").bitcast(BF16).rearrange("p (f t) -> p f t", f=11)
    GEL = kb.view("GEL").bitcast(BF16).rearrange("p (f t) -> p f t", f=11)
    GST = kb.view("GST")
    R3H = kb.view("R3H").bitcast(BF16).rearrange("p (c t) -> p c t", c=8)
    R3L = kb.view("R3L").bitcast(BF16).rearrange("p (c t) -> p c t", c=8)
    R3T = kb.view("R3T").rearrange("p (b t) -> p b t", b=2)
    W1S = kb.view("W1S").rearrange("p (f k c) -> p f k c", f=2, k=8)
    W1H = kb.view("W1H").bitcast(BF16).rearrange("p (b f k c) -> p b f k c",
                                                 b=2, f=2, k=8)
    W1L = kb.view("W1L").bitcast(BF16).rearrange("p (b f k c) -> p b f k c",
                                                 b=2, f=2, k=8)
    W2S = kb.view("W2S").rearrange("p (k c) -> p k c", k=11)
    W2H = kb.view("W2H").bitcast(BF16).rearrange("p (b k c) -> p b k c", b=2, k=11)
    W2L = kb.view("W2L").bitcast(BF16).rearrange("p (b k c) -> p b k c", b=2, k=11)
    LPB = kb.view("LPB").rearrange("p (b t) -> p b t", b=2)
    LPR1 = kb.view("LPR1")[0:1, :]
    TY1 = kb.view("TY1")
    DRW = kb.view("DRW").rearrange("p (b t) -> p b t", b=3)[0:1]
    RRS = kb.view("RRS").rearrange("p (b t) -> p b t", b=2)

    ofree = {}
    for th in range(2):
        hcol = slice(1024 * th, 1024 * th + 1024)

        for c in range(8):
            cb = c % 2

            @blk.gpsimd
            def _(g, th=th, c=c, cb=cb):
                kb.wait(g, *r3_sp_done)
                if ("r3t", cb) in wfree:
                    kb.wait(g, wfree[("r3t", cb)])
                tks["r3t"] = kb.dma(g, "RT%d" % cb, R3T[:, cb, :],
                                    dr["r3_sp"][c][:, hcol])

            @blk.vector
            def _(v, th=th, c=c, cb=cb):
                kb.wait(v, tks["r3t"])
                if ("r3hl", c) in wfree:
                    kb.wait(v, wfree[("r3hl", c)])
                kb.chain(v, "V")
                v.tensor_copy(R3H[:, c, :], R3T[:, cb, :])
                kb.inc(KB.nop(v), "V")
                kb.chain(v, "V")
                ev = v.tensor_tensor(R3L[:, c, :], R3T[:, cb, :], R3H[:, c, :],
                                     AL.subtract)
                wfree[("r3t", cb)] = kb.inc(ev, "V")
                tks["r3hl"] = wfree[("r3t", cb)]
        r3hl_done = kb.tok("V")

        for e in range(8):
            @blk.gpsimd
            def _(g, th=th, e=e):
                kb.wait(g, lp_done)
                kb.wait(g, *r3_sp_done)
                if "lpr" in wfree:
                    kb.wait(g, wfree["lpr"])
                for ti in range(8):
                    tt = 8 * th + ti
                    tks["lpr"] = kb.dma(g, "LPX",
                                        LPR1[:, 128 * ti : 128 * ti + 128],
                                        LPT[:, tt, e : e + 1], slow=True)

            for half in range(2):
                @blk.tensor
                def _(t, half=half):
                    kb.wait(t, tks["lpr"])
                    kb.use_bank(t, 6)
                    tks["lpb"] = kb.inc(
                        t.matmul(kb.banks[6].ap()[:, :], C.ones_row,
                                 LPR1[:, 512 * half : 512 * half + 512],
                                 start=True, stop=True), "T")

                @blk.vector
                def _(v, half=half):
                    kb.wait(v, tks["lpb"])
                    if ("lpbf", half) in wfree:
                        kb.wait(v, wfree[("lpbf", half)])
                    kb.chain(v, "V")
                    ev = v.tensor_copy(LPB[:, half], kb.banks[6].ap()[:, :])
                    kb.bank_free[6] = kb.inc(ev, "V")
                    tks["lpbe"] = kb.bank_free[6]
                    wfree["lpr"] = tks["lpbe"]
            lpb_tok = tks["lpbe"]

            w1tok = {}

            def w1_fetch(fi, e=e):
                wb = fi % 2

                @blk.sync
                def _(s, e=e, fi=fi, wb=wb):
                    if ("w1s",) in wfree:
                        kb.wait(s, wfree[("w1s",)])
                    else:
                        kb.wait(s, *r3_sp_done)
                        kb.wait(s, kb.tok("T"))
                    tks["w1a"] = kb.dma(s, "WM0", W1S[:, 0],
                                        dr["w1_t"][e][fi].rearrange("k p c -> p k c"))
                    tks["w1"] = kb.dma(s, "WM1", W1S[:, 1],
                                       dr["w1_t"][e][fi + 11]
                                       .rearrange("k p c -> p k c"))

                @blk.vector
                def _(v, fi=fi, wb=wb):
                    kb.wait(v, tks["w1a"], tks["w1"])
                    if ("w1hl", wb) in wfree:
                        kb.wait(v, wfree[("w1hl", wb)])
                    for f2 in range(2):
                        kb.chain(v, "V")
                        v.tensor_copy(W1H[:, wb, f2], W1S[:, f2])
                        kb.inc(KB.nop(v), "V")
                        kb.chain(v, "V")
                        ev = v.tensor_tensor(W1L[:, wb, f2], W1S[:, f2],
                                             W1H[:, wb, f2], AL.subtract)
                        kb.inc(ev, "V")
                    wfree[("w1s",)] = kb.tok("V")
                    w1tok[fi] = wfree[("w1s",)]

            w1_fetch(0)
            for fi in range(11):
                wb = fi % 2
                if fi + 1 < 11:
                    w1_fetch(fi + 1)
                tks["w1hl"] = w1tok[fi]

                for nt2 in range(2):
                    col = slice(512 * nt2, 512 * nt2 + 512)
                    b1 = (2 * fi + nt2) % 2
                    b2 = 2 + b1

                    @blk.tensor
                    def _(t, th=th, e=e, fi=fi, nt2=nt2, wb=wb, b1=b1, b2=b2,
                          col=col, lpb_tok=lpb_tok, r3hl_done=r3hl_done):
                        kb.wait(t, tks["w1hl"], lpb_tok, r3hl_done)
                        if ("ge", fi) in ofree:
                            kb.wait(t, ofree[("ge", fi)])
                        kb.use_bank(t, b1)
                        idx = 0
                        for wop, xop in ((0, 0), (0, 1), (1, 0)):
                            for k in range(8):
                                xs = (R3H if xop == 0 else R3L)[:, k, col]
                                mm = t.matmul(kb.banks[b1].ap()[:, :],
                                              W1H[:, wb, 0, k, :] if wop == 0
                                              else W1L[:, wb, 0, k, :],
                                              xs, start=(idx == 0),
                                              stop=(idx == 23))
                                idx += 1
                        tks["y1"] = kb.inc(mm, "T")
                        kb.use_bank(t, b2)
                        idx = 0
                        for wop, xop in ((0, 0), (0, 1), (1, 0)):
                            for k in range(8):
                                xs = (R3H if xop == 0 else R3L)[:, k, col]
                                mm = t.matmul(kb.banks[b2].ap()[:, :],
                                              W1H[:, wb, 1, k, :] if wop == 0
                                              else W1L[:, wb, 1, k, :],
                                              xs, start=(idx == 0),
                                              stop=(idx == 23))
                                idx += 1
                        tks["y2"] = kb.inc(mm, "T")
                        if nt2 == 1:
                            tks["w1done%d" % wb] = tks["y2"]
                            wfree[("w1hl", wb)] = tks["y2"]

                    @blk.scalar
                    def _(s, b1=b1):
                        kb.wait(s, tks["y1"])
                        if "ty1" in wfree:
                            kb.wait(s, wfree["ty1"])
                        kb.wait(s, wfree.get("r4prev"))
                        kb.chain(s, "S")
                        ev = s.activation(TY1, kb.banks[b1].ap()[:, :], AF.Silu)
                        kb.bank_free[b1] = kb.inc(ev, "S")
                        tks["ty1"] = kb.tok("S")

                    @blk.vector
                    def _(v, fi=fi, nt2=nt2, b2=b2):
                        kb.wait(v, tks["y2"], tks["ty1"])
                        if "gst" in wfree:
                            kb.wait(v, wfree["gst"])
                        kb.chain(v, "V")
                        ev = v.tensor_tensor(GST, kb.banks[b2].ap()[:, :],
                                             LPB[:, nt2], AL.mult)
                        kb.bank_free[b2] = kb.inc(ev, "V")
                        kb.chain(v, "V")
                        v.tensor_tensor(GST, GST, TY1, AL.mult)
                        wfree["ty1"] = kb.inc(KB.nop(v), "V")
                        gcol = slice(512 * nt2, 512 * nt2 + 512)
                        kb.chain(v, "V")
                        v.tensor_copy(GEH[:, fi, gcol], GST)
                        kb.inc(KB.nop(v), "V")
                        kb.chain(v, "V")
                        ev = v.tensor_tensor(GEL[:, fi, gcol], GST,
                                             GEH[:, fi, gcol], AL.subtract)
                        wfree["gst"] = kb.inc(ev, "V")
                        tks["g"] = wfree["gst"]
            g_done = kb.tok("V")
            wfree[("lpbf", 0)] = g_done
            wfree[("lpbf", 1)] = g_done

            w2tok = {}

            def w2_fetch(m, e=e):
                wb = m % 2

                @blk.sync
                def _(s, e=e, m=m, wb=wb):
                    if ("w2s",) in wfree:
                        kb.wait(s, wfree[("w2s",)])
                    else:
                        kb.wait(s, *r3_sp_done)
                    tks["w2"] = kb.dma(s, "UC0", W2S,
                                       dr["w2_t"][e][m].rearrange("k p c -> p k c"))

                @blk.vector
                def _(v, m=m, wb=wb):
                    kb.wait(v, tks["w2"])
                    if ("w2hl", wb) in wfree:
                        kb.wait(v, wfree[("w2hl", wb)])
                    kb.chain(v, "V")
                    v.tensor_copy(W2H[:, wb], W2S)
                    kb.inc(KB.nop(v), "V")
                    kb.chain(v, "V")
                    ev = v.tensor_tensor(W2L[:, wb], W2S, W2H[:, wb], AL.subtract)
                    wfree[("w2s",)] = kb.inc(ev, "V")
                    w2tok[m] = wfree[("w2s",)]

            w2_fetch(0)
            for m in range(8):
                wb = m % 2
                bo = 4 + wb
                if m + 1 < 8:
                    w2_fetch(m + 1)
                tks["w2hl"] = w2tok[m]

                for nt2 in range(2):
                    gcol = slice(512 * nt2, 512 * nt2 + 512)

                    @blk.tensor
                    def _(t, m=m, nt2=nt2, wb=wb, bo=bo, gcol=gcol, g_done=g_done):
                        kb.wait(t, tks["w2hl"], g_done)
                        kb.use_bank(t, bo)
                        idx = 0
                        for wop, xop in ((0, 0), (0, 1), (1, 0)):
                            for kf in range(11):
                                gs = (GEH if xop == 0 else GEL)[:, kf, gcol]
                                mm = t.matmul(kb.banks[bo].ap()[:, :],
                                              W2H[:, wb, kf, :] if wop == 0
                                              else W2L[:, wb, kf, :],
                                              gs, start=(idx == 0),
                                              stop=(idx == 32))
                                idx += 1
                        tks["o"] = kb.inc(mm, "T")
                        if nt2 == 1:
                            wfree[("w2hl", wb)] = tks["o"]

                    @blk.vector
                    def _(v, e=e, m=m, nt2=nt2, bo=bo):
                        kb.wait(v, tks["o"])
                        kb.chain(v, "V")
                        ocol = slice(512 * nt2, 512 * nt2 + 512)
                        if e == 0:
                            ev = v.tensor_copy(OA[:, m, ocol],
                                               kb.banks[bo].ap()[:, :])
                        else:
                            ev = v.tensor_tensor(OA[:, m, ocol], OA[:, m, ocol],
                                                 kb.banks[bo].ap()[:, :], AL.add)
                        kb.bank_free[bo] = kb.inc(ev, "V")
                        tks["oacc"] = kb.bank_free[bo]
            if e == 7:
                o_done = kb.tok("V")
            for fi in range(11):
                ofree[("ge", fi)] = tks["o"]

        for nt2 in range(2):
            col = slice(1024 * th + 512 * nt2, 1024 * th + 512 * nt2 + 512)
            ocol = slice(512 * nt2, 512 * nt2 + 512)
            last_st = [None]
            for m in range(8):
                @blk.gpsimd
                def _(g, th=th, m=m, col=col):
                    kb.wait(g, *r3_sp_done)
                    if ("rrs", m % 2) in wfree:
                        kb.wait(g, wfree[("rrs", m % 2)])
                    tks["rrs"] = kb.dma(g, "RT%d" % (m % 2), RRS[:, m % 2],
                                        dr["r3_sp"][m][:, col])

                @blk.vector
                def _(v, th=th, m=m, nt2=nt2, ocol=ocol, o_done=o_done):
                    kb.wait(v, o_done, tks["rrs"])
                    kb.chain(v, "V")
                    v.tensor_tensor(RRS[:, m % 2], OA[:, m, ocol], RRS[:, m % 2],
                                    AL.add)
                    tks["res"] = kb.inc(KB.nop(v), "V")
                    kb.chain(v, "V")
                    kb.wait(v, last_st[0])
                    v.tensor_tensor(SQ8, RRS[:, m % 2], RRS[:, m % 2], AL.mult)
                    tks["sq"] = kb.inc(KB.nop(v), "V")
                    kb.chain(v, "V")
                    v.tensor_copy(OA[:, m, ocol], RRS[:, m % 2])
                    wfree[("rrs2", m % 2)] = kb.inc(KB.nop(v), "V")

                @blk.tensor
                def _(t, m=m):
                    if m == 0:
                        kb.use_bank(t, 6, 7)
                    kb.wait(t, kb.tok("V"))
                    t.matmul(kb.banks[6].ap()[0:1, :], C.ones_col, RRS[:, m % 2],
                             start=(m == 0), stop=(m == 7))
                    mm = t.matmul(kb.banks[7].ap()[0:1, :], C.ones_col, SQ8,
                                  start=(m == 0), stop=(m == 7))
                    tks["lnst"] = last_st[0] = kb.inc(mm, "T")
                    wfree[("rrs", m % 2)] = tks["lnst"]

            @blk.vector
            def _(v):
                kb.wait(v, tks["lnst"])
                kb.chain(v, "V")
                v.tensor_scalar(DRW[:, 0, :], kb.banks[6].ap()[0:1, :],
                                1.0 / 1024.0, None, AL.mult)
                kb.bank_free[6] = kb.inc(KB.nop(v), "V")
                kb.chain(v, "V")
                v.tensor_scalar(DRW[:, 1, :], kb.banks[7].ap()[0:1, :],
                                1.0 / 1024.0, None, AL.mult)
                kb.bank_free[7] = kb.inc(KB.nop(v), "V")
                kb.chain(v, "V")
                v.tensor_tensor(DRW[:, 2, :], DRW[:, 0, :], DRW[:, 0, :], AL.mult)
                kb.inc(KB.nop(v), "V")
                kb.chain(v, "V")
                v.tensor_tensor(DRW[:, 1, :], DRW[:, 1, :], DRW[:, 2, :],
                                AL.subtract)
                tks["var"] = kb.inc(KB.nop(v), "V")

            @blk.scalar
            def _(s):
                kb.wait(s, tks["var"])
                kb.chain(s, "S")
                kb.inc(s.activation(DRW[:, 2, :], DRW[:, 1, :], AF.Sqrt,
                                    bias=C.epscol[0:1, 0:1]), "S")
            s_sdd = kb.tok("S")

            @blk.vector
            def _(v):
                kb.wait(v, s_sdd)
                kb.chain(v, "V")
                v.reciprocal(DRW[:, 1, :], DRW[:, 2, :])
                tks["rstd"] = kb.inc(KB.nop(v), "V")

            @blk.tensor
            def _(t):
                kb.wait(t, tks["rstd"])
                kb.use_bank(t, 6, 7)
                tks["bmu"] = kb.inc(
                    t.matmul(kb.banks[6].ap()[:, :], C.ones_row, DRW[:, 0, :],
                             start=True, stop=True), "T")
                tks["brs"] = kb.inc(
                    t.matmul(kb.banks[7].ap()[:, :], C.ones_row, DRW[:, 1, :],
                             start=True, stop=True), "T")

            for m in range(8):
                @blk.vector
                def _(v, m=m, nt2=nt2, ocol=ocol):
                    kb.wait(v, tks["bmu"], tks["brs"])
                    kb.wait(v, wfree.get("r4prev"))
                    kb.chain(v, "V")
                    v.tensor_tensor(TY1, OA[:, m, ocol], kb.banks[6].ap()[:, :],
                                    AL.subtract)
                    kb.inc(KB.nop(v), "V")
                    kb.chain(v, "V")
                    v.tensor_tensor(TY1, TY1, kb.banks[7].ap()[:, :], AL.mult)
                    tks["r4"] = kb.inc(KB.nop(v), "V")

                @blk.gpsimd
                def _(g, th=th, m=m, nt2=nt2, col=col):
                    kb.wait(g, tks["r4"])
                    wfree["r4prev"] = kb.dma(g, "GO%d" % (m % 2),
                                             out_fm[m][:, col], TY1)
            kb.bank_free[6] = kb.tok("V")
            kb.bank_free[7] = kb.tok("V")
        for fi in range(11):
            ofree[("ge", fi)] = kb.tok("V")
        wfree[("r3hl", 0)] = kb.tok("T")
        for c in range(8):
            wfree[("r3hl", c)] = wfree[("r3hl", 0)]

    @blk.gpsimd
    def _(g):
        kb.chain(g, "GO0")
        kb.chain(g, "GO1")
